# version 06
# baseline (speedup 1.0000x reference)
import numpy as np
from contextlib import ExitStack

import concourse.bass as bass
import concourse.mybir as mybir
from concourse.bass_utils import run_bass_kernel_spmd
from concourse.alu_op_type import AluOpType as AO

F32 = mybir.dt.float32
BF16 = mybir.dt.bfloat16
I32 = mybir.dt.int32
AF = mybir.ActivationFunctionType
AX = mybir.AxisListType.X

N, D, NT, KC8, H = 2048, 1024, 16, 8, 16
E, S, F, FT = 16, 64, 2048, 16
EPS = 1e-5
MAGIC = 12582912.0


def host_consts():
    half = 32
    inv = (1.0 / 10000.0 ** (np.arange(half, dtype=np.float32) / half))
    ang = np.arange(N, dtype=np.float32)[:, None] * inv
    cos, sin = np.cos(ang).astype(np.float32), np.sin(ang).astype(np.float32)
    cq64 = np.repeat(cos.T, 2, axis=0)[:64]
    sq64 = np.repeat(sin.T, 2, axis=0)[:64]
    mpos = np.empty(N, np.int64)
    for c in range(16):
        mpos[128 * c:128 * c + 128] = 16 * np.arange(128) + c
    ck64, sk64 = cq64[:, mpos], sq64[:, mpos]
    dup = lambda t: np.concatenate([t, t], 0).astype(np.float32)
    return {"tcq": dup(cq64), "tsq": dup(sq64), "tck": dup(ck64),
            "tsk": dup(sk64)}


def build(debug=(), stop_after=None, p1_mode=None):
    nc = bass.Bass("TRN2", target_bir_lowering=False, dynamic_dma_scratch_size=16384,
                   detect_race_conditions=False)
    dr = lambda nm, sh, kind="ExternalInput": nc.dram_tensor(nm, sh, F32, kind=kind).ap()
    x_in = dr("x", [N, D]); cvec_in = dr("cvec", [D])
    wq_in = dr("wq", [D, D]); wqb_in = dr("wqb", [D]); u_in = dr("u", [64, 32])
    wo_in = dr("wo", [D, D]); wmod_in = dr("wmod", [D, 6 * D])
    ng_in = dr("ng", [D]); nb_in = dr("nb", [D])
    sg_in = dr("sg", [D]); sbv_in = dr("sbv", [D]); sc_in = dr("sc", [1])
    semb_in = dr("semb", [E * S, D])
    wg_in = dr("wgx", [E, D, F]); bg_in = dr("bgx", [E, F])
    wu_in = dr("wux", [E, D, F]); bu_in = dr("bux", [E, F])
    wd_in = dr("wdx", [E, F, D]); bd_in = dr("bdx", [E, D])
    tcq_in = dr("tcq", [128, N]); tsq_in = dr("tsq", [128, N])
    tck_in = dr("tck", [128, N]); tsk_in = dr("tsk", [128, N])
    y_out = dr("y", [N, D], "ExternalOutput")
    d_x2 = dr("d_x2", [N, D], "Internal")
    d_bn = dr("d_bn", [8192], "Internal")

    dbg_t = {}
    def dbg(nm, sh):
        if nm in debug and nm not in dbg_t:
            dbg_t[nm] = nc.dram_tensor("dbg_" + nm, sh, F32, kind="ExternalOutput").ap()
        return nm in debug

    with ExitStack() as ctx:
        BFN = 83584
        FFN_ = 9984
        ARB = ctx.enter_context(nc.sbuf_tensor("ARB", [128, BFN], BF16))
        ARF = ctx.enter_context(nc.sbuf_tensor("ARF", [128, FFN_], F32))

        def bf(off, n, p0=0, p1=128):
            assert off + n <= BFN, (off, n)
            return ARB[p0:p1, off:off + n]

        def f32(off, nf, p0=0, p1=128):
            assert off + nf <= FFN_, (off, nf)
            return ARF[p0:p1, off:off + nf]

        def i32f(off, nf, p0=0, p1=128):
            return ARF[:, off:off + nf].bitcast(I32)[p0:p1, :]

        ftr0, ftr1, ftr2 = 0, 1024, 2048
        FW = 3072
        f_wtmp = FW
        f_qbrow = FW
        f_rows = FW
        f_gn, f_bn = FW, FW + 1024
        f_mx, f_sub, f_lcrow = FW, FW + 2048, FW + 4096
        f_bd, f_fb = FW, FW + 2048
        fsm = 9216
        f_stat = fsm
        f_alpha = fsm + 64
        f_thr = fsm + 80
        f_silu = fsm + 96
        f_qbc = fsm + 112
        f_rl = fsm + 128
        f_rlc = fsm + 144
        f_sccol = fsm + 160
        f_row = fsm + 176
        f_row2 = fsm + 688

        G1 = 0
        a_p65 = 0
        SMB = 16640
        sm_pad = SMB + 0
        sm_ab = SMB + 128
        sm_silub = SMB + 192
        sm_ones = SMB + 256
        sm_mxt = None
        B = 17024
        b_projT = B + 0
        b_lgT = B + 0
        b_wg = B + 24576
        b_wu = B + 0
        b_wd = B + 8192
        b_gu = B + 12288
        b_wtT = B + 16384
        b_ylo = B + 16384
        b_seN = B + 16384
        b_disp = B + 16384
        b_eo = B + 16384
        b_woT = B + 24576
        b_seT = B + 24576
        b_modA = B + 32768
        b_mxb = B + 32768
        b_slt = B + 32768
        b_tab = B + 35840
        b_modB = B + 36864
        sm_mxt = B + 38912
        b_j6 = B + 40960
        b_gb = B + 41984
        b_absw = B + 44032
        b_wmod = 0
        b_xqT = B + 44032
        b_pt = B + 44032
        b_e12q = B + 48128
        b_e12k = B + 50176
        b_kp = B + 52224
        b_qla = B + 54272
        b_kla = B + 56320
        b_xnT = B + 44032
        b_cu = B + 44032
        b_sgn = B + 52224
        b_abt = B + 52224
        b_wtv = B + 0
        b_yhi = B + 58368

        def ytile(i):
            return bf((b_ylo + 1024 * i) if i < 8 else (b_yhi + 1024 * (i - 8)), 1024)

        ps_a = ctx.enter_context(nc.psum_tensor("ps_a", [128, 1024], F32))
        ps_b = ctx.enter_context(nc.psum_tensor("ps_b", [128, 1024], F32))
        ps_y = ctx.enter_context(nc.psum_tensor("ps_y", [128, 1536], F32))
        ps_s = ctx.enter_context(nc.psum_tensor("ps_s", [128, 512], F32))

        class Sem:
            def __init__(self, nm):
                self.h = ctx.enter_context(nc.semaphore(nm))
                self.n = 0
            def inc(self, inst, k=1):
                inst.then_inc(self.h, k)
                self.n += k
                return self.n
        sv = Sem("sv"); st_ = Sem("st"); ss = Sem("ss"); bar = Sem("bar")

        class DmaChain:
            def __init__(self, name, k=4):
                self.sems = [Sem(f"{name}{i}") for i in range(k)]
                self.k = k
                self.i = 0
            def _pre(self, eng):
                s = self.sems[self.i % self.k]
                if s.n > 0:
                    eng.wait_ge(s.h, s.n)
                return s
            def issue(self, eng, **kw):
                s = self._pre(eng)
                s.inc(eng.dma_start(**kw), 16)
                self.i += 1
                return (s.h, s.n)
            def issue_t(self, eng, dst, src):
                s = self._pre(eng)
                s.inc(eng.dma_start_transpose(dst, src), 16)
                self.i += 1
                return (s.h, s.n)
            def plan(self, n):
                vals = {id(s): s.n for s in self.sems}
                toks = []
                for j in range(n):
                    s = self.sems[(self.i + j) % self.k]
                    vals[id(s)] += 16
                    toks.append((s.h, vals[id(s)]))
                return toks
            def wait_all(self, eng):
                for s in self.sems:
                    if s.n > 0:
                        eng.wait_ge(s.h, s.n)

        W = lambda eng, tok: eng.wait_ge(tok[0], tok[1])
        cha = DmaChain("cha", 10)
        chs = DmaChain("chs", 4)
        chg = DmaChain("chg", 8)
        cho = DmaChain("cho", 4)

        blk = ctx.enter_context(nc.Block())
        ENG = (nc.sync, nc.tensor, nc.vector, nc.scalar, nc.gpsimd)

        def barrier():
            for eng in ENG:
                eng.drain()
                eng.nop().then_inc(bar.h, 1)
            bar.n += 5
            for eng in ENG:
                eng.wait_ge(bar.h, bar.n)

        STAT = lambda j: f32(f_stat, 64)[:, j:j + 1]

        def dbg_store(nm, dst_ap, src_ap, eng_sem=None):
            if eng_sem is not None:
                nc.gpsimd.wait_ge(eng_sem.h, eng_sem.n)
            tok = chg.issue(nc.gpsimd, out=dst_ap, in_=src_ap)
            W(nc.gpsimd, tok)

        def dma_transpose_batch(items, gate_sem, gate_val):
            nc.sync.wait_ge(gate_sem.h, gate_val)
            for dst, src in items:
                cha.issue_t(nc.sync, dst, src)

        t_cv = cha.issue(nc.sync, out=f32(f_silu, 8),
                         in_=cvec_in.rearrange("(k p) -> p k", p=128),
                         allow_slow_non_contiguous=True)
        t_qbc = cha.issue(nc.sync, out=f32(f_qbc, 8),
                          in_=wqb_in.rearrange("(o p) -> p o", p=128),
                          allow_slow_non_contiguous=True)
        for o, t in ((b_tab, tcq_in), (b_tab + 2048, tsq_in),
                     (b_tab + 4096, tck_in), (b_tab + 6144, tsk_in)):
            chg.issue(nc.gpsimd, out=bf(o, 2048), in_=t)
        nc.vector.memset(bf(sm_ones, 64), 1.0)
        nc.vector.memset(f32(f_stat, 64), 0.0)
        sv.inc(nc.vector.memset(bf(sm_pad, 128), 0.0))
        nc.gpsimd.wait_ge(sv.h, sv.n)
        t_u = chg.issue(nc.gpsimd, out=bf(sm_pad, 128)[0:64, 0:32], in_=u_in)

        def ternary(w_in, wtT_off, scratch_guard_sv):
            nc.scalar.wait_ge(sv.h, scratch_guard_sv)
            nc.sync.wait_ge(ss.h, ss.n)
            for hh in range(2):
                toks = []
                for q in range(4):
                    if hh == 1 and q == 0:
                        nc.sync.wait_ge(ss.h, ss.n)
                    toks.append(cha.issue(
                        nc.sync, out=f32(f_wtmp + 1024 * q, 1024),
                        in_=w_in[128 * (4 * hh + q):128 * (4 * hh + q) + 128, :]))
                for q in range(4):
                    ot = 4 * hh + q
                    W(nc.scalar, toks[q])
                    ss.inc(nc.scalar.activation(
                        bf(b_absw + 1024 * ot, 1024), f32(f_wtmp + 1024 * q, 1024),
                        AF.Abs, accum_out=f32(f_alpha, 8)[:, ot:ot + 1]))
                    ss.inc(nc.scalar.activation(
                        bf(b_sgn + 1024 * ot, 1024), f32(f_wtmp + 1024 * q, 1024),
                        AF.Sign))
            nc.vector.wait_ge(ss.h, ss.n)
            sv.inc(nc.vector.tensor_reduce(f32(f_thr, 1), f32(f_alpha, 8), AX, AO.add))
            sv.inc(nc.vector.tensor_scalar(f32(f_alpha, 8), f32(f_alpha, 8),
                                           1.0 / D, None, op0=AO.mult))
            nc.sync.wait_ge(sv.h, sv.n)
            t1 = cha.issue(nc.sync, out=d_bn[0:128][:, None], in_=f32(f_thr, 1))
            W(nc.sync, t1)
            t2 = cha.issue(nc.sync, out=f32(f_row, 512, p1=1)[:, 0:128],
                           in_=d_bn[0:128][None, :])
            W(nc.vector, t2)
            sv.inc(nc.vector.tensor_reduce(f32(f_row2, 1, p1=1),
                                           f32(f_row, 512, p1=1)[:, 0:128], AX, AO.add))
            nc.vector.drain()
            sv.inc(nc.vector.tensor_scalar(f32(f_row2, 1, p1=1), f32(f_row2, 1, p1=1),
                                           0.05 / (D * D), None, op0=AO.mult))
            nc.sync.wait_ge(sv.h, sv.n)
            t3 = cha.issue(nc.sync, out=d_bn[128:129][None, :], in_=f32(f_row2, 1, p1=1))
            W(nc.sync, t3)
            t4 = cha.issue(nc.sync, out=f32(f_thr, 1),
                           in_=d_bn[128:129][None, :].to_broadcast((128, 1)))
            W(nc.vector, t4)
            cha.wait_all(nc.vector)
            chs.wait_all(nc.vector)
            for ot in range(8):
                nc.vector.scalar_tensor_tensor(bf(b_wtv + 1024 * ot, 1024),
                                               bf(b_absw + 1024 * ot, 1024),
                                               f32(f_thr, 1),
                                               bf(b_sgn + 1024 * ot, 1024),
                                               op0=AO.is_ge, op1=AO.mult)
            for ot in range(8):
                inst = nc.vector.tensor_scalar(bf(b_wtv + 1024 * ot, 1024),
                                               bf(b_wtv + 1024 * ot, 1024),
                                               f32(f_alpha, 8)[:, ot:ot + 1],
                                               None, op0=AO.mult)
            sv.inc(inst)
            scratch_free = sv.n
            items = []
            for ot in range(8):
                for kc in range(KC8):
                    items.append((bf(wtT_off + 1024 * kc, 1024)[:, 128 * ot:128 * ot + 128],
                                  bf(b_wtv + 1024 * ot, 1024)[:, 128 * kc:128 * kc + 128]))
            dma_transpose_batch(items, sv, scratch_free)
            return scratch_free

        sem_me = Sem("me")
        W(nc.scalar, t_cv)
        ss.inc(nc.scalar.activation(f32(f_silu, 8), f32(f_silu, 8), AF.Silu))
        nc.vector.wait_ge(ss.h, ss.n)
        sv.inc(nc.vector.tensor_copy(bf(sm_silub, 16)[:, 0:8], f32(f_silu, 8)))
        silu_done = sv.n
        psmap = [ps_a[0:1, 0:512], ps_a[0:1, 512:1024], ps_b[0:1, 0:512],
                 ps_b[0:1, 512:1024], ps_y[0:1, 0:512], ps_y[0:1, 512:1024]]
        wm_toks = chg.plan(16)
        mm_done = {}
        mod_row_tok = None
        nc.tensor.wait_ge(sv.h, silu_done)
        for pph in range(2):
            for kc in range(KC8):
                idx = pph * 8 + kc
                W(nc.tensor, wm_toks[idx])
                if pph == 1 and kc == 0:
                    nc.tensor.wait_ge(sem_me.h, 1)
                for sl in range(6):
                    inst = nc.tensor.matmul(
                        psmap[sl], bf(sm_silub, 16)[:, kc:kc + 1],
                        bf(b_wmod + 3072 * (kc % 2), 3072)[:, 512 * sl:512 * sl + 512],
                        start=(kc == 0), stop=(kc == 7))
                st_.inc(inst)
                mm_done[idx] = st_.n
        for idx in range(16):
            pph, kc = idx // 8, idx % 8
            if idx >= 2:
                nc.gpsimd.wait_ge(st_.h, mm_done[idx - 2])
            tok = chg.issue(nc.gpsimd,
                            out=bf(b_wmod + 3072 * (kc % 2), 3072),
                            in_=wmod_in[128 * kc:128 * kc + 128,
                                        3072 * pph:3072 * (pph + 1)])
            assert tok == wm_toks[idx], (tok, wm_toks[idx])

        def evac_mod(pph, handshake):
            global_tok = None
            nc.vector.wait_ge(st_.h, mm_done[pph * 8 + 7])
            for sl in range(6):
                tok = getattr(evac_mod, "row_tok", None)
                if tok is not None:
                    W(nc.vector, tok)
                inst = nc.vector.tensor_copy(f32(f_row, 512, p1=1), psmap[sl])
                if handshake and sl == 5:
                    sem_me.inc(inst)
                else:
                    sv.inc(inst)
                nc.sync.wait_ge(sv.h, sv.n)
                if handshake and sl == 5:
                    nc.sync.wait_ge(sem_me.h, 1)
                evac_mod.row_tok = cha.issue(
                    nc.sync,
                    out=d_bn[1024 + 3072 * pph + 512 * sl:
                             1024 + 3072 * pph + 512 * (sl + 1)][None, :],
                    in_=f32(f_row, 512, p1=1))

        g1 = ternary(wq_in, b_wtT, sv.n)
        evac_mod(0, True)
        absw_free_sv = ternary(wo_in, b_woT, g1)
        evac_mod(1, False)

        nc.vector.wait_ge(sv.h, absw_free_sv)
        nc.sync.wait_ge(sv.h, absw_free_sv)
        W(nc.sync, t_u)
        tU = cha.issue_t(nc.sync, bf(b_abt + 4096, 128), bf(sm_pad, 128))
        W(nc.vector, tU)
        ut = bf(b_abt + 4096, 128, p0=0, p1=32)
        abT = bf(b_abt, 128, p0=0, p1=32)
        nc.vector.tensor_copy(abT[:, 0:64:2], ut[:, 0:32])
        nc.vector.tensor_copy(abT[:, 1:64:2], ut[:, 32:64])
        nc.vector.tensor_copy(abT[:, 64:128:2], ut[:, 32:64])
        nc.vector.tensor_scalar(abT[:, 65:128:2], ut[:, 0:32], -1.0, None, op0=AO.mult)
        nc.vector.memset(bf(b_abt, 128, p0=32, p1=64), 0.0)
        sv.inc(nc.vector.memset(bf(b_abt, 128, p0=64, p1=128), 0.0))
        nc.sync.wait_ge(sv.h, sv.n)
        tAB = cha.issue_t(nc.sync, bf(b_abt + 8192, 128), bf(b_abt, 128))
        W(nc.vector, tAB)
        nc.vector.tensor_scalar(bf(sm_ab, 64)[:, 0:32], bf(b_abt + 8192, 128)[:, 0:32],
                                0.125, None, op0=AO.mult)
        sv.inc(nc.vector.tensor_copy(bf(sm_ab, 64)[:, 32:64],
                                     bf(b_abt + 8192, 128)[:, 0:32]))
        ab_done = sv.n
        cha.wait_all(nc.sync)

        def mod_bcast(spec):
            for off, kind, j in spec:
                nc.sync.wait_ge(sv.h, sv.n)
                t5 = cha.issue(nc.sync, out=f32(ftr2, 1024),
                               in_=d_bn[1024 + 1024 * j:1024 + 1024 * (j + 1)][None, :]
                               .to_broadcast((128, 1024)))
                W(nc.vector, t5)
                if kind == "cp":
                    sv.inc(nc.vector.tensor_copy(bf(off, 1024), f32(ftr2, 1024)))
                elif kind == "ca":
                    sv.inc(nc.vector.tensor_scalar(bf(off, 1024), f32(ftr2, 1024),
                                                   1.0, None, op0=AO.add))
                else:
                    nc.vector.tensor_scalar(f32(ftr0, 1024), f32(ftr2, 1024), 0.0, 2.0,
                                            op0=AO.is_ge, op1=AO.mult)
                    nc.vector.tensor_scalar(f32(ftr0, 1024), f32(ftr0, 1024), -1.0,
                                            None, op0=AO.add)
                    nc.vector.tensor_scalar(f32(ftr1, 1024), f32(ftr2, 1024), -1.0,
                                            None, op0=AO.mult)
                    nc.vector.tensor_tensor(f32(ftr1, 1024), f32(ftr1, 1024),
                                            f32(ftr2, 1024), op=AO.max)
                    nc.vector.tensor_scalar(f32(ftr1, 1024), f32(ftr1, 1024), 0.25,
                                            None, op0=AO.max)
                    sv.inc(nc.vector.tensor_tensor(bf(off, 1024), f32(ftr0, 1024),
                                                   f32(ftr1, 1024), op=AO.mult))

        dbg_wtt = dbg("wtt", [128, 8 * 1024])
        dbg_wot = dbg("wot", [128, 8 * 1024])
        dbg_wtv = dbg("wtv2", [128, 8 * 1024])
        if dbg_wtt or dbg_wot or dbg_wtv:
            cha.wait_all(nc.gpsimd)
            chs.wait_all(nc.gpsimd)
            if "wtt" in dbg_t:
                dbg_store("wtt", dbg_t["wtt"], bf(b_wtT, 8192), sv)
            if "wot" in dbg_t:
                dbg_store("wot", dbg_t["wot"], bf(b_woT, 8192), sv)
            if "wtv2" in dbg_t:
                dbg_store("wtv2", dbg_t["wtv2"], bf(b_wtv, 8192), sv)
        mod_bcast([(b_modA + 1024, "cp", 0), (b_modA, "ca", 1), (b_modA + 2048, "g", 2)])
        if dbg("mod", [128, 6 * 1024]):
            for j, off in enumerate([b_modA + 1024, b_modA, b_modA + 2048]):
                dbg_store("mod", dbg_t["mod"][:, 1024 * j:1024 * (j + 1)],
                          bf(off, 1024), sv)
        barrier()
        if stop_after == 1:
            cho.wait_all(nc.gpsimd); chg.wait_all(nc.gpsimd)
            cha.wait_all(nc.gpsimd); chs.wait_all(nc.gpsimd)
            return nc, sorted(dbg_t)

        slot_v, slot_s = {}, {}
        p1n = 2 if p1_mode == "nt2" else NT
        for i in range(p1n):
            par = i % 2
            sb_ = 8 * par
            foff = ftr0 + 1024 * par
            xt = f32(foff, 1024)
            if i >= 2:
                nc.sync.wait_ge(sv.h, slot_v[i - 2])
                nc.sync.wait_ge(ss.h, slot_s[i - 2])
            tx = cha.issue(nc.sync, out=xt, in_=x_in[128 * i:128 * i + 128, :])
            W(nc.vector, tx)
            if i >= 2:
                nc.vector.wait_ge(ss.h, slot_s[i - 2])
            nc.vector.tensor_reduce(STAT(sb_ + 0), xt, AX, AO.add)
            nc.vector.drain()
            sv.inc(nc.vector.tensor_scalar(STAT(sb_ + 1), STAT(sb_ + 0), -1.0 / D,
                                           None, op0=AO.mult))
            nc.scalar.wait_ge(sv.h, sv.n)
            ss.inc(nc.scalar.activation(xt, xt, AF.Identity, bias=STAT(sb_ + 1)))
            ss.inc(nc.scalar.activation(bf(G1 + 1024 * i, 1024), xt, AF.Square,
                                        accum_out=STAT(sb_ + 2)))
            nc.vector.wait_ge(ss.h, ss.n)
            sv.inc(nc.vector.tensor_scalar(STAT(sb_ + 3), STAT(sb_ + 2), 1.0 / D, EPS,
                                           op0=AO.mult, op1=AO.add))
            nc.scalar.wait_ge(sv.h, sv.n)
            ss.inc(nc.scalar.activation(STAT(sb_ + 3), STAT(sb_ + 3), AF.Sqrt))
            nc.vector.wait_ge(ss.h, ss.n)
            sv.inc(nc.vector.reciprocal(STAT(sb_ + 4), STAT(sb_ + 3)))
            nc.scalar.wait_ge(sv.h, sv.n)
            ss.inc(nc.scalar.activation(xt, xt, AF.Identity, scale=STAT(sb_ + 4)))
            nc.vector.wait_ge(ss.h, ss.n)
            nc.vector.tensor_tensor(xt, xt, bf(b_modA, 1024), op=AO.mult)
            sv.inc(nc.vector.tensor_tensor(xt, xt, bf(b_modA + 1024, 1024), op=AO.add))
            nc.vector.tensor_reduce(STAT(sb_ + 5), xt, AX, AO.max,
                                    apply_absolute_value=True)
            nc.vector.drain()
            nc.vector.tensor_scalar(STAT(sb_ + 5), STAT(sb_ + 5), 1e-6, None, op0=AO.max)
            nc.vector.drain()
            nc.vector.reciprocal(STAT(sb_ + 6), STAT(sb_ + 5))
            nc.vector.drain()
            nc.vector.tensor_scalar(STAT(sb_ + 6), STAT(sb_ + 6), 7.0, None, op0=AO.mult)
            sv.inc(nc.vector.tensor_scalar(STAT(sb_ + 7), STAT(sb_ + 5), 1.0 / 7.0,
                                           None, op0=AO.mult))
            nc.scalar.wait_ge(sv.h, sv.n)
            ss.inc(nc.scalar.activation(xt, xt, AF.Identity, scale=STAT(sb_ + 6)))
            nc.vector.wait_ge(ss.h, ss.n)
            nc.vector.tensor_scalar(f32(ftr2, 1024), xt, MAGIC, None, op0=AO.add)
            nc.vector.tensor_scalar(xt, f32(ftr2, 1024), -MAGIC, None, op0=AO.add)
            sv.inc(nc.vector.tensor_copy(i32f(ftr2, 1024), i32f(foff, 1024)))
            slot_v[i] = sv.n
            nc.scalar.wait_ge(sv.h, sv.n)
            ss.inc(nc.scalar.activation(bf(G1 + 1024 * i, 1024), f32(ftr2, 1024),
                                        AF.Identity, scale=STAT(sb_ + 7)))
            slot_s[i] = ss.n
            nc.sync.wait_ge(ss.h, ss.n)
            for kc in range(KC8):
                src = bf(G1 + 1024 * i, 1024)[:, 128 * kc:128 * kc + 128]
                dst = bf(b_xqT + 2048 * kc, 2048)[:, 128 * i:128 * i + 128]
                cha.issue_t(nc.sync, dst, src)
        if dbg("p1rnd", [128, 1024]):
            dbg_store("p1rnd", dbg_t["p1rnd"], f32(ftr2, 1024), ss)
        if dbg("p1st", [128, 64]):
            dbg_store("p1st", dbg_t["p1st"], f32(f_stat, 64), ss)
        if dbg("hq", [N, D]):
            for i in range(NT):
                dbg_store("hq", dbg_t["hq"][128 * i:128 * i + 128, :],
                          bf(G1 + 1024 * i, 1024), ss)
        barrier()
        if stop_after == 2:
            cho.wait_all(nc.gpsimd); chg.wait_all(nc.gpsimd)
            cha.wait_all(nc.gpsimd); chs.wait_all(nc.gpsimd)
            return nc, sorted(dbg_t)

        t_qbrow = cha.issue(nc.sync, out=f32(f_qbrow, 1024),
                            in_=wqb_in[None, :].to_broadcast((128, 1024)))
        cha.wait_all(nc.vector)
        chs.wait_all(nc.vector)
        for i in range(NT):
            p65t = bf(a_p65 + 1040 * i, 1040).rearrange("p (c w) -> p c w", w=65)
            inst = nc.vector.memset(p65t[:, :, 64:65], 1.0)
        sv.inc(inst)
        ones65 = sv.n
        cha.wait_all(nc.tensor)
        chs.wait_all(nc.tensor)
        evq = []
        for ot in range(8):
            ps = ps_a if ot % 2 == 0 else ps_b
            for sl in range(4):
                k = ot * 4 + sl
                tgt = ps[:, 512 * (sl % 2):512 * (sl % 2) + 512]
                if k >= 2:
                    nc.tensor.wait_ge(sv.h, evq[k - 2])
                for kc in range(KC8):
                    inst = nc.tensor.matmul(
                        tgt, bf(b_wtT + 1024 * kc, 1024)[:, 128 * ot:128 * ot + 128],
                        bf(b_xqT + 2048 * kc, 2048)[:, 512 * sl:512 * sl + 512],
                        start=(kc == 0), stop=(kc == 7))
                st_.inc(inst)
                nc.vector.wait_ge(st_.h, st_.n)
                if k == 0:
                    W(nc.vector, t_qbc)
                sv.inc(nc.vector.tensor_scalar(
                    bf(b_projT + 2048 * ot, 2048)[:, 512 * sl:512 * sl + 512],
                    tgt, f32(f_qbc, 8)[:, ot:ot + 1], None, op0=AO.add))
                evq.append(sv.n)
        W(nc.tensor, t_qbrow)
        ev2 = []
        for i in range(NT):
            for s2 in range(2):
                kk = 2 * i + s2
                ps = ps_a if kk % 2 == 0 else ps_b
                tgt = ps[:, 0:512]
                if kk >= 2:
                    nc.tensor.wait_ge(sv.h, ev2[kk - 2])
                for kc in range(KC8):
                    inst = nc.tensor.matmul(
                        tgt, bf(b_xqT + 2048 * kc, 2048)[:, 128 * i:128 * i + 128],
                        bf(b_wtT + 1024 * kc, 1024)[:, 512 * s2:512 * s2 + 512],
                        start=(kc == 0), stop=(kc == 7))
                st_.inc(inst)
                nc.vector.wait_ge(st_.h, st_.n)
                if kk == 0:
                    nc.vector.wait_ge(sv.h, ones65)
                    W(nc.vector, t_qbrow)
                p65t = bf(a_p65 + 1040 * i, 1040).rearrange("p (c w) -> p c w", w=65)
                sv.inc(nc.vector.tensor_tensor(
                    p65t[:, 8 * s2:8 * s2 + 8, 0:64],
                    tgt.rearrange("p (c w) -> p c w", w=64),
                    f32(f_qbrow, 1024)[:, 512 * s2:512 * s2 + 512]
                    .rearrange("p (c w) -> p c w", w=64),
                    op=AO.add))
                ev2.append(sv.n)
        if dbg("projT", [128, 8 * 2048]):
            dbg_store("projT", dbg_t["projT"], bf(b_projT, 16384), sv)
        if dbg("p65", [128, 16 * 1040]):
            dbg_store("p65", dbg_t["p65"], bf(a_p65, 16640), sv)
        barrier()
        if stop_after == 3:
            cho.wait_all(nc.gpsimd); chg.wait_all(nc.gpsimd)
            cha.wait_all(nc.gpsimd); chs.wait_all(nc.gpsimd)
            return nc, sorted(dbg_t)

        nc.vector.memset(bf(b_kla, 2048, p0=32, p1=33), 1.0)
        sv.inc(nc.vector.memset(f32(f_rl, 16), 1.0))
        yev_prev = sv.n
        scores_done_prev = 0
        pv_prev = {}
        exp_prev = {}
        rows = lambda r: f32(f_rows, 2048, p0=32 * r, p1=32 * r + 1)
        nheads = int(p1_mode[1:]) if (p1_mode or "").startswith("h") else \
            (1 if (p1_mode or "").startswith("s") else H)
        for h in range(nheads):
            hb = 64 * (h % 2)
            projt = bf(b_projT + 2048 * (h // 2), 2048)
            nc.vector.wait_ge(st_.h, scores_done_prev)
            nc.vector.tensor_tensor(bf(b_e12q, 2048, p0=0, p1=64),
                                    projt[hb:hb + 64, :],
                                    bf(b_tab, 2048)[hb:hb + 64, :], op=AO.mult)
            sv.inc(nc.vector.tensor_tensor(bf(b_e12q, 2048, p0=64, p1=128),
                                           projt[hb:hb + 64, :],
                                           bf(b_tab + 2048, 2048)[hb:hb + 64, :],
                                           op=AO.mult))
            for c in range(16):
                cb_ = 64 * (c % 2)
                src = bf(b_projT + 2048 * (c // 2), 2048)[cb_:cb_ + 64,
                                                          128 * h:128 * h + 128]
                nc.vector.tensor_tensor(
                    bf(b_e12k, 2048, p0=0, p1=64)[:, 128 * c:128 * c + 128], src,
                    bf(b_tab + 4096, 2048)[cb_:cb_ + 64, 128 * c:128 * c + 128],
                    op=AO.mult)
                inst = nc.vector.tensor_tensor(
                    bf(b_e12k, 2048, p0=64, p1=128)[:, 128 * c:128 * c + 128], src,
                    bf(b_tab + 6144, 2048)[cb_:cb_ + 64, 128 * c:128 * c + 128],
                    op=AO.mult)
            sv.inc(inst)
            e12_done = sv.n
            if p1_mode == "s1":
                continue
            nc.tensor.wait_ge(sv.h, e12_done)
            if h == 0:
                nc.tensor.wait_ge(sv.h, ab_done)
            if h > 0:
                nc.tensor.wait_ge(ss.h, exp_prev[15])
                nc.tensor.wait_ge(sv.h, yev_prev)
            for which, aboff, e12off, laoff in ((0, 0, b_e12q, b_qla),
                                                (1, 32, b_e12k, b_kla)):
                if which == 1:
                    nc.tensor.wait_ge(sv.h, sv.n)
                    nc.tensor.wait_ge(ss.h, ss.n)
                for sl in range(4):
                    ps = (ps_a, ps_a, ps_b, ps_b)[sl]
                    tgt = ps[0:32, 512 * (sl % 2):512 * (sl % 2) + 512]
                    st_.inc(nc.tensor.matmul(tgt, bf(sm_ab, 64)[:, aboff:aboff + 32],
                                             bf(e12off, 2048)[:, 512 * sl:512 * sl + 512],
                                             start=True, stop=True))
                    if p1_mode == "s2a":
                        continue
                    nc.vector.wait_ge(st_.h, st_.n)
                    sv.inc(nc.vector.tensor_copy(
                        bf(laoff, 2048, p0=0, p1=32)[:, 512 * sl:512 * sl + 512], tgt))
                    if p1_mode == "s2b":
                        continue
                    nc.scalar.wait_ge(sv.h, sv.n)
                    ss.inc(nc.scalar.activation(
                        bf(b_e12q, 2048, p0=0, p1=32)[:, 512 * sl:512 * sl + 512],
                        tgt, AF.Square))
                nc.tensor.wait_ge(ss.h, ss.n)
                if p1_mode == "s2":
                    continue
                bev = []
                for sl in range(4):
                    tgt = ps_y[0:1, 512 * (sl % 2):512 * (sl % 2) + 512]
                    if sl >= 2:
                        nc.tensor.wait_ge(sv.h, bev[sl - 2])
                    st_.inc(nc.tensor.matmul(tgt, bf(sm_ones, 64)[0:32, 0:1],
                                             bf(b_e12q, 2048, p0=0, p1=32)[:, 512 * sl:512 * sl + 512],
                                             start=True, stop=True))
                    nc.vector.wait_ge(st_.h, st_.n)
                    sv.inc(nc.vector.tensor_copy(
                        rows(which)[:, 512 * sl:512 * sl + 512], tgt))
                    bev.append(sv.n)
            if p1_mode in ("s2", "s2a", "s2b", "s3"):
                continue
            sv.inc(nc.vector.tensor_reduce(STAT(16)[0:1, :], rows(1), AX, AO.max))
            nc.vector.drain()
            sv.inc(nc.vector.tensor_scalar(rows(2), rows(0), STAT(16)[0:1, :],
                                           None, op0=AO.mult))
            nc.scalar.wait_ge(sv.h, sv.n)
            ss.inc(nc.scalar.activation(rows(3), rows(2), AF.Sqrt))
            nc.vector.wait_ge(ss.h, ss.n)
            sv.inc(nc.vector.tensor_scalar(bf(b_qla, 2048, p0=32, p1=33),
                                           rows(3), -1.0, None, op0=AO.mult))
            qla_done = sv.n
            if h == 0 and dbg("ql0", [33, N]):
                dbg_store("ql0", dbg_t["ql0"][0:33, :], bf(b_qla, 2048, p0=0, p1=33), sv)
            if h == 0 and dbg("kl0", [33, N]):
                dbg_store("kl0", dbg_t["kl0"][0:33, :], bf(b_kla, 2048, p0=0, p1=33), sv)
            nc.tensor.wait_ge(sv.h, qla_done)
            if p1_mode == "s4":
                continue
            exp_done = {}
            pv_done = {}
            for c in range(16):
                pt = bf(b_pt + 2048 * (c % 2), 2048)
                if c >= 1:
                    nc.tensor.wait_ge(ss.h, exp_done[c - 1])
                for half, psx in ((0, ps_a), (1, ps_b)):
                    for q2 in range(2):
                        st_.inc(nc.tensor.matmul(
                            psx[:, 512 * q2:512 * q2 + 512],
                            bf(b_kla, 2048, p0=0, p1=33)[:, 128 * c:128 * c + 128],
                            bf(b_qla, 2048, p0=0, p1=33)[:, 1024 * half + 512 * q2:
                                                         1024 * half + 512 * q2 + 512],
                            start=True, stop=True))
                    nc.scalar.wait_ge(st_.h, st_.n)
                    if c >= 2:
                        nc.scalar.wait_ge(st_.h, pv_done[c - 2])
                    elif h > 0:
                        nc.scalar.wait_ge(st_.h, pv_prev[c + 14])
                    ss.inc(nc.scalar.activation(pt[:, 1024 * half:1024 * half + 1024],
                                                psx[:, 0:1024], AF.Exp))
                exp_done[c] = ss.n
                if p1_mode == "s5":
                    pv_done[c] = st_.n
                    if c == 3:
                        scores_done = st_.n
                    continue
                nc.tensor.wait_ge(ss.h, exp_done[c])
                for j in range(16):
                    yc = 512 * (j // 7) + 65 * (j % 7) if j < 14 else 1024 + 65 * (j - 14)
                    inst = nc.tensor.matmul(
                        ps_y[:, yc:yc + 65], pt[:, 128 * j:128 * j + 128],
                        bf(a_p65 + 1040 * h, 1040)[:, 65 * c:65 * c + 65],
                        start=(c == 0 and j in (0, 7, 14)),
                        stop=(c == 15 and j in (6, 13, 15)))
                st_.inc(inst)
                pv_done[c] = st_.n
                if c == 3:
                    scores_done = st_.n
            scores_done_prev = scores_done
            exp_prev, pv_prev = exp_done, pv_done
            if p1_mode == "s5":
                continue
            nc.vector.wait_ge(st_.h, pv_done[15])
            for j in range(16):
                yc = 512 * (j // 7) + 65 * (j % 7) if j < 14 else 1024 + 65 * (j - 14)
                nc.vector.reciprocal(f32(f_rl, 16)[:, j:j + 1], ps_y[:, yc + 64:yc + 65])
            nc.vector.drain()
            for j in range(16):
                yc = 512 * (j // 7) + 65 * (j % 7) if j < 14 else 1024 + 65 * (j - 14)
                inst = nc.vector.tensor_scalar(
                    ytile(j)[:, 64 * h:64 * h + 64],
                    ps_y[:, yc:yc + 64], f32(f_rl, 16)[:, j:j + 1], None, op0=AO.mult)
            sv.inc(inst)
            yev_prev = sv.n
        if dbg("yattn", [N, D]):
            for i in range(NT):
                dbg_store("yattn", dbg_t["yattn"][128 * i:128 * i + 128, :], ytile(i), sv)
        barrier()
        if stop_after == 4:
            cho.wait_all(nc.gpsimd); chg.wait_all(nc.gpsimd)
            cha.wait_all(nc.gpsimd); chs.wait_all(nc.gpsimd)
            return nc, sorted(dbg_t)

        yqT_toks = {}
        ev3 = []
        xsp_toks = {}

        def p4_quant(i):
            par = i % 2
            sb_ = 8 * par
            yt = ytile(i)
            foff = ftr0 + 1024 * par
            zt = f32(foff, 1024)
            if i >= 2:
                nc.vector.wait_ge(ss.h, slot_s[i - 2])
            nc.vector.tensor_reduce(STAT(sb_ + 5), yt, AX, AO.max,
                                    apply_absolute_value=True)
            nc.vector.drain()
            nc.vector.tensor_scalar(STAT(sb_ + 5), STAT(sb_ + 5), 1e-6, None, op0=AO.max)
            nc.vector.drain()
            nc.vector.reciprocal(STAT(sb_ + 6), STAT(sb_ + 5))
            nc.vector.drain()
            nc.vector.tensor_scalar(STAT(sb_ + 6), STAT(sb_ + 6), 7.0, None, op0=AO.mult)
            sv.inc(nc.vector.tensor_scalar(STAT(sb_ + 7), STAT(sb_ + 5), 1.0 / 7.0,
                                           None, op0=AO.mult))
            nc.scalar.wait_ge(sv.h, sv.n)
            ss.inc(nc.scalar.activation(zt, yt, AF.Identity, scale=STAT(sb_ + 6)))
            nc.vector.wait_ge(ss.h, ss.n)
            nc.vector.tensor_scalar(f32(ftr2, 1024), zt, MAGIC, None, op0=AO.add)
            nc.vector.tensor_scalar(zt, f32(ftr2, 1024), -MAGIC, None, op0=AO.add)
            sv.inc(nc.vector.tensor_copy(i32f(ftr2, 1024), i32f(foff, 1024)))
            nc.scalar.wait_ge(sv.h, sv.n)
            ss.inc(nc.scalar.activation(yt, f32(ftr2, 1024), AF.Identity,
                                        scale=STAT(sb_ + 7)))
            slot_s[i] = ss.n
            nc.sync.wait_ge(ss.h, ss.n)
            toks = []
            for kc in range(KC8):
                src = yt[:, 128 * kc:128 * kc + 128]
                dst = bf(G1 + 2048 * kc, 2048)[:, 128 * i:128 * i + 128]
                toks.append(cha.issue_t(nc.sync, dst, src))
            yqT_toks[i] = toks

        def p4_wo(i):
            par = i % 2
            xt = f32(3072 + 1024 * par, 1024)
            x2t = f32(5120, 1024)
            if i >= 2:
                nc.sync.wait_ge(sv.h, ev3[2 * i - 3])
            tx = cha.issue(nc.sync, out=xt, in_=x_in[128 * i:128 * i + 128, :])
            for t in yqT_toks[i]:
                W(nc.tensor, t)
            for s2 in range(2):
                kk = 2 * i + s2
                ps = ps_a if kk % 2 == 0 else ps_b
                tgt = ps[:, 0:512]
                if kk >= 2:
                    nc.tensor.wait_ge(sv.h, ev3[kk - 2])
                for kc in range(KC8):
                    inst = nc.tensor.matmul(
                        tgt, bf(G1 + 2048 * kc, 2048)[:, 128 * i:128 * i + 128],
                        bf(b_woT + 1024 * kc, 1024)[:, 512 * s2:512 * s2 + 512],
                        start=(kc == 0), stop=(kc == 7))
                st_.inc(inst)
                nc.vector.wait_ge(st_.h, st_.n)
                if s2 == 0:
                    W(nc.vector, tx)
                    if i >= 1:
                        W(nc.vector, xsp_toks[i - 1])
                nc.vector.tensor_tensor(tgt, tgt,
                                        bf(b_modA + 2048, 1024)[:, 512 * s2:512 * s2 + 512],
                                        op=AO.mult)
                sv.inc(nc.vector.tensor_tensor(x2t[:, 512 * s2:512 * s2 + 512], tgt,
                                               xt[:, 512 * s2:512 * s2 + 512], op=AO.add))
                ev3.append(sv.n)
            nc.sync.wait_ge(sv.h, sv.n)
            xsp_toks[i] = cho.issue(nc.sync, out=d_x2[128 * i:128 * i + 128, :], in_=x2t)

        for i in range(NT + 1):
            if i < NT:
                p4_quant(i)
            if i >= 1:
                p4_wo(i - 1)
        if dbg("yq", [N, D]):
            for i in range(NT):
                dbg_store("yq", dbg_t["yq"][128 * i:128 * i + 128, :], ytile(i), ss)
        if dbg("x2", [N, D]):
            for i in range(NT):
                nc.gpsimd.wait_ge(xsp_toks[i][0], xsp_toks[i][1])
                tokd = chg.issue(nc.gpsimd, out=dbg_t["x2"][128 * i:128 * i + 128, :],
                                 in_=d_x2[128 * i:128 * i + 128, :])
                W(nc.gpsimd, tokd)
        barrier()
        if stop_after == 5:
            cho.wait_all(nc.gpsimd); chg.wait_all(nc.gpsimd)
            cha.wait_all(nc.gpsimd); chs.wait_all(nc.gpsimd)
            return nc, sorted(dbg_t)

        mod_bcast([(b_modB + 1024, "cp", 3), (b_modB, "ca", 4), (b_gb, "g", 5)])
        t_gn = cha.issue(nc.sync, out=f32(f_gn, 1024),
                         in_=ng_in[None, :].to_broadcast((128, 1024)))
        t_bn2 = cha.issue(nc.sync, out=f32(f_bn, 1024),
                          in_=nb_in[None, :].to_broadcast((128, 1024)))
        modb_done = sv.n
        if dbg("mod", [128, 6 * 1024]):
            for jj, off in enumerate([b_modB + 1024, b_modB, b_gb]):
                dbg_store("mod", dbg_t["mod"][:, 1024 * (jj + 3):1024 * (jj + 4)],
                          bf(off, 1024), sv)
        for i in range(NT):
            par = i % 2
            sb_ = 8 * par
            xt = f32(ftr0 + 1024 * par, 1024)
            if i >= 2:
                nc.sync.wait_ge(sv.h, slot_v[i - 2])
            else:
                nc.sync.wait_ge(sv.h, modb_done)
            W(nc.sync, xsp_toks[i])
            tx = cha.issue(nc.sync, out=xt, in_=d_x2[128 * i:128 * i + 128, :])
            W(nc.vector, tx)

            def ln_pass(mul_off, add_off, affine_f32):
                nc.vector.tensor_reduce(STAT(sb_ + 0), xt, AX, AO.add)
                nc.vector.drain()
                sv.inc(nc.vector.tensor_scalar(STAT(sb_ + 1), STAT(sb_ + 0), -1.0 / D,
                                               None, op0=AO.mult))
                nc.scalar.wait_ge(sv.h, sv.n)
                ss.inc(nc.scalar.activation(xt, xt, AF.Identity, bias=STAT(sb_ + 1)))
                ss.inc(nc.scalar.activation(bf(b_j6, 1024), xt, AF.Square,
                                            accum_out=STAT(sb_ + 2)))
                nc.vector.wait_ge(ss.h, ss.n)
                sv.inc(nc.vector.tensor_scalar(STAT(sb_ + 3), STAT(sb_ + 2), 1.0 / D,
                                               EPS, op0=AO.mult, op1=AO.add))
                nc.scalar.wait_ge(sv.h, sv.n)
                ss.inc(nc.scalar.activation(STAT(sb_ + 3), STAT(sb_ + 3), AF.Sqrt))
                nc.vector.wait_ge(ss.h, ss.n)
                sv.inc(nc.vector.reciprocal(STAT(sb_ + 4), STAT(sb_ + 3)))
                nc.scalar.wait_ge(sv.h, sv.n)
                ss.inc(nc.scalar.activation(xt, xt, AF.Identity, scale=STAT(sb_ + 4)))
                nc.vector.wait_ge(ss.h, ss.n)
                if affine_f32:
                    W(nc.vector, t_gn)
                    W(nc.vector, t_bn2)
                    nc.vector.tensor_tensor(xt, xt, f32(mul_off, 1024), op=AO.mult)
                    sv.inc(nc.vector.tensor_tensor(bf(G1 + 1024 * i, 1024), xt,
                                                   f32(add_off, 1024), op=AO.add))
                else:
                    nc.vector.tensor_tensor(xt, xt, bf(mul_off, 1024), op=AO.mult)
                    sv.inc(nc.vector.tensor_tensor(xt, xt, bf(add_off, 1024), op=AO.add))
            ln_pass(b_modB, b_modB + 1024, False)
            ln_pass(f_gn, f_bn, True)
            slot_v[i] = sv.n
            nc.sync.wait_ge(sv.h, sv.n)
            for kc in range(KC8):
                src = bf(G1 + 1024 * i, 1024)[:, 128 * kc:128 * kc + 128]
                dst = bf(b_xnT + 2048 * kc, 2048)[:, 128 * i:128 * i + 128]
                cha.issue_t(nc.sync, dst, src)
        if dbg("xn", [N, D]):
            for i in range(NT):
                dbg_store("xn", dbg_t["xn"][128 * i:128 * i + 128, :],
                          bf(G1 + 1024 * i, 1024), sv)
        barrier()
        if stop_after == 6:
            cho.wait_all(nc.gpsimd); chg.wait_all(nc.gpsimd)
            cha.wait_all(nc.gpsimd); chs.wait_all(nc.gpsimd)
            return nc, sorted(dbg_t)

        se_toks = [chg.issue(nc.gpsimd, out=bf(b_seN + 1024 * et, 1024),
                             in_=semb_in[128 * et:128 * et + 128, :]) for et in range(8)]
        t_sg = cha.issue(nc.sync, out=f32(f_gn, 1024),
                         in_=sg_in[None, :].to_broadcast((128, 1024)))
        t_sb = cha.issue(nc.sync, out=f32(f_bn, 1024),
                         in_=sbv_in[None, :].to_broadcast((128, 1024)))
        t_sc = cha.issue(nc.sync, out=f32(f_sccol, 1),
                         in_=sc_in[0:1][None, :].to_broadcast((128, 1)))
        W(nc.vector, t_sg); W(nc.vector, t_sb); W(nc.vector, t_sc)
        nc.vector.tensor_scalar(f32(f_gn, 1024), f32(f_gn, 1024),
                                f32(f_sccol, 1), None, op0=AO.mult)
        sv.inc(nc.vector.tensor_scalar(f32(f_bn, 1024), f32(f_bn, 1024),
                                       f32(f_sccol, 1), None, op0=AO.mult))
        for et in range(8):
            par = et % 2
            sb_ = 8 * par
            set_ = bf(b_seN + 1024 * et, 1024)
            zt = f32(ftr0 + 1024 * par, 1024)
            W(nc.vector, se_toks[et])
            if et >= 2:
                nc.vector.wait_ge(ss.h, slot_s[et - 2])
            nc.vector.tensor_reduce(STAT(sb_ + 0), set_, AX, AO.add)
            nc.vector.drain()
            sv.inc(nc.vector.tensor_scalar(STAT(sb_ + 1), STAT(sb_ + 0), -1.0 / D,
                                           None, op0=AO.mult))
            nc.scalar.wait_ge(sv.h, sv.n)
            ss.inc(nc.scalar.activation(zt, set_, AF.Identity, bias=STAT(sb_ + 1)))
            ss.inc(nc.scalar.activation(bf(b_j6, 1024), zt, AF.Square,
                                        accum_out=STAT(sb_ + 2)))
            nc.vector.wait_ge(ss.h, ss.n)
            sv.inc(nc.vector.tensor_scalar(STAT(sb_ + 3), STAT(sb_ + 2), 1.0 / D, EPS,
                                           op0=AO.mult, op1=AO.add))
            nc.scalar.wait_ge(sv.h, sv.n)
            ss.inc(nc.scalar.activation(STAT(sb_ + 3), STAT(sb_ + 3), AF.Sqrt))
            nc.vector.wait_ge(ss.h, ss.n)
            sv.inc(nc.vector.reciprocal(STAT(sb_ + 4), STAT(sb_ + 3)))
            nc.scalar.wait_ge(sv.h, sv.n)
            ss.inc(nc.scalar.activation(zt, zt, AF.Identity, scale=STAT(sb_ + 4)))
            slot_s[et] = ss.n
            nc.vector.wait_ge(ss.h, ss.n)
            nc.vector.tensor_tensor(zt, zt, f32(f_gn, 1024), op=AO.mult)
            sv.inc(nc.vector.tensor_tensor(set_, zt, f32(f_bn, 1024), op=AO.add))
        items = []
        for et in range(8):
            for kc in range(KC8):
                items.append((bf(b_seT + 1024 * kc, 1024)[:, 128 * et:128 * et + 128],
                              bf(b_seN + 1024 * et, 1024)[:, 128 * kc:128 * kc + 128]))
        dma_transpose_batch(items, sv, sv.n)
        cha.wait_all(nc.tensor)
        chs.wait_all(nc.tensor)
        ev4 = []
        for et in range(8):
            for sl in range(4):
                kk = 4 * et + sl
                ps = ps_a if kk % 2 == 0 else ps_b
                tgt = ps[:, 0:512]
                if kk >= 2:
                    nc.tensor.wait_ge(sv.h, ev4[kk - 2])
                for kc in range(KC8):
                    inst = nc.tensor.matmul(
                        tgt, bf(b_seT + 1024 * kc, 1024)[:, 128 * et:128 * et + 128],
                        bf(b_xnT + 2048 * kc, 2048)[:, 512 * sl:512 * sl + 512],
                        start=(kc == 0), stop=(kc == 7))
                st_.inc(inst)
                nc.vector.wait_ge(st_.h, st_.n)
                sv.inc(nc.vector.tensor_copy(
                    bf(b_lgT + 2048 * et, 2048)[:, 512 * sl:512 * sl + 512], tgt))
                ev4.append(sv.n)
        if dbg("logitsT", [128, 8 * 2048]):
            dbg_store("logitsT", dbg_t["logitsT"], bf(b_lgT, 16384), sv)
        barrier()
        if stop_after == 7:
            cho.wait_all(nc.gpsimd); chg.wait_all(nc.gpsimd)
            cha.wait_all(nc.gpsimd); chs.wait_all(nc.gpsimd)
            return nc, sorted(dbg_t)

        nc.vector.tensor_tensor(f32(f_mx, 2048), bf(b_lgT, 2048),
                                bf(b_lgT + 2048, 2048), op=AO.max)
        for et in range(2, 8):
            nc.vector.tensor_tensor(f32(f_mx, 2048), f32(f_mx, 2048),
                                    bf(b_lgT + 2048 * et, 2048), op=AO.max)
        sv.inc(nc.vector.tensor_copy(bf(b_mxb, 2048), f32(f_mx, 2048)))
        red_after = {}
        mxv0 = sv.n
        for t in range(16):
            if t >= 2:
                nc.sync.wait_ge(sv.h, red_after[t - 2])
            if t == 0:
                nc.sync.wait_ge(sv.h, mxv0)
            tokt = cha.issue_t(nc.sync, bf(sm_mxt + 128 * (t % 2), 128),
                               bf(b_mxb, 2048)[:, 128 * t:128 * t + 128])
            W(nc.vector, tokt)
            sv.inc(nc.vector.tensor_reduce(f32(f_rl, 16)[:, t:t + 1],
                                           bf(sm_mxt + 128 * (t % 2), 128), AX, AO.max))
            red_after[t] = sv.n
        nc.sync.wait_ge(sv.h, sv.n)
        t6 = cha.issue(nc.sync, out=d_bn[0:2048].rearrange("(t p) -> p t", p=128),
                       in_=f32(f_rl, 16), allow_slow_non_contiguous=True)
        W(nc.sync, t6)
        t7 = cha.issue(nc.sync, out=f32(f_mx, 2048),
                       in_=d_bn[0:2048][None, :].to_broadcast((128, 2048)))
        W(nc.vector, t7)
        for et in range(8):
            sv.inc(nc.vector.tensor_tensor(f32(f_sub, 2048), bf(b_lgT + 2048 * et, 2048),
                                           f32(f_mx, 2048), op=AO.subtract))
            nc.scalar.wait_ge(sv.h, sv.n)
            ss.inc(nc.scalar.activation(bf(b_cu + 2048 * et, 2048), f32(f_sub, 2048),
                                        AF.Exp))
            if et < 7:
                nc.vector.wait_ge(ss.h, ss.n)
        nc.tensor.wait_ge(ss.h, ss.n)
        lcv = []
        for sl in range(4):
            if sl >= 1:
                nc.tensor.wait_ge(sv.h, lcv[sl - 1])
            for et in range(8):
                inst = nc.tensor.matmul(ps_s[0:1, 0:512], bf(sm_ones, 64)[:, 0:1],
                                        bf(b_cu + 2048 * et, 2048)[:, 512 * sl:512 * sl + 512],
                                        start=(et == 0), stop=(et == 7))
            st_.inc(inst)
            nc.vector.wait_ge(st_.h, st_.n)
            sv.inc(nc.vector.tensor_copy(
                f32(f_lcrow, 2048, p1=1)[:, 512 * sl:512 * sl + 512],
                ps_s[0:1, 0:512]))
            lcv.append(sv.n)
        sv.inc(nc.vector.reciprocal(f32(f_lcrow, 2048, p1=1), f32(f_lcrow, 2048, p1=1)))
        nc.sync.wait_ge(sv.h, sv.n)
        t8 = cha.issue(nc.sync, out=d_bn[2048:4096][None, :],
                       in_=f32(f_lcrow, 2048, p1=1))
        W(nc.sync, t8)
        t_rlc = cha.issue(nc.sync, out=f32(f_rlc, 16),
                          in_=d_bn[2048:4096].rearrange("(t p) -> p t", p=128),
                          allow_slow_non_contiguous=True)
        if dbg("combUT", [128, 8 * 2048]):
            dbg_store("combUT", dbg_t["combUT"], bf(b_cu, 16384), ss)
        if dbg("rlc", [128, 16]):
            nc.gpsimd.wait_ge(t_rlc[0], t_rlc[1])
            tokd = chg.issue(nc.gpsimd, out=dbg_t["rlc"], in_=f32(f_rlc, 16))
            W(nc.gpsimd, tokd)
        barrier()
        if stop_after == 8:
            cho.wait_all(nc.gpsimd); chg.wait_all(nc.gpsimd)
            cha.wait_all(nc.gpsimd); chs.wait_all(nc.gpsimd)
            return nc, sorted(dbg_t)

        for et in range(8):
            lg = bf(b_lgT + 2048 * et, 2048)
            nc.vector.tensor_reduce(STAT(0), lg, AX, AO.max)
            nc.vector.drain()
            sv.inc(nc.vector.tensor_scalar(STAT(1), STAT(0), -1.0, None, op0=AO.mult))
            nc.scalar.wait_ge(sv.h, sv.n)
            ss.inc(nc.scalar.activation(lg, lg, AF.Exp, bias=STAT(1),
                                        accum_out=STAT(2)))
            nc.vector.wait_ge(ss.h, ss.n)
            sv.inc(nc.vector.reciprocal(STAT(3), STAT(2)))
            nc.scalar.wait_ge(sv.h, sv.n)
            ss.inc(nc.scalar.activation(lg, lg, AF.Identity, scale=STAT(3)))
            nc.sync.wait_ge(ss.h, ss.n)
            for ntx in range(NT):
                src = lg[:, 128 * ntx:128 * ntx + 128]
                dst = bf(b_disp + 1024 * ntx, 1024)[:, 128 * et:128 * et + 128]
                cha.issue_t(nc.sync, dst, src)
        if dbg("dispT", [128, 8 * 2048]):
            dbg_store("dispT", dbg_t["dispT"], bf(b_lgT, 16384), ss)
        barrier()
        if stop_after == 9:
            cho.wait_all(nc.gpsimd); chg.wait_all(nc.gpsimd)
            cha.wait_all(nc.gpsimd); chs.wait_all(nc.gpsimd)
            return nc, sorted(dbg_t)

        cha.wait_all(nc.tensor)
        chs.wait_all(nc.tensor)
        ev5 = []
        for dt in range(8):
            for es2 in range(2):
                kk = 2 * dt + es2
                ps = ps_a if kk % 2 == 0 else ps_b
                tgt = ps[:, 0:512]
                if kk >= 2:
                    nc.tensor.wait_ge(sv.h, ev5[kk - 2])
                for ncx in range(NT):
                    inst = nc.tensor.matmul(
                        tgt, bf(G1 + 1024 * ncx, 1024)[:, 128 * dt:128 * dt + 128],
                        bf(b_disp + 1024 * ncx, 1024)[:, 512 * es2:512 * es2 + 512],
                        start=(ncx == 0), stop=(ncx == 15))
                st_.inc(inst)
                nc.vector.wait_ge(st_.h, st_.n)
                sv.inc(nc.vector.tensor_copy(
                    bf(b_slt + 1024 * dt, 1024)[:, 512 * es2:512 * es2 + 512], tgt))
                ev5.append(sv.n)
        if dbg("slotsT", [128, 8 * 1024]):
            dbg_store("slotsT", dbg_t["slotsT"], bf(b_slt, 8192), sv)
        barrier()
        if stop_after == 10:
            cho.wait_all(nc.gpsimd); chg.wait_all(nc.gpsimd)
            cha.wait_all(nc.gpsimd); chs.wait_all(nc.gpsimd)
            return nc, sorted(dbg_t)

        ntoks = E * (2 * KC8 + 16)
        wtoks = chg.plan(ntoks)
        wg_ld, wu_ld, wd_ld = {}, {}, {}
        ti = 0
        for e in range(E):
            for kc in range(KC8):
                wg_ld[(e, kc)] = wtoks[ti]; ti += 1
                wu_ld[(e, kc)] = wtoks[ti]; ti += 1
            for kc in range(16):
                wd_ld[(e, kc)] = wtoks[ti]; ti += 1
        gumm, dnmm = {}, {}
        gelu_done, gu_done, eo_done = {}, {}, {}
        fb_toks = {}
        for e in range(E):
            epar = e % 2
            if e >= 2:
                nc.sync.wait_ge(ss.h, gelu_done[e - 2])
                nc.sync.wait_ge(sv.h, eo_done[e - 2])
            ta = cha.issue(nc.sync, out=f32(f_fb + 32 * epar, 16),
                           in_=bg_in[e].rearrange("(t p) -> p t", p=128),
                           allow_slow_non_contiguous=True)
            tb = cha.issue(nc.sync, out=f32(f_fb + 64 + 32 * epar, 16),
                           in_=bu_in[e].rearrange("(t p) -> p t", p=128),
                           allow_slow_non_contiguous=True)
            tc = cha.issue(nc.sync, out=f32(f_bd + 1024 * epar, 1024, p1=64),
                           in_=bd_in[e][None, :].to_broadcast((64, 1024)))
            fb_toks[e] = (ta, tb, tc)
            if e > 0:
                nc.tensor.wait_ge(ss.h, gelu_done[e - 1])
                nc.tensor.wait_ge(sv.h, gu_done[e - 1])
            for kc in range(KC8):
                W(nc.tensor, wu_ld[(e, kc)])
                for ft in range(FT):
                    nc.tensor.matmul(ps_a[:, 64 * ft:64 * ft + 64],
                                     bf(b_wg + 2048 * (kc % 4), 2048)[:, 128 * ft:128 * ft + 128],
                                     bf(b_slt + 1024 * kc, 1024)[:, 64 * e:64 * e + 64],
                                     start=(kc == 0 and ft in (0, 8)),
                                     stop=(kc == 7 and ft in (7, 15)))
                for ft in range(FT):
                    inst = nc.tensor.matmul(ps_b[:, 64 * ft:64 * ft + 64],
                                            bf(b_wu + 2048 * (kc % 4), 2048)[:, 128 * ft:128 * ft + 128],
                                            bf(b_slt + 1024 * kc, 1024)[:, 64 * e:64 * e + 64],
                                            start=(kc == 0 and ft in (0, 8)),
                                            stop=(kc == 7 and ft in (7, 15)))
                st_.inc(inst)
                gumm[(e, kc)] = st_.n
            nc.scalar.wait_ge(st_.h, gumm[(e, KC8 - 1)])
            W(nc.scalar, fb_toks[e][0])
            if e > 0:
                nc.scalar.wait_ge(st_.h, dnmm[(e - 1, 15)])
            for ft in range(FT):
                inst = nc.scalar.activation(bf(b_gu, 1024)[:, 64 * ft:64 * ft + 64],
                                            ps_a[:, 64 * ft:64 * ft + 64], AF.Gelu,
                                            bias=f32(f_fb + 32 * epar, 16)[:, ft:ft + 1])
            ss.inc(inst)
            gelu_done[e] = ss.n
            nc.vector.wait_ge(ss.h, gelu_done[e])
            W(nc.vector, fb_toks[e][1])
            for ft in range(FT):
                inst = nc.vector.scalar_tensor_tensor(
                    bf(b_gu, 1024)[:, 64 * ft:64 * ft + 64],
                    ps_b[:, 64 * ft:64 * ft + 64],
                    f32(f_fb + 64 + 32 * epar, 16)[:, ft:ft + 1],
                    bf(b_gu, 1024)[:, 64 * ft:64 * ft + 64],
                    op0=AO.add, op1=AO.mult)
            sv.inc(inst)
            gu_done[e] = sv.n
            nc.tensor.wait_ge(sv.h, gu_done[e])
            if e > 0:
                nc.tensor.wait_ge(sv.h, eo_done[e - 1])
            for kc in range(16):
                W(nc.tensor, wd_ld[(e, kc)])
                for s2 in range(2):
                    inst = nc.tensor.matmul(
                        ps_y[0:64, 512 * s2:512 * s2 + 512],
                        bf(b_gu, 1024)[:, 64 * kc:64 * kc + 64],
                        bf(b_wd + 1024 * (kc % 4), 1024)[:, 512 * s2:512 * s2 + 512],
                        start=(kc == 0), stop=(kc == 15))
                st_.inc(inst)
                dnmm[(e, kc)] = st_.n
            nc.vector.wait_ge(st_.h, dnmm[(e, 15)])
            W(nc.vector, fb_toks[e][2])
            eb = 64 * (e % 2)
            for s2 in range(2):
                inst = nc.vector.tensor_tensor(
                    bf(b_eo + 1024 * (e // 2), 1024)[eb:eb + 64, 512 * s2:512 * s2 + 512],
                    ps_y[0:64, 512 * s2:512 * s2 + 512],
                    f32(f_bd + 1024 * epar, 1024, p1=64)[:, 512 * s2:512 * s2 + 512],
                    op=AO.add)
            sv.inc(inst)
            eo_done[e] = sv.n
        for e in range(E):
            for kc in range(KC8):
                if e > 0 or kc >= 4:
                    pe, pk = (e, kc - 4) if kc >= 4 else (e - 1, kc + 4)
                    nc.gpsimd.wait_ge(st_.h, gumm[(pe, pk)])
                tok = chg.issue(nc.gpsimd, out=bf(b_wg + 2048 * (kc % 4), 2048),
                                in_=wg_in[e, 128 * kc:128 * kc + 128, :])
                assert tok == wg_ld[(e, kc)]
                tok = chg.issue(nc.gpsimd, out=bf(b_wu + 2048 * (kc % 4), 2048),
                                in_=wu_in[e, 128 * kc:128 * kc + 128, :])
                assert tok == wu_ld[(e, kc)]
            for kc in range(16):
                if e > 0 or kc >= 4:
                    pe, pk = (e, kc - 4) if kc >= 4 else (e - 1, kc + 12)
                    nc.gpsimd.wait_ge(st_.h, dnmm[(pe, pk)])
                tok = chg.issue(nc.gpsimd, out=bf(b_wd + 1024 * (kc % 4), 1024),
                                in_=wd_in[e, 128 * kc:128 * kc + 128, :])
                assert tok == wd_ld[(e, kc)]
        if dbg("eoutb", [128, 8 * 1024]):
            dbg_store("eoutb", dbg_t["eoutb"], bf(b_eo, 8192), sv)
        barrier()
        if stop_after == 11:
            cho.wait_all(nc.gpsimd); chg.wait_all(nc.gpsimd)
            cha.wait_all(nc.gpsimd); chs.wait_all(nc.gpsimd)
            return nc, sorted(dbg_t)

        W(nc.vector, t_rlc)
        ev6 = []
        y_toks = {}
        for i in range(NT):
            par = i % 2
            xt = f32(ftr0 + 1024 * par, 1024)
            yt = f32(ftr2, 1024)
            if i >= 2:
                nc.sync.wait_ge(sv.h, ev6[2 * i - 3])
            W(nc.sync, xsp_toks[i])
            tx = cha.issue(nc.sync, out=xt, in_=d_x2[128 * i:128 * i + 128, :])
            for s2 in range(2):
                kk = 2 * i + s2
                ps = ps_a if kk % 2 == 0 else ps_b
                tgt = ps[:, 0:512]
                if kk >= 2:
                    nc.tensor.wait_ge(sv.h, ev6[kk - 2])
                for et in range(8):
                    inst = nc.tensor.matmul(
                        tgt, bf(b_cu + 2048 * et, 2048)[:, 128 * i:128 * i + 128],
                        bf(b_eo + 1024 * et, 1024)[:, 512 * s2:512 * s2 + 512],
                        start=(et == 0), stop=(et == 7))
                st_.inc(inst)
                nc.vector.wait_ge(st_.h, st_.n)
                if s2 == 0:
                    W(nc.vector, tx)
                    if i >= 1:
                        W(nc.vector, y_toks[i - 1])
                nc.vector.tensor_scalar(tgt, tgt, f32(f_rlc, 16)[:, i:i + 1], None,
                                        op0=AO.mult)
                nc.vector.tensor_tensor(tgt, tgt,
                                        bf(b_gb, 1024)[:, 512 * s2:512 * s2 + 512],
                                        op=AO.mult)
                sv.inc(nc.vector.tensor_tensor(yt[:, 512 * s2:512 * s2 + 512], tgt,
                                               xt[:, 512 * s2:512 * s2 + 512], op=AO.add))
                ev6.append(sv.n)
            nc.sync.wait_ge(sv.h, sv.n)
            y_toks[i] = cho.issue(nc.sync, out=y_out[128 * i:128 * i + 128, :], in_=yt)
        cho.wait_all(nc.gpsimd)
        chg.wait_all(nc.gpsimd)
    return nc, sorted(dbg_t)


_CONSTS = host_consts()


def core_inputs(b, inputs):
    return {
        "x": np.ascontiguousarray(inputs["L0_attn_x"][b]),
        "cvec": np.ascontiguousarray(inputs["L0_c"][b]),
        "wq": inputs["L0_attn_Wqkv_w"], "wqb": inputs["L0_attn_Wqkv_b"],
        "u": inputs["L0_attn_U"], "wo": inputs["L0_attn_Wo_w"],
        "wmod": inputs["L0_wmod"],
        "ng": inputs["L0_ffn_norm_g"], "nb": inputs["L0_ffn_norm_b"],
        "sg": inputs["L0_ffn_slot_g"], "sbv": inputs["L0_ffn_slot_b"],
        "sc": inputs["L0_ffn_scale"],
        "semb": np.ascontiguousarray(np.asarray(inputs["L0_ffn_slot_embeds"]).reshape(E * S, D)),
        "wgx": inputs["L0_ffn_Wg"], "bgx": inputs["L0_ffn_bg"],
        "wux": inputs["L0_ffn_Wu"], "bux": inputs["L0_ffn_bu"],
        "wdx": inputs["L0_ffn_Wd"], "bdx": inputs["L0_ffn_bd"],
        **_CONSTS,
    }


def kernel(**inputs):
    nc, _ = build()
    in_maps = [core_inputs(b, inputs) for b in range(8)]
    res = run_bass_kernel_spmd(nc, in_maps, core_ids=list(range(8)))
    return np.stack([res.results[b]["y"] for b in range(8)], axis=0)


# version 07
# speedup vs baseline: 1.0352x; 1.0352x over previous
import numpy as np
from contextlib import ExitStack

import concourse.bass as bass
import concourse.mybir as mybir
from concourse.bass_utils import run_bass_kernel_spmd
from concourse.alu_op_type import AluOpType as AO

F32 = mybir.dt.float32
BF16 = mybir.dt.bfloat16
I32 = mybir.dt.int32
AF = mybir.ActivationFunctionType
AX = mybir.AxisListType.X

N, D, NT, KC8, H = 2048, 1024, 16, 8, 16
E, S, F, FT = 16, 64, 2048, 16
EPS = 1e-5
MAGIC = 12582912.0


def host_consts():
    half = 32
    inv = (1.0 / 10000.0 ** (np.arange(half, dtype=np.float32) / half))
    ang = np.arange(N, dtype=np.float32)[:, None] * inv
    cos, sin = np.cos(ang).astype(np.float32), np.sin(ang).astype(np.float32)
    cq64 = np.repeat(cos.T, 2, axis=0)[:64]
    sq64 = np.repeat(sin.T, 2, axis=0)[:64]
    mpos = np.empty(N, np.int64)
    for c in range(16):
        mpos[128 * c:128 * c + 128] = 16 * np.arange(128) + c
    ck64, sk64 = cq64[:, mpos], sq64[:, mpos]
    dup = lambda t: np.concatenate([t, t], 0).astype(np.float32)
    return {"tcq": dup(cq64), "tsq": dup(sq64), "tck": dup(ck64),
            "tsk": dup(sk64)}


def build(debug=(), stop_after=None, p1_mode=None):
    nc = bass.Bass("TRN2", target_bir_lowering=False, dynamic_dma_scratch_size=16384,
                   detect_race_conditions=False)
    dr = lambda nm, sh, kind="ExternalInput": nc.dram_tensor(nm, sh, F32, kind=kind).ap()
    x_in = dr("x", [N, D]); cvec_in = dr("cvec", [D])
    wq_in = dr("wq", [D, D]); wqb_in = dr("wqb", [D]); u_in = dr("u", [64, 32])
    wo_in = dr("wo", [D, D]); wmod_in = dr("wmod", [D, 6 * D])
    ng_in = dr("ng", [D]); nb_in = dr("nb", [D])
    sg_in = dr("sg", [D]); sbv_in = dr("sbv", [D]); sc_in = dr("sc", [1])
    semb_in = dr("semb", [E * S, D])
    wg_in = dr("wgx", [E, D, F]); bg_in = dr("bgx", [E, F])
    wu_in = dr("wux", [E, D, F]); bu_in = dr("bux", [E, F])
    wd_in = dr("wdx", [E, F, D]); bd_in = dr("bdx", [E, D])
    tcq_in = dr("tcq", [128, N]); tsq_in = dr("tsq", [128, N])
    tck_in = dr("tck", [128, N]); tsk_in = dr("tsk", [128, N])
    y_out = dr("y", [N, D], "ExternalOutput")
    d_x2 = dr("d_x2", [N, D], "Internal")
    d_bn = dr("d_bn", [8192], "Internal")

    dbg_t = {}
    def dbg(nm, sh):
        if nm in debug and nm not in dbg_t:
            dbg_t[nm] = nc.dram_tensor("dbg_" + nm, sh, F32, kind="ExternalOutput").ap()
        return nm in debug

    with ExitStack() as ctx:
        BFN = 83584
        FFN_ = 9984
        ARB = ctx.enter_context(nc.sbuf_tensor("ARB", [128, BFN], BF16))
        ARF = ctx.enter_context(nc.sbuf_tensor("ARF", [128, FFN_], F32))

        def bf(off, n, p0=0, p1=128):
            assert off + n <= BFN, (off, n)
            return ARB[p0:p1, off:off + n]

        def f32(off, nf, p0=0, p1=128):
            assert off + nf <= FFN_, (off, nf)
            return ARF[p0:p1, off:off + nf]

        def i32f(off, nf, p0=0, p1=128):
            return ARF[:, off:off + nf].bitcast(I32)[p0:p1, :]

        ftr0, ftr1, ftr2 = 0, 1024, 2048
        FW = 3072
        f_wtmp = FW
        f_qbrow = FW
        f_rows = FW
        f_gn, f_bn = FW, FW + 1024
        f_mx, f_sub, f_lcrow = FW, FW + 2048, FW + 4096
        f_bd, f_fb = FW, FW + 2048
        fsm = 9216
        f_stat = fsm
        f_alpha = fsm + 64
        f_thr = fsm + 80
        f_silu = fsm + 96
        f_qbc = fsm + 112
        f_rl = fsm + 128
        f_rlc = fsm + 144
        f_sccol = fsm + 160
        f_row = fsm + 176
        f_row2 = fsm + 688

        G1 = 0
        a_p65 = 0
        SMB = 16640
        sm_pad = SMB + 0
        sm_ab = SMB + 128
        sm_silub = SMB + 192
        sm_ones = SMB + 256
        sm_mxt = None
        B = 17024
        b_projT = B + 0
        b_lgT = B + 0
        b_wg = B + 24576
        b_wu = B + 0
        b_wd = B + 8192
        b_gu = B + 12288
        b_wtT = B + 16384
        b_ylo = B + 16384
        b_seN = B + 16384
        b_disp = B + 16384
        b_eo = B + 16384
        b_woT = B + 24576
        b_seT = B + 24576
        b_modA = B + 32768
        b_mxb = B + 32768
        b_slt = B + 32768
        b_tab = B + 35840
        b_modB = B + 36864
        sm_mxt = B + 38912
        b_j6 = B + 40960
        b_gb = B + 41984
        b_absw = B + 44032
        b_wmod = 0
        b_xqT = B + 44032
        b_pt = B + 44032
        b_e12q = B + 48128
        b_e12k = B + 50176
        b_kp = B + 52224
        b_qla = B + 54272
        b_kla = B + 56320
        b_xnT = B + 44032
        b_cu = B + 44032
        b_sgn = B + 52224
        b_abt = B + 52224
        b_wtv = B + 0
        b_yhi = B + 58368

        def ytile(i):
            return bf((b_ylo + 1024 * i) if i < 8 else (b_yhi + 1024 * (i - 8)), 1024)

        ps_a = ctx.enter_context(nc.psum_tensor("ps_a", [128, 1024], F32))
        ps_b = ctx.enter_context(nc.psum_tensor("ps_b", [128, 1024], F32))
        ps_y = ctx.enter_context(nc.psum_tensor("ps_y", [128, 1536], F32))
        ps_s = ctx.enter_context(nc.psum_tensor("ps_s", [128, 512], F32))

        class Sem:
            def __init__(self, nm):
                self.h = ctx.enter_context(nc.semaphore(nm))
                self.n = 0
            def inc(self, inst, k=1):
                inst.then_inc(self.h, k)
                self.n += k
                return self.n
        sv = Sem("sv"); st_ = Sem("st"); ss = Sem("ss"); bar = Sem("bar")

        class DmaChain:
            def __init__(self, name, k=4):
                self.sems = [Sem(f"{name}{i}") for i in range(k)]
                self.k = k
                self.i = 0
            def _pre(self, eng):
                s = self.sems[self.i % self.k]
                if s.n > 0:
                    eng.wait_ge(s.h, s.n)
                return s
            def issue(self, eng, **kw):
                s = self._pre(eng)
                s.inc(eng.dma_start(**kw), 16)
                self.i += 1
                return (s.h, s.n)
            def issue_t(self, eng, dst, src):
                s = self._pre(eng)
                s.inc(eng.dma_start_transpose(dst, src), 16)
                self.i += 1
                return (s.h, s.n)
            def plan(self, n):
                vals = {id(s): s.n for s in self.sems}
                toks = []
                for j in range(n):
                    s = self.sems[(self.i + j) % self.k]
                    vals[id(s)] += 16
                    toks.append((s.h, vals[id(s)]))
                return toks
            def wait_all(self, eng):
                for s in self.sems:
                    if s.n > 0:
                        eng.wait_ge(s.h, s.n)

        W = lambda eng, tok: eng.wait_ge(tok[0], tok[1])
        cha = DmaChain("cha", 10)
        chs = DmaChain("chs", 4)
        chg = DmaChain("chg", 8)
        cho = DmaChain("cho", 4)

        blk = ctx.enter_context(nc.Block())
        ENG = (nc.sync, nc.tensor, nc.vector, nc.scalar, nc.gpsimd)

        def barrier():
            for eng in ENG:
                eng.drain()
                eng.nop().then_inc(bar.h, 1)
            bar.n += 5
            for eng in ENG:
                eng.wait_ge(bar.h, bar.n)

        STAT = lambda j: f32(f_stat, 64)[:, j:j + 1]

        def dbg_store(nm, dst_ap, src_ap, eng_sem=None):
            if eng_sem is not None:
                nc.gpsimd.wait_ge(eng_sem.h, eng_sem.n)
            tok = chg.issue(nc.gpsimd, out=dst_ap, in_=src_ap)
            W(nc.gpsimd, tok)

        def dma_transpose_batch(items, gate_sem, gate_val):
            nc.sync.wait_ge(gate_sem.h, gate_val)
            for dst, src in items:
                cha.issue_t(nc.sync, dst, src)

        t_cv = cha.issue(nc.sync, out=f32(f_silu, 8),
                         in_=cvec_in.rearrange("(k p) -> p k", p=128),
                         allow_slow_non_contiguous=True)
        t_qbc = cha.issue(nc.sync, out=f32(f_qbc, 8),
                          in_=wqb_in.rearrange("(o p) -> p o", p=128),
                          allow_slow_non_contiguous=True)
        for o, t in ((b_tab, tcq_in), (b_tab + 2048, tsq_in),
                     (b_tab + 4096, tck_in), (b_tab + 6144, tsk_in)):
            chg.issue(nc.gpsimd, out=bf(o, 2048), in_=t)
        nc.vector.memset(bf(sm_ones, 64), 1.0)
        nc.vector.memset(f32(f_stat, 64), 0.0)
        sv.inc(nc.vector.memset(bf(sm_pad, 128), 0.0))
        nc.gpsimd.wait_ge(sv.h, sv.n)
        t_u = chg.issue(nc.gpsimd, out=bf(sm_pad, 128)[0:64, 0:32], in_=u_in)

        def ternary(w_in, wtT_off, scratch_guard_sv):
            nc.scalar.wait_ge(sv.h, scratch_guard_sv)
            nc.sync.wait_ge(ss.h, ss.n)
            for hh in range(2):
                toks = []
                for q in range(4):
                    if hh == 1 and q == 0:
                        nc.sync.wait_ge(ss.h, ss.n)
                    toks.append(cha.issue(
                        nc.sync, out=f32(f_wtmp + 1024 * q, 1024),
                        in_=w_in[128 * (4 * hh + q):128 * (4 * hh + q) + 128, :]))
                for q in range(4):
                    ot = 4 * hh + q
                    W(nc.scalar, toks[q])
                    ss.inc(nc.scalar.activation(
                        bf(b_absw + 1024 * ot, 1024), f32(f_wtmp + 1024 * q, 1024),
                        AF.Abs, accum_out=f32(f_alpha, 8)[:, ot:ot + 1]))
                    ss.inc(nc.scalar.activation(
                        bf(b_sgn + 1024 * ot, 1024), f32(f_wtmp + 1024 * q, 1024),
                        AF.Sign))
            nc.vector.wait_ge(ss.h, ss.n)
            sv.inc(nc.vector.tensor_reduce(f32(f_thr, 1), f32(f_alpha, 8), AX, AO.add))
            sv.inc(nc.vector.tensor_scalar(f32(f_alpha, 8), f32(f_alpha, 8),
                                           1.0 / D, None, op0=AO.mult))
            nc.sync.wait_ge(sv.h, sv.n)
            t1 = cha.issue(nc.sync, out=d_bn[0:128][:, None], in_=f32(f_thr, 1))
            W(nc.sync, t1)
            t2 = cha.issue(nc.sync, out=f32(f_row, 512, p1=1)[:, 0:128],
                           in_=d_bn[0:128][None, :])
            W(nc.vector, t2)
            sv.inc(nc.vector.tensor_reduce(f32(f_row2, 1, p1=1),
                                           f32(f_row, 512, p1=1)[:, 0:128], AX, AO.add))
            nc.vector.drain()
            sv.inc(nc.vector.tensor_scalar(f32(f_row2, 1, p1=1), f32(f_row2, 1, p1=1),
                                           0.05 / (D * D), None, op0=AO.mult))
            nc.sync.wait_ge(sv.h, sv.n)
            t3 = cha.issue(nc.sync, out=d_bn[128:129][None, :], in_=f32(f_row2, 1, p1=1))
            W(nc.sync, t3)
            t4 = cha.issue(nc.sync, out=f32(f_thr, 1),
                           in_=d_bn[128:129][None, :].to_broadcast((128, 1)))
            W(nc.vector, t4)
            cha.wait_all(nc.vector)
            chs.wait_all(nc.vector)
            for ot in range(8):
                nc.vector.scalar_tensor_tensor(bf(b_wtv + 1024 * ot, 1024),
                                               bf(b_absw + 1024 * ot, 1024),
                                               f32(f_thr, 1),
                                               bf(b_sgn + 1024 * ot, 1024),
                                               op0=AO.is_ge, op1=AO.mult)
            for ot in range(8):
                inst = nc.vector.tensor_scalar(bf(b_wtv + 1024 * ot, 1024),
                                               bf(b_wtv + 1024 * ot, 1024),
                                               f32(f_alpha, 8)[:, ot:ot + 1],
                                               None, op0=AO.mult)
            sv.inc(inst)
            scratch_free = sv.n
            items = []
            for ot in range(8):
                for kc in range(KC8):
                    items.append((bf(wtT_off + 1024 * kc, 1024)[:, 128 * ot:128 * ot + 128],
                                  bf(b_wtv + 1024 * ot, 1024)[:, 128 * kc:128 * kc + 128]))
            dma_transpose_batch(items, sv, scratch_free)
            return scratch_free

        sem_me = Sem("me")
        W(nc.scalar, t_cv)
        ss.inc(nc.scalar.activation(f32(f_silu, 8), f32(f_silu, 8), AF.Silu))
        nc.vector.wait_ge(ss.h, ss.n)
        sv.inc(nc.vector.tensor_copy(bf(sm_silub, 16)[:, 0:8], f32(f_silu, 8)))
        silu_done = sv.n
        psmap = [ps_a[0:1, 0:512], ps_a[0:1, 512:1024], ps_b[0:1, 0:512],
                 ps_b[0:1, 512:1024], ps_y[0:1, 0:512], ps_y[0:1, 512:1024]]
        wm_toks = chg.plan(16)
        mm_done = {}
        mod_row_tok = None
        nc.tensor.wait_ge(sv.h, silu_done)
        for pph in range(2):
            for kc in range(KC8):
                idx = pph * 8 + kc
                W(nc.tensor, wm_toks[idx])
                if pph == 1 and kc == 0:
                    nc.tensor.wait_ge(sem_me.h, 1)
                for sl in range(6):
                    inst = nc.tensor.matmul(
                        psmap[sl], bf(sm_silub, 16)[:, kc:kc + 1],
                        bf(b_wmod + 3072 * (kc % 2), 3072)[:, 512 * sl:512 * sl + 512],
                        start=(kc == 0), stop=(kc == 7))
                st_.inc(inst)
                mm_done[idx] = st_.n
        for idx in range(16):
            pph, kc = idx // 8, idx % 8
            if idx >= 2:
                nc.gpsimd.wait_ge(st_.h, mm_done[idx - 2])
            tok = chg.issue(nc.gpsimd,
                            out=bf(b_wmod + 3072 * (kc % 2), 3072),
                            in_=wmod_in[128 * kc:128 * kc + 128,
                                        3072 * pph:3072 * (pph + 1)])
            assert tok == wm_toks[idx], (tok, wm_toks[idx])

        def evac_mod(pph, handshake):
            global_tok = None
            nc.vector.wait_ge(st_.h, mm_done[pph * 8 + 7])
            for sl in range(6):
                tok = getattr(evac_mod, "row_tok", None)
                if tok is not None:
                    W(nc.vector, tok)
                inst = nc.vector.tensor_copy(f32(f_row, 512, p1=1), psmap[sl])
                if handshake and sl == 5:
                    sem_me.inc(inst)
                else:
                    sv.inc(inst)
                nc.sync.wait_ge(sv.h, sv.n)
                if handshake and sl == 5:
                    nc.sync.wait_ge(sem_me.h, 1)
                evac_mod.row_tok = cha.issue(
                    nc.sync,
                    out=d_bn[1024 + 3072 * pph + 512 * sl:
                             1024 + 3072 * pph + 512 * (sl + 1)][None, :],
                    in_=f32(f_row, 512, p1=1))

        g1 = ternary(wq_in, b_wtT, sv.n)
        evac_mod(0, True)
        absw_free_sv = ternary(wo_in, b_woT, g1)
        evac_mod(1, False)

        nc.vector.wait_ge(sv.h, absw_free_sv)
        nc.sync.wait_ge(sv.h, absw_free_sv)
        W(nc.sync, t_u)
        tU = cha.issue_t(nc.sync, bf(b_abt + 4096, 128), bf(sm_pad, 128))
        W(nc.vector, tU)
        ut = bf(b_abt + 4096, 128, p0=0, p1=32)
        abT = bf(b_abt, 128, p0=0, p1=32)
        nc.vector.tensor_copy(abT[:, 0:64:2], ut[:, 0:32])
        nc.vector.tensor_copy(abT[:, 1:64:2], ut[:, 32:64])
        nc.vector.tensor_copy(abT[:, 64:128:2], ut[:, 32:64])
        nc.vector.tensor_scalar(abT[:, 65:128:2], ut[:, 0:32], -1.0, None, op0=AO.mult)
        nc.vector.memset(bf(b_abt, 128, p0=32, p1=64), 0.0)
        sv.inc(nc.vector.memset(bf(b_abt, 128, p0=64, p1=128), 0.0))
        nc.sync.wait_ge(sv.h, sv.n)
        tAB = cha.issue_t(nc.sync, bf(b_abt + 8192, 128), bf(b_abt, 128))
        W(nc.vector, tAB)
        nc.vector.tensor_scalar(bf(sm_ab, 64)[:, 0:32], bf(b_abt + 8192, 128)[:, 0:32],
                                0.125, None, op0=AO.mult)
        sv.inc(nc.vector.tensor_copy(bf(sm_ab, 64)[:, 32:64],
                                     bf(b_abt + 8192, 128)[:, 0:32]))
        ab_done = sv.n
        cha.wait_all(nc.sync)

        def mod_bcast(spec):
            for off, kind, j in spec:
                nc.sync.wait_ge(sv.h, sv.n)
                t5 = cha.issue(nc.sync, out=f32(ftr2, 1024),
                               in_=d_bn[1024 + 1024 * j:1024 + 1024 * (j + 1)][None, :]
                               .to_broadcast((128, 1024)))
                W(nc.vector, t5)
                if kind == "cp":
                    sv.inc(nc.vector.tensor_copy(bf(off, 1024), f32(ftr2, 1024)))
                elif kind == "ca":
                    sv.inc(nc.vector.tensor_scalar(bf(off, 1024), f32(ftr2, 1024),
                                                   1.0, None, op0=AO.add))
                else:
                    nc.vector.tensor_scalar(f32(ftr0, 1024), f32(ftr2, 1024), 0.0, 2.0,
                                            op0=AO.is_ge, op1=AO.mult)
                    nc.vector.tensor_scalar(f32(ftr0, 1024), f32(ftr0, 1024), -1.0,
                                            None, op0=AO.add)
                    nc.vector.tensor_scalar(f32(ftr1, 1024), f32(ftr2, 1024), -1.0,
                                            None, op0=AO.mult)
                    nc.vector.tensor_tensor(f32(ftr1, 1024), f32(ftr1, 1024),
                                            f32(ftr2, 1024), op=AO.max)
                    nc.vector.tensor_scalar(f32(ftr1, 1024), f32(ftr1, 1024), 0.25,
                                            None, op0=AO.max)
                    sv.inc(nc.vector.tensor_tensor(bf(off, 1024), f32(ftr0, 1024),
                                                   f32(ftr1, 1024), op=AO.mult))

        dbg_wtt = dbg("wtt", [128, 8 * 1024])
        dbg_wot = dbg("wot", [128, 8 * 1024])
        dbg_wtv = dbg("wtv2", [128, 8 * 1024])
        if dbg_wtt or dbg_wot or dbg_wtv:
            cha.wait_all(nc.gpsimd)
            chs.wait_all(nc.gpsimd)
            if "wtt" in dbg_t:
                dbg_store("wtt", dbg_t["wtt"], bf(b_wtT, 8192), sv)
            if "wot" in dbg_t:
                dbg_store("wot", dbg_t["wot"], bf(b_woT, 8192), sv)
            if "wtv2" in dbg_t:
                dbg_store("wtv2", dbg_t["wtv2"], bf(b_wtv, 8192), sv)
        mod_bcast([(b_modA + 1024, "cp", 0), (b_modA, "ca", 1), (b_modA + 2048, "g", 2)])
        if dbg("mod", [128, 6 * 1024]):
            for j, off in enumerate([b_modA + 1024, b_modA, b_modA + 2048]):
                dbg_store("mod", dbg_t["mod"][:, 1024 * j:1024 * (j + 1)],
                          bf(off, 1024), sv)
        barrier()
        if stop_after == 1:
            cho.wait_all(nc.gpsimd); chg.wait_all(nc.gpsimd)
            cha.wait_all(nc.gpsimd); chs.wait_all(nc.gpsimd)
            return nc, sorted(dbg_t)

        slot_v, slot_s = {}, {}
        p1n = 2 if p1_mode == "nt2" else NT
        for i in range(p1n):
            par = i % 2
            sb_ = 8 * par
            foff = ftr0 + 1024 * par
            xt = f32(foff, 1024)
            if i >= 2:
                nc.sync.wait_ge(sv.h, slot_v[i - 2])
                nc.sync.wait_ge(ss.h, slot_s[i - 2])
            tx = cha.issue(nc.sync, out=xt, in_=x_in[128 * i:128 * i + 128, :])
            W(nc.vector, tx)
            if i >= 2:
                nc.vector.wait_ge(ss.h, slot_s[i - 2])
            nc.vector.tensor_reduce(STAT(sb_ + 0), xt, AX, AO.add)
            nc.vector.drain()
            sv.inc(nc.vector.tensor_scalar(STAT(sb_ + 1), STAT(sb_ + 0), -1.0 / D,
                                           None, op0=AO.mult))
            nc.scalar.wait_ge(sv.h, sv.n)
            ss.inc(nc.scalar.activation(xt, xt, AF.Identity, bias=STAT(sb_ + 1)))
            ss.inc(nc.scalar.activation(bf(G1 + 1024 * i, 1024), xt, AF.Square,
                                        accum_out=STAT(sb_ + 2)))
            nc.vector.wait_ge(ss.h, ss.n)
            sv.inc(nc.vector.tensor_scalar(STAT(sb_ + 3), STAT(sb_ + 2), 1.0 / D, EPS,
                                           op0=AO.mult, op1=AO.add))
            nc.scalar.wait_ge(sv.h, sv.n)
            ss.inc(nc.scalar.activation(STAT(sb_ + 3), STAT(sb_ + 3), AF.Sqrt))
            nc.vector.wait_ge(ss.h, ss.n)
            sv.inc(nc.vector.reciprocal(STAT(sb_ + 4), STAT(sb_ + 3)))
            nc.scalar.wait_ge(sv.h, sv.n)
            ss.inc(nc.scalar.activation(xt, xt, AF.Identity, scale=STAT(sb_ + 4)))
            nc.vector.wait_ge(ss.h, ss.n)
            nc.vector.tensor_tensor(xt, xt, bf(b_modA, 1024), op=AO.mult)
            sv.inc(nc.vector.tensor_tensor(xt, xt, bf(b_modA + 1024, 1024), op=AO.add))
            nc.vector.tensor_reduce(STAT(sb_ + 5), xt, AX, AO.max,
                                    apply_absolute_value=True)
            nc.vector.drain()
            nc.vector.tensor_scalar(STAT(sb_ + 5), STAT(sb_ + 5), 1e-6, None, op0=AO.max)
            nc.vector.drain()
            nc.vector.reciprocal(STAT(sb_ + 6), STAT(sb_ + 5))
            nc.vector.drain()
            nc.vector.tensor_scalar(STAT(sb_ + 6), STAT(sb_ + 6), 7.0, None, op0=AO.mult)
            sv.inc(nc.vector.tensor_scalar(STAT(sb_ + 7), STAT(sb_ + 5), 1.0 / 7.0,
                                           None, op0=AO.mult))
            nc.scalar.wait_ge(sv.h, sv.n)
            ss.inc(nc.scalar.activation(xt, xt, AF.Identity, scale=STAT(sb_ + 6)))
            nc.vector.wait_ge(ss.h, ss.n)
            nc.vector.tensor_scalar(f32(ftr2, 1024), xt, MAGIC, None, op0=AO.add)
            nc.vector.tensor_scalar(xt, f32(ftr2, 1024), -MAGIC, None, op0=AO.add)
            sv.inc(nc.vector.tensor_copy(i32f(ftr2, 1024), i32f(foff, 1024)))
            slot_v[i] = sv.n
            nc.scalar.wait_ge(sv.h, sv.n)
            ss.inc(nc.scalar.activation(bf(G1 + 1024 * i, 1024), f32(ftr2, 1024),
                                        AF.Identity, scale=STAT(sb_ + 7)))
            slot_s[i] = ss.n
            nc.sync.wait_ge(ss.h, ss.n)
            for kc in range(KC8):
                src = bf(G1 + 1024 * i, 1024)[:, 128 * kc:128 * kc + 128]
                dst = bf(b_xqT + 2048 * kc, 2048)[:, 128 * i:128 * i + 128]
                cha.issue_t(nc.sync, dst, src)
        if dbg("p1rnd", [128, 1024]):
            dbg_store("p1rnd", dbg_t["p1rnd"], f32(ftr2, 1024), ss)
        if dbg("p1st", [128, 64]):
            dbg_store("p1st", dbg_t["p1st"], f32(f_stat, 64), ss)
        if dbg("hq", [N, D]):
            for i in range(NT):
                dbg_store("hq", dbg_t["hq"][128 * i:128 * i + 128, :],
                          bf(G1 + 1024 * i, 1024), ss)
        barrier()
        if stop_after == 2:
            cho.wait_all(nc.gpsimd); chg.wait_all(nc.gpsimd)
            cha.wait_all(nc.gpsimd); chs.wait_all(nc.gpsimd)
            return nc, sorted(dbg_t)

        t_qbrow = cha.issue(nc.sync, out=f32(f_qbrow, 1024),
                            in_=wqb_in[None, :].to_broadcast((128, 1024)))
        cha.wait_all(nc.vector)
        chs.wait_all(nc.vector)
        for i in range(NT):
            p65t = bf(a_p65 + 1040 * i, 1040).rearrange("p (c w) -> p c w", w=65)
            inst = nc.vector.memset(p65t[:, :, 64:65], 1.0)
        sv.inc(inst)
        ones65 = sv.n
        cha.wait_all(nc.tensor)
        chs.wait_all(nc.tensor)
        evq = []
        for ot in range(8):
            ps = ps_a if ot % 2 == 0 else ps_b
            for sl in range(4):
                k = ot * 4 + sl
                tgt = ps[:, 512 * (sl % 2):512 * (sl % 2) + 512]
                if k >= 2:
                    nc.tensor.wait_ge(sv.h, evq[k - 2])
                for kc in range(KC8):
                    inst = nc.tensor.matmul(
                        tgt, bf(b_wtT + 1024 * kc, 1024)[:, 128 * ot:128 * ot + 128],
                        bf(b_xqT + 2048 * kc, 2048)[:, 512 * sl:512 * sl + 512],
                        start=(kc == 0), stop=(kc == 7))
                st_.inc(inst)
                nc.vector.wait_ge(st_.h, st_.n)
                if k == 0:
                    W(nc.vector, t_qbc)
                sv.inc(nc.vector.tensor_scalar(
                    bf(b_projT + 2048 * ot, 2048)[:, 512 * sl:512 * sl + 512],
                    tgt, f32(f_qbc, 8)[:, ot:ot + 1], None, op0=AO.add))
                evq.append(sv.n)
        W(nc.tensor, t_qbrow)
        ev2 = []
        for i in range(NT):
            for s2 in range(2):
                kk = 2 * i + s2
                ps = ps_a if kk % 2 == 0 else ps_b
                tgt = ps[:, 0:512]
                if kk >= 2:
                    nc.tensor.wait_ge(sv.h, ev2[kk - 2])
                for kc in range(KC8):
                    inst = nc.tensor.matmul(
                        tgt, bf(b_xqT + 2048 * kc, 2048)[:, 128 * i:128 * i + 128],
                        bf(b_wtT + 1024 * kc, 1024)[:, 512 * s2:512 * s2 + 512],
                        start=(kc == 0), stop=(kc == 7))
                st_.inc(inst)
                nc.vector.wait_ge(st_.h, st_.n)
                if kk == 0:
                    nc.vector.wait_ge(sv.h, ones65)
                    W(nc.vector, t_qbrow)
                p65t = bf(a_p65 + 1040 * i, 1040).rearrange("p (c w) -> p c w", w=65)
                sv.inc(nc.vector.tensor_tensor(
                    p65t[:, 8 * s2:8 * s2 + 8, 0:64],
                    tgt.rearrange("p (c w) -> p c w", w=64),
                    f32(f_qbrow, 1024)[:, 512 * s2:512 * s2 + 512]
                    .rearrange("p (c w) -> p c w", w=64),
                    op=AO.add))
                ev2.append(sv.n)
        if dbg("projT", [128, 8 * 2048]):
            dbg_store("projT", dbg_t["projT"], bf(b_projT, 16384), sv)
        if dbg("p65", [128, 16 * 1040]):
            dbg_store("p65", dbg_t["p65"], bf(a_p65, 16640), sv)
        barrier()
        if stop_after == 3:
            cho.wait_all(nc.gpsimd); chg.wait_all(nc.gpsimd)
            cha.wait_all(nc.gpsimd); chs.wait_all(nc.gpsimd)
            return nc, sorted(dbg_t)

        nc.vector.memset(bf(b_kla, 2048, p0=32, p1=33), 1.0)
        sv.inc(nc.vector.memset(f32(f_rl, 16), 1.0))
        yev_prev = sv.n
        scores_done_prev = 0
        pv_prev = {}
        exp_prev = {}
        rows = lambda r: f32(f_rows, 2048, p0=32 * r, p1=32 * r + 1)
        nheads = int(p1_mode[1:]) if (p1_mode or "").startswith("h") else \
            (1 if (p1_mode or "").startswith("s") else H)
        for h in range(nheads):
            hb = 64 * (h % 2)
            projt = bf(b_projT + 2048 * (h // 2), 2048)
            nc.vector.wait_ge(st_.h, scores_done_prev)
            nc.vector.tensor_tensor(bf(b_e12q, 2048, p0=0, p1=64),
                                    projt[hb:hb + 64, :],
                                    bf(b_tab, 2048)[hb:hb + 64, :], op=AO.mult)
            sv.inc(nc.vector.tensor_tensor(bf(b_e12q, 2048, p0=64, p1=128),
                                           projt[hb:hb + 64, :],
                                           bf(b_tab + 2048, 2048)[hb:hb + 64, :],
                                           op=AO.mult))
            for c in range(16):
                cb_ = 64 * (c % 2)
                src = bf(b_projT + 2048 * (c // 2), 2048)[cb_:cb_ + 64,
                                                          128 * h:128 * h + 128]
                nc.vector.tensor_tensor(
                    bf(b_e12k, 2048, p0=0, p1=64)[:, 128 * c:128 * c + 128], src,
                    bf(b_tab + 4096, 2048)[cb_:cb_ + 64, 128 * c:128 * c + 128],
                    op=AO.mult)
                inst = nc.vector.tensor_tensor(
                    bf(b_e12k, 2048, p0=64, p1=128)[:, 128 * c:128 * c + 128], src,
                    bf(b_tab + 6144, 2048)[cb_:cb_ + 64, 128 * c:128 * c + 128],
                    op=AO.mult)
            sv.inc(inst)
            e12_done = sv.n
            if p1_mode == "s1":
                continue
            nc.tensor.wait_ge(sv.h, e12_done)
            if h == 0:
                nc.tensor.wait_ge(sv.h, ab_done)
            if h > 0:
                nc.tensor.wait_ge(ss.h, exp_prev[15])
                nc.tensor.wait_ge(sv.h, yev_prev)
            for which, aboff, e12off, laoff in ((0, 0, b_e12q, b_qla),
                                                (1, 32, b_e12k, b_kla)):
                if which == 1:
                    nc.tensor.wait_ge(sv.h, sv.n)
                    nc.tensor.wait_ge(ss.h, ss.n)
                for sl in range(4):
                    ps = (ps_a, ps_a, ps_b, ps_b)[sl]
                    tgt = ps[0:32, 512 * (sl % 2):512 * (sl % 2) + 512]
                    st_.inc(nc.tensor.matmul(tgt, bf(sm_ab, 64)[:, aboff:aboff + 32],
                                             bf(e12off, 2048)[:, 512 * sl:512 * sl + 512],
                                             start=True, stop=True))
                    if p1_mode == "s2a":
                        continue
                    nc.vector.wait_ge(st_.h, st_.n)
                    sv.inc(nc.vector.tensor_copy(
                        bf(laoff, 2048, p0=0, p1=32)[:, 512 * sl:512 * sl + 512], tgt))
                    if p1_mode == "s2b":
                        continue
                    nc.scalar.wait_ge(sv.h, sv.n)
                    ss.inc(nc.scalar.activation(
                        bf(b_e12q, 2048, p0=0, p1=32)[:, 512 * sl:512 * sl + 512],
                        tgt, AF.Square))
                nc.tensor.wait_ge(ss.h, ss.n)
                if p1_mode == "s2":
                    continue
                bev = []
                for sl in range(4):
                    tgt = ps_y[0:1, 512 * (sl % 2):512 * (sl % 2) + 512]
                    if sl >= 2:
                        nc.tensor.wait_ge(sv.h, bev[sl - 2])
                    st_.inc(nc.tensor.matmul(tgt, bf(sm_ones, 64)[0:32, 0:1],
                                             bf(b_e12q, 2048, p0=0, p1=32)[:, 512 * sl:512 * sl + 512],
                                             start=True, stop=True))
                    nc.vector.wait_ge(st_.h, st_.n)
                    sv.inc(nc.vector.tensor_copy(
                        rows(which)[:, 512 * sl:512 * sl + 512], tgt))
                    bev.append(sv.n)
            if p1_mode in ("s2", "s2a", "s2b", "s3"):
                continue
            sv.inc(nc.vector.tensor_reduce(STAT(16)[0:1, :], rows(1), AX, AO.max))
            nc.vector.drain()
            sv.inc(nc.vector.tensor_scalar(rows(2), rows(0), STAT(16)[0:1, :],
                                           None, op0=AO.mult))
            nc.scalar.wait_ge(sv.h, sv.n)
            ss.inc(nc.scalar.activation(rows(3), rows(2), AF.Sqrt))
            nc.vector.wait_ge(ss.h, ss.n)
            sv.inc(nc.vector.tensor_scalar(bf(b_qla, 2048, p0=32, p1=33),
                                           rows(3), -1.0, None, op0=AO.mult))
            qla_done = sv.n
            if h == 0 and dbg("ql0", [33, N]):
                dbg_store("ql0", dbg_t["ql0"][0:33, :], bf(b_qla, 2048, p0=0, p1=33), sv)
            if h == 0 and dbg("kl0", [33, N]):
                dbg_store("kl0", dbg_t["kl0"][0:33, :], bf(b_kla, 2048, p0=0, p1=33), sv)
            nc.tensor.wait_ge(sv.h, qla_done)
            if p1_mode == "s4":
                continue
            exp_done = {}
            pv_done = {}
            for c in range(16):
                pt = bf(b_pt + 2048 * (c % 2), 2048)
                if c >= 1:
                    nc.tensor.wait_ge(ss.h, exp_done[c - 1])
                for half, psx in ((0, ps_a), (1, ps_b)):
                    for q2 in range(2):
                        st_.inc(nc.tensor.matmul(
                            psx[:, 512 * q2:512 * q2 + 512],
                            bf(b_kla, 2048, p0=0, p1=33)[:, 128 * c:128 * c + 128],
                            bf(b_qla, 2048, p0=0, p1=33)[:, 1024 * half + 512 * q2:
                                                         1024 * half + 512 * q2 + 512],
                            start=True, stop=True))
                    nc.scalar.wait_ge(st_.h, st_.n)
                    if c >= 2:
                        nc.scalar.wait_ge(st_.h, pv_done[c - 2])
                    elif h > 0:
                        nc.scalar.wait_ge(st_.h, pv_prev[c + 14])
                    ss.inc(nc.scalar.activation(pt[:, 1024 * half:1024 * half + 1024],
                                                psx[:, 0:1024], AF.Exp))
                exp_done[c] = ss.n
                if p1_mode == "s5":
                    pv_done[c] = st_.n
                    if c == 3:
                        scores_done = st_.n
                    continue
                nc.tensor.wait_ge(ss.h, exp_done[c])
                for j in range(16):
                    yc = 512 * (j // 7) + 65 * (j % 7) if j < 14 else 1024 + 65 * (j - 14)
                    inst = nc.tensor.matmul(
                        ps_y[:, yc:yc + 65], pt[:, 128 * j:128 * j + 128],
                        bf(a_p65 + 1040 * h, 1040)[:, 65 * c:65 * c + 65],
                        start=(c == 0 and j in (0, 7, 14)),
                        stop=(c == 15 and j in (6, 13, 15)))
                st_.inc(inst)
                pv_done[c] = st_.n
                if c == 3:
                    scores_done = st_.n
            scores_done_prev = scores_done
            exp_prev, pv_prev = exp_done, pv_done
            if p1_mode == "s5":
                continue
            nc.vector.wait_ge(st_.h, pv_done[15])
            for j in range(16):
                yc = 512 * (j // 7) + 65 * (j % 7) if j < 14 else 1024 + 65 * (j - 14)
                nc.vector.reciprocal(f32(f_rl, 16)[:, j:j + 1], ps_y[:, yc + 64:yc + 65])
            nc.vector.drain()
            for j in range(16):
                yc = 512 * (j // 7) + 65 * (j % 7) if j < 14 else 1024 + 65 * (j - 14)
                inst = nc.vector.tensor_scalar(
                    ytile(j)[:, 64 * h:64 * h + 64],
                    ps_y[:, yc:yc + 64], f32(f_rl, 16)[:, j:j + 1], None, op0=AO.mult)
            sv.inc(inst)
            yev_prev = sv.n
        if dbg("yattn", [N, D]):
            for i in range(NT):
                dbg_store("yattn", dbg_t["yattn"][128 * i:128 * i + 128, :], ytile(i), sv)
        barrier()
        if stop_after == 4:
            cho.wait_all(nc.gpsimd); chg.wait_all(nc.gpsimd)
            cha.wait_all(nc.gpsimd); chs.wait_all(nc.gpsimd)
            return nc, sorted(dbg_t)

        yqT_toks = {}
        ev3 = []
        xsp_toks = {}

        def p4_quant(i):
            par = i % 2
            sb_ = 8 * par
            yt = ytile(i)
            foff = ftr0 + 1024 * par
            zt = f32(foff, 1024)
            if i >= 2:
                nc.vector.wait_ge(ss.h, slot_s[i - 2])
            nc.vector.tensor_reduce(STAT(sb_ + 5), yt, AX, AO.max,
                                    apply_absolute_value=True)
            nc.vector.drain()
            nc.vector.tensor_scalar(STAT(sb_ + 5), STAT(sb_ + 5), 1e-6, None, op0=AO.max)
            nc.vector.drain()
            nc.vector.reciprocal(STAT(sb_ + 6), STAT(sb_ + 5))
            nc.vector.drain()
            nc.vector.tensor_scalar(STAT(sb_ + 6), STAT(sb_ + 6), 7.0, None, op0=AO.mult)
            sv.inc(nc.vector.tensor_scalar(STAT(sb_ + 7), STAT(sb_ + 5), 1.0 / 7.0,
                                           None, op0=AO.mult))
            nc.scalar.wait_ge(sv.h, sv.n)
            ss.inc(nc.scalar.activation(zt, yt, AF.Identity, scale=STAT(sb_ + 6)))
            nc.vector.wait_ge(ss.h, ss.n)
            nc.vector.tensor_scalar(f32(ftr2, 1024), zt, MAGIC, None, op0=AO.add)
            nc.vector.tensor_scalar(zt, f32(ftr2, 1024), -MAGIC, None, op0=AO.add)
            sv.inc(nc.vector.tensor_copy(i32f(ftr2, 1024), i32f(foff, 1024)))
            nc.scalar.wait_ge(sv.h, sv.n)
            ss.inc(nc.scalar.activation(yt, f32(ftr2, 1024), AF.Identity,
                                        scale=STAT(sb_ + 7)))
            slot_s[i] = ss.n
            nc.sync.wait_ge(ss.h, ss.n)
            toks = []
            for kc in range(KC8):
                src = yt[:, 128 * kc:128 * kc + 128]
                dst = bf(G1 + 2048 * kc, 2048)[:, 128 * i:128 * i + 128]
                toks.append(cha.issue_t(nc.sync, dst, src))
            yqT_toks[i] = toks

        def p4_wo(i):
            par = i % 2
            xt = f32(3072 + 1024 * par, 1024)
            x2t = f32(5120, 1024)
            if i >= 2:
                nc.sync.wait_ge(sv.h, ev3[2 * i - 3])
            tx = cha.issue(nc.sync, out=xt, in_=x_in[128 * i:128 * i + 128, :])
            for t in yqT_toks[i]:
                W(nc.tensor, t)
            for s2 in range(2):
                kk = 2 * i + s2
                ps = ps_a if kk % 2 == 0 else ps_b
                tgt = ps[:, 0:512]
                if kk >= 2:
                    nc.tensor.wait_ge(sv.h, ev3[kk - 2])
                for kc in range(KC8):
                    inst = nc.tensor.matmul(
                        tgt, bf(G1 + 2048 * kc, 2048)[:, 128 * i:128 * i + 128],
                        bf(b_woT + 1024 * kc, 1024)[:, 512 * s2:512 * s2 + 512],
                        start=(kc == 0), stop=(kc == 7))
                st_.inc(inst)
                nc.vector.wait_ge(st_.h, st_.n)
                if s2 == 0:
                    W(nc.vector, tx)
                    if i >= 1:
                        W(nc.vector, xsp_toks[i - 1])
                nc.vector.tensor_tensor(tgt, tgt,
                                        bf(b_modA + 2048, 1024)[:, 512 * s2:512 * s2 + 512],
                                        op=AO.mult)
                sv.inc(nc.vector.tensor_tensor(x2t[:, 512 * s2:512 * s2 + 512], tgt,
                                               xt[:, 512 * s2:512 * s2 + 512], op=AO.add))
                ev3.append(sv.n)
            nc.sync.wait_ge(sv.h, sv.n)
            xsp_toks[i] = cho.issue(nc.sync, out=d_x2[128 * i:128 * i + 128, :], in_=x2t)

        for i in range(NT + 1):
            if i < NT:
                p4_quant(i)
            if i >= 1:
                p4_wo(i - 1)
        if dbg("yq", [N, D]):
            for i in range(NT):
                dbg_store("yq", dbg_t["yq"][128 * i:128 * i + 128, :], ytile(i), ss)
        if dbg("x2", [N, D]):
            for i in range(NT):
                nc.gpsimd.wait_ge(xsp_toks[i][0], xsp_toks[i][1])
                tokd = chg.issue(nc.gpsimd, out=dbg_t["x2"][128 * i:128 * i + 128, :],
                                 in_=d_x2[128 * i:128 * i + 128, :])
                W(nc.gpsimd, tokd)
        barrier()
        if stop_after == 5:
            cho.wait_all(nc.gpsimd); chg.wait_all(nc.gpsimd)
            cha.wait_all(nc.gpsimd); chs.wait_all(nc.gpsimd)
            return nc, sorted(dbg_t)

        mod_bcast([(b_modB + 1024, "cp", 3), (b_modB, "ca", 4), (b_gb, "g", 5)])
        t_gn = cha.issue(nc.sync, out=f32(f_gn, 1024),
                         in_=ng_in[None, :].to_broadcast((128, 1024)))
        t_bn2 = cha.issue(nc.sync, out=f32(f_bn, 1024),
                          in_=nb_in[None, :].to_broadcast((128, 1024)))
        modb_done = sv.n
        if dbg("mod", [128, 6 * 1024]):
            for jj, off in enumerate([b_modB + 1024, b_modB, b_gb]):
                dbg_store("mod", dbg_t["mod"][:, 1024 * (jj + 3):1024 * (jj + 4)],
                          bf(off, 1024), sv)
        def ln_stages(i, mul_off, add_off, affine_f32, tx):
            par = i % 2
            sb_ = 8 * par
            xt = f32(ftr0 + 1024 * par, 1024)
            st = []

            def s0():
                if tx is not None:
                    W(nc.vector, tx)
                nc.vector.tensor_reduce(STAT(sb_ + 0), xt, AX, AO.add)
                nc.vector.drain()
                sv.inc(nc.vector.tensor_scalar(STAT(sb_ + 1), STAT(sb_ + 0),
                                               -1.0 / D, None, op0=AO.mult))

            def s1():
                nc.scalar.wait_ge(sv.h, sv.n)
                ss.inc(nc.scalar.activation(xt, xt, AF.Identity, bias=STAT(sb_ + 1)))
                ss.inc(nc.scalar.activation(bf(b_j6, 1024), xt, AF.Square,
                                            accum_out=STAT(sb_ + 2)))

            def s2():
                nc.vector.wait_ge(ss.h, ss.n)
                sv.inc(nc.vector.tensor_scalar(STAT(sb_ + 3), STAT(sb_ + 2), 1.0 / D,
                                               EPS, op0=AO.mult, op1=AO.add))

            def s3():
                nc.scalar.wait_ge(sv.h, sv.n)
                ss.inc(nc.scalar.activation(STAT(sb_ + 3), STAT(sb_ + 3), AF.Sqrt))

            def s4():
                nc.vector.wait_ge(ss.h, ss.n)
                sv.inc(nc.vector.reciprocal(STAT(sb_ + 4), STAT(sb_ + 3)))

            def s5():
                nc.scalar.wait_ge(sv.h, sv.n)
                ss.inc(nc.scalar.activation(xt, xt, AF.Identity, scale=STAT(sb_ + 4)))

            def s6():
                nc.vector.wait_ge(ss.h, ss.n)
                if affine_f32:
                    W(nc.vector, t_gn)
                    W(nc.vector, t_bn2)
                    nc.vector.tensor_tensor(xt, xt, f32(mul_off, 1024), op=AO.mult)
                    sv.inc(nc.vector.tensor_tensor(bf(G1 + 1024 * i, 1024), xt,
                                                   f32(add_off, 1024), op=AO.add))
                else:
                    nc.vector.tensor_tensor(xt, xt, bf(mul_off, 1024), op=AO.mult)
                    sv.inc(nc.vector.tensor_tensor(xt, xt, bf(add_off, 1024),
                                                   op=AO.add))
            return [s0, s1, s2, s3, s4, s5, s6]

        def mk_fin(i):
            def fin():
                slot_v[i] = sv.n
                nc.sync.wait_ge(sv.h, sv.n)
                for kc in range(KC8):
                    src = bf(G1 + 1024 * i, 1024)[:, 128 * kc:128 * kc + 128]
                    dst = bf(b_xnT + 2048 * kc, 2048)[:, 128 * i:128 * i + 128]
                    cha.issue_t(nc.sync, dst, src)
            return fin

        def run_zip(a, b):
            for k in range(max(len(a), len(b))):
                if k < len(a):
                    a[k]()
                if k < len(b):
                    b[k]()

        pend = []
        for i in range(NT):
            par = i % 2
            if i >= 2:
                nc.sync.wait_ge(sv.h, slot_v[i - 2])
            else:
                nc.sync.wait_ge(sv.h, modb_done)
            W(nc.sync, xsp_toks[i])
            tx = cha.issue(nc.sync, out=f32(ftr0 + 1024 * par, 1024),
                           in_=d_x2[128 * i:128 * i + 128, :])
            run_zip(pend, ln_stages(i, b_modB, b_modB + 1024, False, tx))
            pend = ln_stages(i, f_gn, f_bn, True, None) + [mk_fin(i)]
        run_zip(pend, [])
        if dbg("xn", [N, D]):
            for i in range(NT):
                dbg_store("xn", dbg_t["xn"][128 * i:128 * i + 128, :],
                          bf(G1 + 1024 * i, 1024), sv)
        barrier()
        if stop_after == 6:
            cho.wait_all(nc.gpsimd); chg.wait_all(nc.gpsimd)
            cha.wait_all(nc.gpsimd); chs.wait_all(nc.gpsimd)
            return nc, sorted(dbg_t)

        se_toks = [chg.issue(nc.gpsimd, out=bf(b_seN + 1024 * et, 1024),
                             in_=semb_in[128 * et:128 * et + 128, :]) for et in range(8)]
        t_sg = cha.issue(nc.sync, out=f32(f_gn, 1024),
                         in_=sg_in[None, :].to_broadcast((128, 1024)))
        t_sb = cha.issue(nc.sync, out=f32(f_bn, 1024),
                         in_=sbv_in[None, :].to_broadcast((128, 1024)))
        t_sc = cha.issue(nc.sync, out=f32(f_sccol, 1),
                         in_=sc_in[0:1][None, :].to_broadcast((128, 1)))
        W(nc.vector, t_sg); W(nc.vector, t_sb); W(nc.vector, t_sc)
        nc.vector.tensor_scalar(f32(f_gn, 1024), f32(f_gn, 1024),
                                f32(f_sccol, 1), None, op0=AO.mult)
        sv.inc(nc.vector.tensor_scalar(f32(f_bn, 1024), f32(f_bn, 1024),
                                       f32(f_sccol, 1), None, op0=AO.mult))
        for et in range(8):
            par = et % 2
            sb_ = 8 * par
            set_ = bf(b_seN + 1024 * et, 1024)
            zt = f32(ftr0 + 1024 * par, 1024)
            W(nc.vector, se_toks[et])
            if et >= 2:
                nc.vector.wait_ge(ss.h, slot_s[et - 2])
            nc.vector.tensor_reduce(STAT(sb_ + 0), set_, AX, AO.add)
            nc.vector.drain()
            sv.inc(nc.vector.tensor_scalar(STAT(sb_ + 1), STAT(sb_ + 0), -1.0 / D,
                                           None, op0=AO.mult))
            nc.scalar.wait_ge(sv.h, sv.n)
            ss.inc(nc.scalar.activation(zt, set_, AF.Identity, bias=STAT(sb_ + 1)))
            ss.inc(nc.scalar.activation(bf(b_j6, 1024), zt, AF.Square,
                                        accum_out=STAT(sb_ + 2)))
            nc.vector.wait_ge(ss.h, ss.n)
            sv.inc(nc.vector.tensor_scalar(STAT(sb_ + 3), STAT(sb_ + 2), 1.0 / D, EPS,
                                           op0=AO.mult, op1=AO.add))
            nc.scalar.wait_ge(sv.h, sv.n)
            ss.inc(nc.scalar.activation(STAT(sb_ + 3), STAT(sb_ + 3), AF.Sqrt))
            nc.vector.wait_ge(ss.h, ss.n)
            sv.inc(nc.vector.reciprocal(STAT(sb_ + 4), STAT(sb_ + 3)))
            nc.scalar.wait_ge(sv.h, sv.n)
            ss.inc(nc.scalar.activation(zt, zt, AF.Identity, scale=STAT(sb_ + 4)))
            slot_s[et] = ss.n
            nc.vector.wait_ge(ss.h, ss.n)
            nc.vector.tensor_tensor(zt, zt, f32(f_gn, 1024), op=AO.mult)
            sv.inc(nc.vector.tensor_tensor(set_, zt, f32(f_bn, 1024), op=AO.add))
        items = []
        for et in range(8):
            for kc in range(KC8):
                items.append((bf(b_seT + 1024 * kc, 1024)[:, 128 * et:128 * et + 128],
                              bf(b_seN + 1024 * et, 1024)[:, 128 * kc:128 * kc + 128]))
        dma_transpose_batch(items, sv, sv.n)
        cha.wait_all(nc.tensor)
        chs.wait_all(nc.tensor)
        ev4 = []
        for et in range(8):
            for sl in range(4):
                kk = 4 * et + sl
                ps = ps_a if kk % 2 == 0 else ps_b
                tgt = ps[:, 0:512]
                if kk >= 2:
                    nc.tensor.wait_ge(sv.h, ev4[kk - 2])
                for kc in range(KC8):
                    inst = nc.tensor.matmul(
                        tgt, bf(b_seT + 1024 * kc, 1024)[:, 128 * et:128 * et + 128],
                        bf(b_xnT + 2048 * kc, 2048)[:, 512 * sl:512 * sl + 512],
                        start=(kc == 0), stop=(kc == 7))
                st_.inc(inst)
                nc.vector.wait_ge(st_.h, st_.n)
                sv.inc(nc.vector.tensor_copy(
                    bf(b_lgT + 2048 * et, 2048)[:, 512 * sl:512 * sl + 512], tgt))
                ev4.append(sv.n)
        if dbg("logitsT", [128, 8 * 2048]):
            dbg_store("logitsT", dbg_t["logitsT"], bf(b_lgT, 16384), sv)
        barrier()
        if stop_after == 7:
            cho.wait_all(nc.gpsimd); chg.wait_all(nc.gpsimd)
            cha.wait_all(nc.gpsimd); chs.wait_all(nc.gpsimd)
            return nc, sorted(dbg_t)

        nc.vector.tensor_tensor(f32(f_mx, 2048), bf(b_lgT, 2048),
                                bf(b_lgT + 2048, 2048), op=AO.max)
        for et in range(2, 8):
            nc.vector.tensor_tensor(f32(f_mx, 2048), f32(f_mx, 2048),
                                    bf(b_lgT + 2048 * et, 2048), op=AO.max)
        sv.inc(nc.vector.tensor_copy(bf(b_mxb, 2048), f32(f_mx, 2048)))
        red_after = {}
        mxv0 = sv.n
        for t in range(16):
            if t >= 2:
                nc.sync.wait_ge(sv.h, red_after[t - 2])
            if t == 0:
                nc.sync.wait_ge(sv.h, mxv0)
            tokt = cha.issue_t(nc.sync, bf(sm_mxt + 128 * (t % 2), 128),
                               bf(b_mxb, 2048)[:, 128 * t:128 * t + 128])
            W(nc.vector, tokt)
            sv.inc(nc.vector.tensor_reduce(f32(f_rl, 16)[:, t:t + 1],
                                           bf(sm_mxt + 128 * (t % 2), 128), AX, AO.max))
            red_after[t] = sv.n
        nc.sync.wait_ge(sv.h, sv.n)
        t6 = cha.issue(nc.sync, out=d_bn[0:2048].rearrange("(t p) -> p t", p=128),
                       in_=f32(f_rl, 16), allow_slow_non_contiguous=True)
        W(nc.sync, t6)
        t7 = cha.issue(nc.sync, out=f32(f_mx, 2048),
                       in_=d_bn[0:2048][None, :].to_broadcast((128, 2048)))
        W(nc.vector, t7)
        for et in range(8):
            sv.inc(nc.vector.tensor_tensor(f32(f_sub, 2048), bf(b_lgT + 2048 * et, 2048),
                                           f32(f_mx, 2048), op=AO.subtract))
            nc.scalar.wait_ge(sv.h, sv.n)
            ss.inc(nc.scalar.activation(bf(b_cu + 2048 * et, 2048), f32(f_sub, 2048),
                                        AF.Exp))
            if et < 7:
                nc.vector.wait_ge(ss.h, ss.n)
        nc.tensor.wait_ge(ss.h, ss.n)
        lcv = []
        for sl in range(4):
            if sl >= 1:
                nc.tensor.wait_ge(sv.h, lcv[sl - 1])
            for et in range(8):
                inst = nc.tensor.matmul(ps_s[0:1, 0:512], bf(sm_ones, 64)[:, 0:1],
                                        bf(b_cu + 2048 * et, 2048)[:, 512 * sl:512 * sl + 512],
                                        start=(et == 0), stop=(et == 7))
            st_.inc(inst)
            nc.vector.wait_ge(st_.h, st_.n)
            sv.inc(nc.vector.tensor_copy(
                f32(f_lcrow, 2048, p1=1)[:, 512 * sl:512 * sl + 512],
                ps_s[0:1, 0:512]))
            lcv.append(sv.n)
        sv.inc(nc.vector.reciprocal(f32(f_lcrow, 2048, p1=1), f32(f_lcrow, 2048, p1=1)))
        nc.sync.wait_ge(sv.h, sv.n)
        t8 = cha.issue(nc.sync, out=d_bn[2048:4096][None, :],
                       in_=f32(f_lcrow, 2048, p1=1))
        W(nc.sync, t8)
        t_rlc = cha.issue(nc.sync, out=f32(f_rlc, 16),
                          in_=d_bn[2048:4096].rearrange("(t p) -> p t", p=128),
                          allow_slow_non_contiguous=True)
        if dbg("combUT", [128, 8 * 2048]):
            dbg_store("combUT", dbg_t["combUT"], bf(b_cu, 16384), ss)
        if dbg("rlc", [128, 16]):
            nc.gpsimd.wait_ge(t_rlc[0], t_rlc[1])
            tokd = chg.issue(nc.gpsimd, out=dbg_t["rlc"], in_=f32(f_rlc, 16))
            W(nc.gpsimd, tokd)
        barrier()
        if stop_after == 8:
            cho.wait_all(nc.gpsimd); chg.wait_all(nc.gpsimd)
            cha.wait_all(nc.gpsimd); chs.wait_all(nc.gpsimd)
            return nc, sorted(dbg_t)

        for et in range(8):
            lg = bf(b_lgT + 2048 * et, 2048)
            nc.vector.tensor_reduce(STAT(0), lg, AX, AO.max)
            nc.vector.drain()
            sv.inc(nc.vector.tensor_scalar(STAT(1), STAT(0), -1.0, None, op0=AO.mult))
            nc.scalar.wait_ge(sv.h, sv.n)
            ss.inc(nc.scalar.activation(lg, lg, AF.Exp, bias=STAT(1),
                                        accum_out=STAT(2)))
            nc.vector.wait_ge(ss.h, ss.n)
            sv.inc(nc.vector.reciprocal(STAT(3), STAT(2)))
            nc.scalar.wait_ge(sv.h, sv.n)
            ss.inc(nc.scalar.activation(lg, lg, AF.Identity, scale=STAT(3)))
            nc.sync.wait_ge(ss.h, ss.n)
            for ntx in range(NT):
                src = lg[:, 128 * ntx:128 * ntx + 128]
                dst = bf(b_disp + 1024 * ntx, 1024)[:, 128 * et:128 * et + 128]
                cha.issue_t(nc.sync, dst, src)
        if dbg("dispT", [128, 8 * 2048]):
            dbg_store("dispT", dbg_t["dispT"], bf(b_lgT, 16384), ss)
        barrier()
        if stop_after == 9:
            cho.wait_all(nc.gpsimd); chg.wait_all(nc.gpsimd)
            cha.wait_all(nc.gpsimd); chs.wait_all(nc.gpsimd)
            return nc, sorted(dbg_t)

        cha.wait_all(nc.tensor)
        chs.wait_all(nc.tensor)
        ev5 = []
        for dt in range(8):
            for es2 in range(2):
                kk = 2 * dt + es2
                ps = ps_a if kk % 2 == 0 else ps_b
                tgt = ps[:, 0:512]
                if kk >= 2:
                    nc.tensor.wait_ge(sv.h, ev5[kk - 2])
                for ncx in range(NT):
                    inst = nc.tensor.matmul(
                        tgt, bf(G1 + 1024 * ncx, 1024)[:, 128 * dt:128 * dt + 128],
                        bf(b_disp + 1024 * ncx, 1024)[:, 512 * es2:512 * es2 + 512],
                        start=(ncx == 0), stop=(ncx == 15))
                st_.inc(inst)
                nc.vector.wait_ge(st_.h, st_.n)
                sv.inc(nc.vector.tensor_copy(
                    bf(b_slt + 1024 * dt, 1024)[:, 512 * es2:512 * es2 + 512], tgt))
                ev5.append(sv.n)
        if dbg("slotsT", [128, 8 * 1024]):
            dbg_store("slotsT", dbg_t["slotsT"], bf(b_slt, 8192), sv)
        barrier()
        if stop_after == 10:
            cho.wait_all(nc.gpsimd); chg.wait_all(nc.gpsimd)
            cha.wait_all(nc.gpsimd); chs.wait_all(nc.gpsimd)
            return nc, sorted(dbg_t)

        ntoks = E * (2 * KC8 + 16)
        wtoks = chg.plan(ntoks)
        wg_ld, wu_ld, wd_ld = {}, {}, {}
        ti = 0
        for e in range(E):
            for kc in range(KC8):
                wg_ld[(e, kc)] = wtoks[ti]; ti += 1
                wu_ld[(e, kc)] = wtoks[ti]; ti += 1
            for kc in range(16):
                wd_ld[(e, kc)] = wtoks[ti]; ti += 1
        gumm, dnmm = {}, {}
        gelu_done, gu_done, eo_done = {}, {}, {}
        fb_toks = {}
        for e in range(E):
            epar = e % 2
            if e >= 2:
                nc.sync.wait_ge(ss.h, gelu_done[e - 2])
                nc.sync.wait_ge(sv.h, eo_done[e - 2])
            ta = cha.issue(nc.sync, out=f32(f_fb + 32 * epar, 16),
                           in_=bg_in[e].rearrange("(t p) -> p t", p=128),
                           allow_slow_non_contiguous=True)
            tb = cha.issue(nc.sync, out=f32(f_fb + 64 + 32 * epar, 16),
                           in_=bu_in[e].rearrange("(t p) -> p t", p=128),
                           allow_slow_non_contiguous=True)
            tc = cha.issue(nc.sync, out=f32(f_bd + 1024 * epar, 1024, p1=64),
                           in_=bd_in[e][None, :].to_broadcast((64, 1024)))
            fb_toks[e] = (ta, tb, tc)
            if e > 0:
                nc.tensor.wait_ge(ss.h, gelu_done[e - 1])
                nc.tensor.wait_ge(sv.h, gu_done[e - 1])
            for kc in range(KC8):
                W(nc.tensor, wu_ld[(e, kc)])
                for ft in range(FT):
                    nc.tensor.matmul(ps_a[:, 64 * ft:64 * ft + 64],
                                     bf(b_wg + 2048 * (kc % 4), 2048)[:, 128 * ft:128 * ft + 128],
                                     bf(b_slt + 1024 * kc, 1024)[:, 64 * e:64 * e + 64],
                                     start=(kc == 0 and ft in (0, 8)),
                                     stop=(kc == 7 and ft in (7, 15)))
                for ft in range(FT):
                    inst = nc.tensor.matmul(ps_b[:, 64 * ft:64 * ft + 64],
                                            bf(b_wu + 2048 * (kc % 4), 2048)[:, 128 * ft:128 * ft + 128],
                                            bf(b_slt + 1024 * kc, 1024)[:, 64 * e:64 * e + 64],
                                            start=(kc == 0 and ft in (0, 8)),
                                            stop=(kc == 7 and ft in (7, 15)))
                st_.inc(inst)
                gumm[(e, kc)] = st_.n
            nc.scalar.wait_ge(st_.h, gumm[(e, KC8 - 1)])
            W(nc.scalar, fb_toks[e][0])
            if e > 0:
                nc.scalar.wait_ge(st_.h, dnmm[(e - 1, 15)])
            for ft in range(FT):
                inst = nc.scalar.activation(bf(b_gu, 1024)[:, 64 * ft:64 * ft + 64],
                                            ps_a[:, 64 * ft:64 * ft + 64], AF.Gelu,
                                            bias=f32(f_fb + 32 * epar, 16)[:, ft:ft + 1])
            ss.inc(inst)
            gelu_done[e] = ss.n
            nc.vector.wait_ge(ss.h, gelu_done[e])
            W(nc.vector, fb_toks[e][1])
            for ft in range(FT):
                inst = nc.vector.scalar_tensor_tensor(
                    bf(b_gu, 1024)[:, 64 * ft:64 * ft + 64],
                    ps_b[:, 64 * ft:64 * ft + 64],
                    f32(f_fb + 64 + 32 * epar, 16)[:, ft:ft + 1],
                    bf(b_gu, 1024)[:, 64 * ft:64 * ft + 64],
                    op0=AO.add, op1=AO.mult)
            sv.inc(inst)
            gu_done[e] = sv.n
            nc.tensor.wait_ge(sv.h, gu_done[e])
            if e > 0:
                nc.tensor.wait_ge(sv.h, eo_done[e - 1])
            for kc in range(16):
                W(nc.tensor, wd_ld[(e, kc)])
                for s2 in range(2):
                    inst = nc.tensor.matmul(
                        ps_y[0:64, 512 * s2:512 * s2 + 512],
                        bf(b_gu, 1024)[:, 64 * kc:64 * kc + 64],
                        bf(b_wd + 1024 * (kc % 4), 1024)[:, 512 * s2:512 * s2 + 512],
                        start=(kc == 0), stop=(kc == 15))
                st_.inc(inst)
                dnmm[(e, kc)] = st_.n
            nc.vector.wait_ge(st_.h, dnmm[(e, 15)])
            W(nc.vector, fb_toks[e][2])
            eb = 64 * (e % 2)
            for s2 in range(2):
                inst = nc.vector.tensor_tensor(
                    bf(b_eo + 1024 * (e // 2), 1024)[eb:eb + 64, 512 * s2:512 * s2 + 512],
                    ps_y[0:64, 512 * s2:512 * s2 + 512],
                    f32(f_bd + 1024 * epar, 1024, p1=64)[:, 512 * s2:512 * s2 + 512],
                    op=AO.add)
            sv.inc(inst)
            eo_done[e] = sv.n
        for e in range(E):
            for kc in range(KC8):
                if e > 0 or kc >= 4:
                    pe, pk = (e, kc - 4) if kc >= 4 else (e - 1, kc + 4)
                    nc.gpsimd.wait_ge(st_.h, gumm[(pe, pk)])
                tok = chg.issue(nc.gpsimd, out=bf(b_wg + 2048 * (kc % 4), 2048),
                                in_=wg_in[e, 128 * kc:128 * kc + 128, :])
                assert tok == wg_ld[(e, kc)]
                tok = chg.issue(nc.gpsimd, out=bf(b_wu + 2048 * (kc % 4), 2048),
                                in_=wu_in[e, 128 * kc:128 * kc + 128, :])
                assert tok == wu_ld[(e, kc)]
            for kc in range(16):
                if e > 0 or kc >= 4:
                    pe, pk = (e, kc - 4) if kc >= 4 else (e - 1, kc + 12)
                    nc.gpsimd.wait_ge(st_.h, dnmm[(pe, pk)])
                tok = chg.issue(nc.gpsimd, out=bf(b_wd + 1024 * (kc % 4), 1024),
                                in_=wd_in[e, 128 * kc:128 * kc + 128, :])
                assert tok == wd_ld[(e, kc)]
        if dbg("eoutb", [128, 8 * 1024]):
            dbg_store("eoutb", dbg_t["eoutb"], bf(b_eo, 8192), sv)
        barrier()
        if stop_after == 11:
            cho.wait_all(nc.gpsimd); chg.wait_all(nc.gpsimd)
            cha.wait_all(nc.gpsimd); chs.wait_all(nc.gpsimd)
            return nc, sorted(dbg_t)

        W(nc.vector, t_rlc)
        ev6 = []
        y_toks = {}
        for i in range(NT):
            par = i % 2
            xt = f32(ftr0 + 1024 * par, 1024)
            yt = f32(ftr2, 1024)
            if i >= 2:
                nc.sync.wait_ge(sv.h, ev6[2 * i - 3])
            W(nc.sync, xsp_toks[i])
            tx = cha.issue(nc.sync, out=xt, in_=d_x2[128 * i:128 * i + 128, :])
            for s2 in range(2):
                kk = 2 * i + s2
                ps = ps_a if kk % 2 == 0 else ps_b
                tgt = ps[:, 0:512]
                if kk >= 2:
                    nc.tensor.wait_ge(sv.h, ev6[kk - 2])
                for et in range(8):
                    inst = nc.tensor.matmul(
                        tgt, bf(b_cu + 2048 * et, 2048)[:, 128 * i:128 * i + 128],
                        bf(b_eo + 1024 * et, 1024)[:, 512 * s2:512 * s2 + 512],
                        start=(et == 0), stop=(et == 7))
                st_.inc(inst)
                nc.vector.wait_ge(st_.h, st_.n)
                if s2 == 0:
                    W(nc.vector, tx)
                    if i >= 1:
                        W(nc.vector, y_toks[i - 1])
                nc.vector.tensor_scalar(tgt, tgt, f32(f_rlc, 16)[:, i:i + 1], None,
                                        op0=AO.mult)
                nc.vector.tensor_tensor(tgt, tgt,
                                        bf(b_gb, 1024)[:, 512 * s2:512 * s2 + 512],
                                        op=AO.mult)
                sv.inc(nc.vector.tensor_tensor(yt[:, 512 * s2:512 * s2 + 512], tgt,
                                               xt[:, 512 * s2:512 * s2 + 512], op=AO.add))
                ev6.append(sv.n)
            nc.sync.wait_ge(sv.h, sv.n)
            y_toks[i] = cho.issue(nc.sync, out=y_out[128 * i:128 * i + 128, :], in_=yt)
        cho.wait_all(nc.gpsimd)
        chg.wait_all(nc.gpsimd)
    return nc, sorted(dbg_t)


_CONSTS = host_consts()


def core_inputs(b, inputs):
    return {
        "x": np.ascontiguousarray(inputs["L0_attn_x"][b]),
        "cvec": np.ascontiguousarray(inputs["L0_c"][b]),
        "wq": inputs["L0_attn_Wqkv_w"], "wqb": inputs["L0_attn_Wqkv_b"],
        "u": inputs["L0_attn_U"], "wo": inputs["L0_attn_Wo_w"],
        "wmod": inputs["L0_wmod"],
        "ng": inputs["L0_ffn_norm_g"], "nb": inputs["L0_ffn_norm_b"],
        "sg": inputs["L0_ffn_slot_g"], "sbv": inputs["L0_ffn_slot_b"],
        "sc": inputs["L0_ffn_scale"],
        "semb": np.ascontiguousarray(np.asarray(inputs["L0_ffn_slot_embeds"]).reshape(E * S, D)),
        "wgx": inputs["L0_ffn_Wg"], "bgx": inputs["L0_ffn_bg"],
        "wux": inputs["L0_ffn_Wu"], "bux": inputs["L0_ffn_bu"],
        "wdx": inputs["L0_ffn_Wd"], "bdx": inputs["L0_ffn_bd"],
        **_CONSTS,
    }


def kernel(**inputs):
    nc, _ = build()
    in_maps = [core_inputs(b, inputs) for b in range(8)]
    res = run_bass_kernel_spmd(nc, in_maps, core_ids=list(range(8)))
    return np.stack([res.results[b]["y"] for b in range(8)], axis=0)


# version 08 (final)
# speedup vs baseline: 1.0948x; 1.0576x over previous
import numpy as np
from contextlib import ExitStack

import concourse.bass as bass
import concourse.mybir as mybir
from concourse.bass_utils import run_bass_kernel_spmd
from concourse.alu_op_type import AluOpType as AO

F32 = mybir.dt.float32
BF16 = mybir.dt.bfloat16
I32 = mybir.dt.int32
AF = mybir.ActivationFunctionType
AX = mybir.AxisListType.X

N, D, NT, KC8, H = 2048, 1024, 16, 8, 16
E, S, F, FT = 16, 64, 2048, 16
EPS = 1e-5
MAGIC = 12582912.0


def host_consts():
    half = 32
    inv = (1.0 / 10000.0 ** (np.arange(half, dtype=np.float32) / half))
    ang = np.arange(N, dtype=np.float32)[:, None] * inv
    cos, sin = np.cos(ang).astype(np.float32), np.sin(ang).astype(np.float32)
    cq64 = np.repeat(cos.T, 2, axis=0)[:64]
    sq64 = np.repeat(sin.T, 2, axis=0)[:64]
    mpos = np.empty(N, np.int64)
    for c in range(16):
        mpos[128 * c:128 * c + 128] = 16 * np.arange(128) + c
    ck64, sk64 = cq64[:, mpos], sq64[:, mpos]
    dup = lambda t: np.concatenate([t, t], 0).astype(np.float32)
    return {"tcq": dup(cq64), "tsq": dup(sq64), "tck": dup(ck64),
            "tsk": dup(sk64)}


def build(debug=(), stop_after=None, p1_mode=None):
    nc = bass.Bass("TRN2", target_bir_lowering=False, dynamic_dma_scratch_size=16384,
                   detect_race_conditions=False)
    dr = lambda nm, sh, kind="ExternalInput": nc.dram_tensor(nm, sh, F32, kind=kind).ap()
    x_in = dr("x", [N, D]); cvec_in = dr("cvec", [D])
    wq_in = dr("wq", [D, D]); wqb_in = dr("wqb", [D]); u_in = dr("u", [64, 32])
    wo_in = dr("wo", [D, D]); wmod_in = dr("wmod", [D, 6 * D])
    ng_in = dr("ng", [D]); nb_in = dr("nb", [D])
    sg_in = dr("sg", [D]); sbv_in = dr("sbv", [D]); sc_in = dr("sc", [1])
    semb_in = dr("semb", [E * S, D])
    wg_in = dr("wgx", [E, D, F]); bg_in = dr("bgx", [E, F])
    wu_in = dr("wux", [E, D, F]); bu_in = dr("bux", [E, F])
    wd_in = dr("wdx", [E, F, D]); bd_in = dr("bdx", [E, D])
    tcq_in = dr("tcq", [128, N]); tsq_in = dr("tsq", [128, N])
    tck_in = dr("tck", [128, N]); tsk_in = dr("tsk", [128, N])
    y_out = dr("y", [N, D], "ExternalOutput")
    d_x2 = dr("d_x2", [N, D], "Internal")
    d_bn = dr("d_bn", [8192], "Internal")

    dbg_t = {}
    def dbg(nm, sh):
        if nm in debug and nm not in dbg_t:
            dbg_t[nm] = nc.dram_tensor("dbg_" + nm, sh, F32, kind="ExternalOutput").ap()
        return nm in debug

    with ExitStack() as ctx:
        BFN = 83584
        FFN_ = 9984
        ARB = ctx.enter_context(nc.sbuf_tensor("ARB", [128, BFN], BF16))
        ARF = ctx.enter_context(nc.sbuf_tensor("ARF", [128, FFN_], F32))

        def bf(off, n, p0=0, p1=128):
            assert off + n <= BFN, (off, n)
            return ARB[p0:p1, off:off + n]

        def f32(off, nf, p0=0, p1=128):
            assert off + nf <= FFN_, (off, nf)
            return ARF[p0:p1, off:off + nf]

        def i32f(off, nf, p0=0, p1=128):
            return ARF[:, off:off + nf].bitcast(I32)[p0:p1, :]

        ftr0, ftr1, ftr2 = 0, 1024, 2048
        FW = 3072
        f_wtmp = FW
        f_qbrow = FW
        f_rows = FW
        f_gn, f_bn = FW, FW + 1024
        f_mx, f_sub, f_lcrow = FW, FW + 2048, FW + 4096
        f_bd, f_fb = FW, FW + 2048
        fsm = 9216
        f_stat = fsm
        f_alpha = fsm + 64
        f_thr = fsm + 80
        f_silu = fsm + 96
        f_qbc = fsm + 112
        f_rl = fsm + 128
        f_rlc = fsm + 144
        f_sccol = fsm + 160
        f_row = fsm + 176
        f_row2 = fsm + 688

        G1 = 0
        a_p65 = 0
        SMB = 16640
        sm_pad = SMB + 0
        sm_ab = SMB + 128
        sm_silub = SMB + 192
        sm_ones = SMB + 256
        sm_mxt = None
        B = 17024
        b_projT = B + 0
        b_lgT = B + 0
        b_wg = B + 24576
        b_wu = B + 0
        b_wd = B + 8192
        b_gu = B + 12288
        b_wtT = B + 16384
        b_ylo = B + 16384
        b_seN = B + 16384
        b_disp = B + 16384
        b_eo = B + 16384
        b_woT = B + 24576
        b_seT = B + 24576
        b_modA = B + 32768
        b_mxb = B + 32768
        b_slt = B + 32768
        b_tab = B + 35840
        b_modB = B + 36864
        sm_mxt = B + 38912
        b_j6 = B + 40960
        b_gb = B + 41984
        b_absw = B + 44032
        b_wmod = 0
        b_xqT = B + 44032
        b_pt = B + 44032
        b_e12q = B + 48128
        b_e12k = B + 50176
        b_kp = B + 52224
        b_qla = B + 54272
        b_kla = B + 56320
        b_xnT = B + 44032
        b_cu = B + 44032
        b_sgn = B + 52224
        b_abt = B + 52224
        b_wtv = B + 0
        b_yhi = B + 58368

        def ytile(i):
            return bf((b_ylo + 1024 * i) if i < 8 else (b_yhi + 1024 * (i - 8)), 1024)

        ps_a = ctx.enter_context(nc.psum_tensor("ps_a", [128, 1024], F32))
        ps_b = ctx.enter_context(nc.psum_tensor("ps_b", [128, 1024], F32))
        ps_y = ctx.enter_context(nc.psum_tensor("ps_y", [128, 1536], F32))
        ps_s = ctx.enter_context(nc.psum_tensor("ps_s", [128, 512], F32))

        class Sem:
            def __init__(self, nm):
                self.h = ctx.enter_context(nc.semaphore(nm))
                self.n = 0
            def inc(self, inst, k=1):
                inst.then_inc(self.h, k)
                self.n += k
                return self.n
        sv = Sem("sv"); st_ = Sem("st"); ss = Sem("ss"); bar = Sem("bar")

        class DmaChain:
            def __init__(self, name, k=4):
                self.sems = [Sem(f"{name}{i}") for i in range(k)]
                self.k = k
                self.i = 0
            def _pre(self, eng):
                s = self.sems[self.i % self.k]
                if s.n > 0:
                    eng.wait_ge(s.h, s.n)
                return s
            def issue(self, eng, **kw):
                s = self._pre(eng)
                s.inc(eng.dma_start(**kw), 16)
                self.i += 1
                return (s.h, s.n)
            def issue_t(self, eng, dst, src):
                s = self._pre(eng)
                s.inc(eng.dma_start_transpose(dst, src), 16)
                self.i += 1
                return (s.h, s.n)
            def plan(self, n):
                vals = {id(s): s.n for s in self.sems}
                toks = []
                for j in range(n):
                    s = self.sems[(self.i + j) % self.k]
                    vals[id(s)] += 16
                    toks.append((s.h, vals[id(s)]))
                return toks
            def wait_all(self, eng):
                for s in self.sems:
                    if s.n > 0:
                        eng.wait_ge(s.h, s.n)

        W = lambda eng, tok: eng.wait_ge(tok[0], tok[1])
        cha = DmaChain("cha", 10)
        chs = DmaChain("chs", 4)
        chg = DmaChain("chg", 8)
        cho = DmaChain("cho", 4)

        blk = ctx.enter_context(nc.Block())
        ENG = (nc.sync, nc.tensor, nc.vector, nc.scalar, nc.gpsimd)

        def barrier():
            for eng in ENG:
                eng.drain()
                eng.nop().then_inc(bar.h, 1)
            bar.n += 5
            for eng in ENG:
                eng.wait_ge(bar.h, bar.n)

        STAT = lambda j: f32(f_stat, 64)[:, j:j + 1]

        def dbg_store(nm, dst_ap, src_ap, eng_sem=None):
            if eng_sem is not None:
                nc.gpsimd.wait_ge(eng_sem.h, eng_sem.n)
            tok = chg.issue(nc.gpsimd, out=dst_ap, in_=src_ap)
            W(nc.gpsimd, tok)

        def dma_transpose_batch(items, gate_sem, gate_val):
            nc.sync.wait_ge(gate_sem.h, gate_val)
            for dst, src in items:
                cha.issue_t(nc.sync, dst, src)

        t_cv = cha.issue(nc.sync, out=f32(f_silu, 8),
                         in_=cvec_in.rearrange("(k p) -> p k", p=128),
                         allow_slow_non_contiguous=True)
        t_qbc = cha.issue(nc.sync, out=f32(f_qbc, 8),
                          in_=wqb_in.rearrange("(o p) -> p o", p=128),
                          allow_slow_non_contiguous=True)
        for o, t in ((b_tab, tcq_in), (b_tab + 2048, tsq_in),
                     (b_tab + 4096, tck_in), (b_tab + 6144, tsk_in)):
            chg.issue(nc.gpsimd, out=bf(o, 2048), in_=t)
        nc.vector.memset(bf(sm_ones, 64), 1.0)
        nc.vector.memset(f32(f_stat, 64), 0.0)
        sv.inc(nc.vector.memset(bf(sm_pad, 128), 0.0))
        nc.gpsimd.wait_ge(sv.h, sv.n)
        t_u = chg.issue(nc.gpsimd, out=bf(sm_pad, 128)[0:64, 0:32], in_=u_in)

        def ternary(w_in, wtT_off, scratch_guard_sv):
            nc.scalar.wait_ge(sv.h, scratch_guard_sv)
            nc.sync.wait_ge(ss.h, ss.n)
            for hh in range(2):
                toks = []
                for q in range(4):
                    if hh == 1 and q == 0:
                        nc.sync.wait_ge(ss.h, ss.n)
                    toks.append(cha.issue(
                        nc.sync, out=f32(f_wtmp + 1024 * q, 1024),
                        in_=w_in[128 * (4 * hh + q):128 * (4 * hh + q) + 128, :]))
                for q in range(4):
                    ot = 4 * hh + q
                    W(nc.scalar, toks[q])
                    ss.inc(nc.scalar.activation(
                        bf(b_absw + 1024 * ot, 1024), f32(f_wtmp + 1024 * q, 1024),
                        AF.Abs, accum_out=f32(f_alpha, 8)[:, ot:ot + 1]))
                    ss.inc(nc.scalar.activation(
                        bf(b_sgn + 1024 * ot, 1024), f32(f_wtmp + 1024 * q, 1024),
                        AF.Sign))
            nc.vector.wait_ge(ss.h, ss.n)
            sv.inc(nc.vector.tensor_reduce(f32(f_thr, 1), f32(f_alpha, 8), AX, AO.add))
            sv.inc(nc.vector.tensor_scalar(f32(f_alpha, 8), f32(f_alpha, 8),
                                           1.0 / D, None, op0=AO.mult))
            nc.sync.wait_ge(sv.h, sv.n)
            t1 = cha.issue(nc.sync, out=d_bn[0:128][:, None], in_=f32(f_thr, 1))
            W(nc.sync, t1)
            t2 = cha.issue(nc.sync, out=f32(f_row, 512, p1=1)[:, 0:128],
                           in_=d_bn[0:128][None, :])
            W(nc.vector, t2)
            sv.inc(nc.vector.tensor_reduce(f32(f_row2, 1, p1=1),
                                           f32(f_row, 512, p1=1)[:, 0:128], AX, AO.add))
            nc.vector.drain()
            sv.inc(nc.vector.tensor_scalar(f32(f_row2, 1, p1=1), f32(f_row2, 1, p1=1),
                                           0.05 / (D * D), None, op0=AO.mult))
            nc.sync.wait_ge(sv.h, sv.n)
            t3 = cha.issue(nc.sync, out=d_bn[128:129][None, :], in_=f32(f_row2, 1, p1=1))
            W(nc.sync, t3)
            t4 = cha.issue(nc.sync, out=f32(f_thr, 1),
                           in_=d_bn[128:129][None, :].to_broadcast((128, 1)))
            W(nc.vector, t4)
            cha.wait_all(nc.vector)
            chs.wait_all(nc.vector)
            for ot in range(8):
                nc.vector.scalar_tensor_tensor(bf(b_wtv + 1024 * ot, 1024),
                                               bf(b_absw + 1024 * ot, 1024),
                                               f32(f_thr, 1),
                                               bf(b_sgn + 1024 * ot, 1024),
                                               op0=AO.is_ge, op1=AO.mult)
            for ot in range(8):
                inst = nc.vector.tensor_scalar(bf(b_wtv + 1024 * ot, 1024),
                                               bf(b_wtv + 1024 * ot, 1024),
                                               f32(f_alpha, 8)[:, ot:ot + 1],
                                               None, op0=AO.mult)
            sv.inc(inst)
            scratch_free = sv.n
            items = []
            for ot in range(8):
                for kc in range(KC8):
                    items.append((bf(wtT_off + 1024 * kc, 1024)[:, 128 * ot:128 * ot + 128],
                                  bf(b_wtv + 1024 * ot, 1024)[:, 128 * kc:128 * kc + 128]))
            dma_transpose_batch(items, sv, scratch_free)
            return scratch_free

        sem_me = Sem("me")
        W(nc.scalar, t_cv)
        ss.inc(nc.scalar.activation(f32(f_silu, 8), f32(f_silu, 8), AF.Silu))
        nc.vector.wait_ge(ss.h, ss.n)
        sv.inc(nc.vector.tensor_copy(bf(sm_silub, 16)[:, 0:8], f32(f_silu, 8)))
        silu_done = sv.n
        psmap = [ps_a[0:1, 0:512], ps_a[0:1, 512:1024], ps_b[0:1, 0:512],
                 ps_b[0:1, 512:1024], ps_y[0:1, 0:512], ps_y[0:1, 512:1024]]
        wm_toks = chg.plan(16)
        mm_done = {}
        mod_row_tok = None
        nc.tensor.wait_ge(sv.h, silu_done)
        for pph in range(2):
            for kc in range(KC8):
                idx = pph * 8 + kc
                W(nc.tensor, wm_toks[idx])
                if pph == 1 and kc == 0:
                    nc.tensor.wait_ge(sem_me.h, 1)
                for sl in range(6):
                    inst = nc.tensor.matmul(
                        psmap[sl], bf(sm_silub, 16)[:, kc:kc + 1],
                        bf(b_wmod + 3072 * (kc % 2), 3072)[:, 512 * sl:512 * sl + 512],
                        start=(kc == 0), stop=(kc == 7))
                st_.inc(inst)
                mm_done[idx] = st_.n
        for idx in range(16):
            pph, kc = idx // 8, idx % 8
            if idx >= 2:
                nc.gpsimd.wait_ge(st_.h, mm_done[idx - 2])
            tok = chg.issue(nc.gpsimd,
                            out=bf(b_wmod + 3072 * (kc % 2), 3072),
                            in_=wmod_in[128 * kc:128 * kc + 128,
                                        3072 * pph:3072 * (pph + 1)])
            assert tok == wm_toks[idx], (tok, wm_toks[idx])

        def evac_mod(pph, handshake):
            global_tok = None
            nc.vector.wait_ge(st_.h, mm_done[pph * 8 + 7])
            for sl in range(6):
                tok = getattr(evac_mod, "row_tok", None)
                if tok is not None:
                    W(nc.vector, tok)
                inst = nc.vector.tensor_copy(f32(f_row, 512, p1=1), psmap[sl])
                if handshake and sl == 5:
                    sem_me.inc(inst)
                else:
                    sv.inc(inst)
                nc.sync.wait_ge(sv.h, sv.n)
                if handshake and sl == 5:
                    nc.sync.wait_ge(sem_me.h, 1)
                evac_mod.row_tok = cha.issue(
                    nc.sync,
                    out=d_bn[1024 + 3072 * pph + 512 * sl:
                             1024 + 3072 * pph + 512 * (sl + 1)][None, :],
                    in_=f32(f_row, 512, p1=1))

        g1 = ternary(wq_in, b_wtT, sv.n)
        evac_mod(0, True)
        absw_free_sv = ternary(wo_in, b_woT, g1)
        evac_mod(1, False)

        nc.vector.wait_ge(sv.h, absw_free_sv)
        nc.sync.wait_ge(sv.h, absw_free_sv)
        W(nc.sync, t_u)
        tU = cha.issue_t(nc.sync, bf(b_abt + 4096, 128), bf(sm_pad, 128))
        W(nc.vector, tU)
        ut = bf(b_abt + 4096, 128, p0=0, p1=32)
        abT = bf(b_abt, 128, p0=0, p1=32)
        nc.vector.tensor_copy(abT[:, 0:64:2], ut[:, 0:32])
        nc.vector.tensor_copy(abT[:, 1:64:2], ut[:, 32:64])
        nc.vector.tensor_copy(abT[:, 64:128:2], ut[:, 32:64])
        nc.vector.tensor_scalar(abT[:, 65:128:2], ut[:, 0:32], -1.0, None, op0=AO.mult)
        nc.vector.memset(bf(b_abt, 128, p0=32, p1=64), 0.0)
        sv.inc(nc.vector.memset(bf(b_abt, 128, p0=64, p1=128), 0.0))
        nc.sync.wait_ge(sv.h, sv.n)
        tAB = cha.issue_t(nc.sync, bf(b_abt + 8192, 128), bf(b_abt, 128))
        W(nc.vector, tAB)
        nc.vector.tensor_scalar(bf(sm_ab, 64)[:, 0:32], bf(b_abt + 8192, 128)[:, 0:32],
                                0.125, None, op0=AO.mult)
        sv.inc(nc.vector.tensor_copy(bf(sm_ab, 64)[:, 32:64],
                                     bf(b_abt + 8192, 128)[:, 0:32]))
        ab_done = sv.n
        cha.wait_all(nc.sync)

        def mod_bcast(spec):
            for off, kind, j in spec:
                nc.sync.wait_ge(sv.h, sv.n)
                t5 = cha.issue(nc.sync, out=f32(ftr2, 1024),
                               in_=d_bn[1024 + 1024 * j:1024 + 1024 * (j + 1)][None, :]
                               .to_broadcast((128, 1024)))
                W(nc.vector, t5)
                if kind == "cp":
                    sv.inc(nc.vector.tensor_copy(bf(off, 1024), f32(ftr2, 1024)))
                elif kind == "ca":
                    sv.inc(nc.vector.tensor_scalar(bf(off, 1024), f32(ftr2, 1024),
                                                   1.0, None, op0=AO.add))
                else:
                    nc.vector.tensor_scalar(f32(ftr0, 1024), f32(ftr2, 1024), 0.0, 2.0,
                                            op0=AO.is_ge, op1=AO.mult)
                    nc.vector.tensor_scalar(f32(ftr0, 1024), f32(ftr0, 1024), -1.0,
                                            None, op0=AO.add)
                    nc.vector.tensor_scalar(f32(ftr1, 1024), f32(ftr2, 1024), -1.0,
                                            None, op0=AO.mult)
                    nc.vector.tensor_tensor(f32(ftr1, 1024), f32(ftr1, 1024),
                                            f32(ftr2, 1024), op=AO.max)
                    nc.vector.tensor_scalar(f32(ftr1, 1024), f32(ftr1, 1024), 0.25,
                                            None, op0=AO.max)
                    sv.inc(nc.vector.tensor_tensor(bf(off, 1024), f32(ftr0, 1024),
                                                   f32(ftr1, 1024), op=AO.mult))

        dbg_wtt = dbg("wtt", [128, 8 * 1024])
        dbg_wot = dbg("wot", [128, 8 * 1024])
        dbg_wtv = dbg("wtv2", [128, 8 * 1024])
        if dbg_wtt or dbg_wot or dbg_wtv:
            cha.wait_all(nc.gpsimd)
            chs.wait_all(nc.gpsimd)
            if "wtt" in dbg_t:
                dbg_store("wtt", dbg_t["wtt"], bf(b_wtT, 8192), sv)
            if "wot" in dbg_t:
                dbg_store("wot", dbg_t["wot"], bf(b_woT, 8192), sv)
            if "wtv2" in dbg_t:
                dbg_store("wtv2", dbg_t["wtv2"], bf(b_wtv, 8192), sv)
        mod_bcast([(b_modA + 1024, "cp", 0), (b_modA, "ca", 1), (b_modA + 2048, "g", 2)])
        if dbg("mod", [128, 6 * 1024]):
            for j, off in enumerate([b_modA + 1024, b_modA, b_modA + 2048]):
                dbg_store("mod", dbg_t["mod"][:, 1024 * j:1024 * (j + 1)],
                          bf(off, 1024), sv)
        barrier()
        if stop_after == 1:
            cho.wait_all(nc.gpsimd); chg.wait_all(nc.gpsimd)
            cha.wait_all(nc.gpsimd); chs.wait_all(nc.gpsimd)
            return nc, sorted(dbg_t)

        slot_v, slot_s = {}, {}
        p1n = 2 if p1_mode == "nt2" else NT
        for i in range(p1n):
            par = i % 2
            sb_ = 8 * par
            foff = ftr0 + 1024 * par
            xt = f32(foff, 1024)
            if i >= 2:
                nc.sync.wait_ge(sv.h, slot_v[i - 2])
                nc.sync.wait_ge(ss.h, slot_s[i - 2])
            tx = cha.issue(nc.sync, out=xt, in_=x_in[128 * i:128 * i + 128, :])
            W(nc.vector, tx)
            if i >= 2:
                nc.vector.wait_ge(ss.h, slot_s[i - 2])
            nc.vector.tensor_reduce(STAT(sb_ + 0), xt, AX, AO.add)
            nc.vector.drain()
            sv.inc(nc.vector.tensor_scalar(STAT(sb_ + 1), STAT(sb_ + 0), -1.0 / D,
                                           None, op0=AO.mult))
            nc.scalar.wait_ge(sv.h, sv.n)
            ss.inc(nc.scalar.activation(xt, xt, AF.Identity, bias=STAT(sb_ + 1)))
            ss.inc(nc.scalar.activation(bf(G1 + 1024 * i, 1024), xt, AF.Square,
                                        accum_out=STAT(sb_ + 2)))
            nc.vector.wait_ge(ss.h, ss.n)
            sv.inc(nc.vector.tensor_scalar(STAT(sb_ + 3), STAT(sb_ + 2), 1.0 / D, EPS,
                                           op0=AO.mult, op1=AO.add))
            nc.scalar.wait_ge(sv.h, sv.n)
            ss.inc(nc.scalar.activation(STAT(sb_ + 3), STAT(sb_ + 3), AF.Sqrt))
            nc.vector.wait_ge(ss.h, ss.n)
            sv.inc(nc.vector.reciprocal(STAT(sb_ + 4), STAT(sb_ + 3)))
            nc.scalar.wait_ge(sv.h, sv.n)
            ss.inc(nc.scalar.activation(xt, xt, AF.Identity, scale=STAT(sb_ + 4)))
            nc.vector.wait_ge(ss.h, ss.n)
            nc.vector.tensor_tensor(xt, xt, bf(b_modA, 1024), op=AO.mult)
            sv.inc(nc.vector.tensor_tensor(xt, xt, bf(b_modA + 1024, 1024), op=AO.add))
            nc.vector.tensor_reduce(STAT(sb_ + 5), xt, AX, AO.max,
                                    apply_absolute_value=True)
            nc.vector.drain()
            nc.vector.tensor_scalar(STAT(sb_ + 5), STAT(sb_ + 5), 1e-6, None, op0=AO.max)
            nc.vector.drain()
            nc.vector.reciprocal(STAT(sb_ + 6), STAT(sb_ + 5))
            nc.vector.drain()
            nc.vector.tensor_scalar(STAT(sb_ + 6), STAT(sb_ + 6), 7.0, None, op0=AO.mult)
            sv.inc(nc.vector.tensor_scalar(STAT(sb_ + 7), STAT(sb_ + 5), 1.0 / 7.0,
                                           None, op0=AO.mult))
            nc.scalar.wait_ge(sv.h, sv.n)
            ss.inc(nc.scalar.activation(xt, xt, AF.Identity, scale=STAT(sb_ + 6)))
            nc.vector.wait_ge(ss.h, ss.n)
            nc.vector.tensor_scalar(f32(ftr2, 1024), xt, MAGIC, None, op0=AO.add)
            nc.vector.tensor_scalar(xt, f32(ftr2, 1024), -MAGIC, None, op0=AO.add)
            sv.inc(nc.vector.tensor_copy(i32f(ftr2, 1024), i32f(foff, 1024)))
            slot_v[i] = sv.n
            nc.scalar.wait_ge(sv.h, sv.n)
            ss.inc(nc.scalar.activation(bf(G1 + 1024 * i, 1024), f32(ftr2, 1024),
                                        AF.Identity, scale=STAT(sb_ + 7)))
            slot_s[i] = ss.n
            nc.sync.wait_ge(ss.h, ss.n)
            for kc in range(KC8):
                src = bf(G1 + 1024 * i, 1024)[:, 128 * kc:128 * kc + 128]
                dst = bf(b_xqT + 2048 * kc, 2048)[:, 128 * i:128 * i + 128]
                cha.issue_t(nc.sync, dst, src)
        if dbg("p1rnd", [128, 1024]):
            dbg_store("p1rnd", dbg_t["p1rnd"], f32(ftr2, 1024), ss)
        if dbg("p1st", [128, 64]):
            dbg_store("p1st", dbg_t["p1st"], f32(f_stat, 64), ss)
        if dbg("hq", [N, D]):
            for i in range(NT):
                dbg_store("hq", dbg_t["hq"][128 * i:128 * i + 128, :],
                          bf(G1 + 1024 * i, 1024), ss)
        barrier()
        if stop_after == 2:
            cho.wait_all(nc.gpsimd); chg.wait_all(nc.gpsimd)
            cha.wait_all(nc.gpsimd); chs.wait_all(nc.gpsimd)
            return nc, sorted(dbg_t)

        t_qbrow = cha.issue(nc.sync, out=f32(f_qbrow, 1024),
                            in_=wqb_in[None, :].to_broadcast((128, 1024)))
        cha.wait_all(nc.vector)
        chs.wait_all(nc.vector)
        for i in range(NT):
            p65t = bf(a_p65 + 1040 * i, 1040).rearrange("p (c w) -> p c w", w=65)
            inst = nc.vector.memset(p65t[:, :, 64:65], 1.0)
        sv.inc(inst)
        ones65 = sv.n
        cha.wait_all(nc.tensor)
        chs.wait_all(nc.tensor)
        evq = []
        for ot in range(8):
            ps = ps_a if ot % 2 == 0 else ps_b
            for sl in range(4):
                k = ot * 4 + sl
                tgt = ps[:, 512 * (sl % 2):512 * (sl % 2) + 512]
                if k >= 2:
                    nc.tensor.wait_ge(sv.h, evq[k - 2])
                for kc in range(KC8):
                    inst = nc.tensor.matmul(
                        tgt, bf(b_wtT + 1024 * kc, 1024)[:, 128 * ot:128 * ot + 128],
                        bf(b_xqT + 2048 * kc, 2048)[:, 512 * sl:512 * sl + 512],
                        start=(kc == 0), stop=(kc == 7))
                st_.inc(inst)
                nc.vector.wait_ge(st_.h, st_.n)
                if k == 0:
                    W(nc.vector, t_qbc)
                sv.inc(nc.vector.tensor_scalar(
                    bf(b_projT + 2048 * ot, 2048)[:, 512 * sl:512 * sl + 512],
                    tgt, f32(f_qbc, 8)[:, ot:ot + 1], None, op0=AO.add))
                evq.append(sv.n)
        W(nc.tensor, t_qbrow)
        ev2 = []
        for i in range(NT):
            for s2 in range(2):
                kk = 2 * i + s2
                ps = ps_a if kk % 2 == 0 else ps_b
                tgt = ps[:, 0:512]
                if kk >= 2:
                    nc.tensor.wait_ge(sv.h, ev2[kk - 2])
                for kc in range(KC8):
                    inst = nc.tensor.matmul(
                        tgt, bf(b_xqT + 2048 * kc, 2048)[:, 128 * i:128 * i + 128],
                        bf(b_wtT + 1024 * kc, 1024)[:, 512 * s2:512 * s2 + 512],
                        start=(kc == 0), stop=(kc == 7))
                st_.inc(inst)
                nc.vector.wait_ge(st_.h, st_.n)
                if kk == 0:
                    nc.vector.wait_ge(sv.h, ones65)
                    W(nc.vector, t_qbrow)
                p65t = bf(a_p65 + 1040 * i, 1040).rearrange("p (c w) -> p c w", w=65)
                sv.inc(nc.vector.tensor_tensor(
                    p65t[:, 8 * s2:8 * s2 + 8, 0:64],
                    tgt.rearrange("p (c w) -> p c w", w=64),
                    f32(f_qbrow, 1024)[:, 512 * s2:512 * s2 + 512]
                    .rearrange("p (c w) -> p c w", w=64),
                    op=AO.add))
                ev2.append(sv.n)
        if dbg("projT", [128, 8 * 2048]):
            dbg_store("projT", dbg_t["projT"], bf(b_projT, 16384), sv)
        if dbg("p65", [128, 16 * 1040]):
            dbg_store("p65", dbg_t["p65"], bf(a_p65, 16640), sv)
        barrier()
        if stop_after == 3:
            cho.wait_all(nc.gpsimd); chg.wait_all(nc.gpsimd)
            cha.wait_all(nc.gpsimd); chs.wait_all(nc.gpsimd)
            return nc, sorted(dbg_t)

        nc.vector.memset(bf(b_kla, 2048, p0=32, p1=33), 1.0)
        sv.inc(nc.vector.memset(f32(f_rl, 16), 1.0))
        yev_prev = sv.n
        scores_done_prev = 0
        pv_prev = {}
        exp_prev = {}
        rows = lambda r: f32(f_rows, 2048, p0=32 * r, p1=32 * r + 1)
        nheads = int(p1_mode[1:]) if (p1_mode or "").startswith("h") else \
            (1 if (p1_mode or "").startswith("s") else H)
        for h in range(nheads):
            hb = 64 * (h % 2)
            projt = bf(b_projT + 2048 * (h // 2), 2048)
            nc.vector.wait_ge(st_.h, scores_done_prev)
            nc.vector.tensor_tensor(bf(b_e12q, 2048, p0=0, p1=64),
                                    projt[hb:hb + 64, :],
                                    bf(b_tab, 2048)[hb:hb + 64, :], op=AO.mult)
            sv.inc(nc.vector.tensor_tensor(bf(b_e12q, 2048, p0=64, p1=128),
                                           projt[hb:hb + 64, :],
                                           bf(b_tab + 2048, 2048)[hb:hb + 64, :],
                                           op=AO.mult))
            for c in range(16):
                cb_ = 64 * (c % 2)
                src = bf(b_projT + 2048 * (c // 2), 2048)[cb_:cb_ + 64,
                                                          128 * h:128 * h + 128]
                nc.vector.tensor_tensor(
                    bf(b_e12k, 2048, p0=0, p1=64)[:, 128 * c:128 * c + 128], src,
                    bf(b_tab + 4096, 2048)[cb_:cb_ + 64, 128 * c:128 * c + 128],
                    op=AO.mult)
                inst = nc.vector.tensor_tensor(
                    bf(b_e12k, 2048, p0=64, p1=128)[:, 128 * c:128 * c + 128], src,
                    bf(b_tab + 6144, 2048)[cb_:cb_ + 64, 128 * c:128 * c + 128],
                    op=AO.mult)
            sv.inc(inst)
            e12_done = sv.n
            if p1_mode == "s1":
                continue
            nc.tensor.wait_ge(sv.h, e12_done)
            if h == 0:
                nc.tensor.wait_ge(sv.h, ab_done)
            if h > 0:
                nc.tensor.wait_ge(ss.h, exp_prev[15])
                nc.tensor.wait_ge(sv.h, yev_prev)
            for which, aboff, e12off, laoff in ((0, 0, b_e12q, b_qla),
                                                (1, 32, b_e12k, b_kla)):
                if which == 1:
                    nc.tensor.wait_ge(sv.h, sv.n)
                    nc.tensor.wait_ge(ss.h, ss.n)
                for sl in range(4):
                    ps = (ps_a, ps_a, ps_b, ps_b)[sl]
                    tgt = ps[0:32, 512 * (sl % 2):512 * (sl % 2) + 512]
                    st_.inc(nc.tensor.matmul(tgt, bf(sm_ab, 64)[:, aboff:aboff + 32],
                                             bf(e12off, 2048)[:, 512 * sl:512 * sl + 512],
                                             start=True, stop=True))
                    if p1_mode == "s2a":
                        continue
                    nc.vector.wait_ge(st_.h, st_.n)
                    sv.inc(nc.vector.tensor_copy(
                        bf(laoff, 2048, p0=0, p1=32)[:, 512 * sl:512 * sl + 512], tgt))
                    if p1_mode == "s2b":
                        continue
                    nc.scalar.wait_ge(sv.h, sv.n)
                    ss.inc(nc.scalar.activation(
                        bf(b_e12q, 2048, p0=0, p1=32)[:, 512 * sl:512 * sl + 512],
                        tgt, AF.Square))
                nc.tensor.wait_ge(ss.h, ss.n)
                if p1_mode == "s2":
                    continue
                bev = []
                for sl in range(4):
                    tgt = ps_y[0:1, 512 * (sl % 2):512 * (sl % 2) + 512]
                    if sl >= 2:
                        nc.tensor.wait_ge(sv.h, bev[sl - 2])
                    st_.inc(nc.tensor.matmul(tgt, bf(sm_ones, 64)[0:32, 0:1],
                                             bf(b_e12q, 2048, p0=0, p1=32)[:, 512 * sl:512 * sl + 512],
                                             start=True, stop=True))
                    nc.vector.wait_ge(st_.h, st_.n)
                    sv.inc(nc.vector.tensor_copy(
                        rows(which)[:, 512 * sl:512 * sl + 512], tgt))
                    bev.append(sv.n)
            if p1_mode in ("s2", "s2a", "s2b", "s3"):
                continue
            sv.inc(nc.vector.tensor_reduce(STAT(16)[0:1, :], rows(1), AX, AO.max))
            nc.vector.drain()
            sv.inc(nc.vector.tensor_scalar(rows(2), rows(0), STAT(16)[0:1, :],
                                           None, op0=AO.mult))
            nc.scalar.wait_ge(sv.h, sv.n)
            ss.inc(nc.scalar.activation(rows(3), rows(2), AF.Sqrt))
            nc.vector.wait_ge(ss.h, ss.n)
            sv.inc(nc.vector.tensor_scalar(bf(b_qla, 2048, p0=32, p1=33),
                                           rows(3), -1.0, None, op0=AO.mult))
            qla_done = sv.n
            if h == 0 and dbg("ql0", [33, N]):
                dbg_store("ql0", dbg_t["ql0"][0:33, :], bf(b_qla, 2048, p0=0, p1=33), sv)
            if h == 0 and dbg("kl0", [33, N]):
                dbg_store("kl0", dbg_t["kl0"][0:33, :], bf(b_kla, 2048, p0=0, p1=33), sv)
            nc.tensor.wait_ge(sv.h, qla_done)
            if p1_mode == "s4":
                continue
            exp_done = {}
            pv_done = {}
            for c in range(16):
                pt = bf(b_pt + 2048 * (c % 2), 2048)
                if c >= 1:
                    nc.tensor.wait_ge(ss.h, exp_done[c - 1])
                for half, psx in ((0, ps_a), (1, ps_b)):
                    for q2 in range(2):
                        st_.inc(nc.tensor.matmul(
                            psx[:, 512 * q2:512 * q2 + 512],
                            bf(b_kla, 2048, p0=0, p1=33)[:, 128 * c:128 * c + 128],
                            bf(b_qla, 2048, p0=0, p1=33)[:, 1024 * half + 512 * q2:
                                                         1024 * half + 512 * q2 + 512],
                            start=True, stop=True))
                    nc.scalar.wait_ge(st_.h, st_.n)
                    if c >= 2:
                        nc.scalar.wait_ge(st_.h, pv_done[c - 2])
                    elif h > 0:
                        nc.scalar.wait_ge(st_.h, pv_prev[c + 14])
                    ss.inc(nc.scalar.activation(pt[:, 1024 * half:1024 * half + 1024],
                                                psx[:, 0:1024], AF.Exp))
                exp_done[c] = ss.n
                if p1_mode == "s5":
                    pv_done[c] = st_.n
                    if c == 3:
                        scores_done = st_.n
                    continue
                nc.tensor.wait_ge(ss.h, exp_done[c])
                for j in range(16):
                    yc = 512 * (j // 7) + 65 * (j % 7) if j < 14 else 1024 + 65 * (j - 14)
                    inst = nc.tensor.matmul(
                        ps_y[:, yc:yc + 65], pt[:, 128 * j:128 * j + 128],
                        bf(a_p65 + 1040 * h, 1040)[:, 65 * c:65 * c + 65],
                        start=(c == 0 and j in (0, 7, 14)),
                        stop=(c == 15 and j in (6, 13, 15)))
                st_.inc(inst)
                pv_done[c] = st_.n
                if c == 3:
                    scores_done = st_.n
            scores_done_prev = scores_done
            exp_prev, pv_prev = exp_done, pv_done
            if p1_mode == "s5":
                continue
            nc.vector.wait_ge(st_.h, pv_done[15])
            for j in range(16):
                yc = 512 * (j // 7) + 65 * (j % 7) if j < 14 else 1024 + 65 * (j - 14)
                nc.vector.reciprocal(f32(f_rl, 16)[:, j:j + 1], ps_y[:, yc + 64:yc + 65])
            nc.vector.drain()
            for j in range(16):
                yc = 512 * (j // 7) + 65 * (j % 7) if j < 14 else 1024 + 65 * (j - 14)
                inst = nc.vector.tensor_scalar(
                    ytile(j)[:, 64 * h:64 * h + 64],
                    ps_y[:, yc:yc + 64], f32(f_rl, 16)[:, j:j + 1], None, op0=AO.mult)
            sv.inc(inst)
            yev_prev = sv.n
        if dbg("yattn", [N, D]):
            for i in range(NT):
                dbg_store("yattn", dbg_t["yattn"][128 * i:128 * i + 128, :], ytile(i), sv)
        barrier()
        if stop_after == 4:
            cho.wait_all(nc.gpsimd); chg.wait_all(nc.gpsimd)
            cha.wait_all(nc.gpsimd); chs.wait_all(nc.gpsimd)
            return nc, sorted(dbg_t)

        yqT_toks = {}
        ev3 = []
        xsp_toks = {}

        def p4_quant(i):
            par = i % 2
            sb_ = 8 * par
            yt = ytile(i)
            foff = ftr0 + 1024 * par
            zt = f32(foff, 1024)
            if i >= 2:
                nc.vector.wait_ge(ss.h, slot_s[i - 2])
            nc.vector.tensor_reduce(STAT(sb_ + 5), yt, AX, AO.max,
                                    apply_absolute_value=True)
            nc.vector.drain()
            nc.vector.tensor_scalar(STAT(sb_ + 5), STAT(sb_ + 5), 1e-6, None, op0=AO.max)
            nc.vector.drain()
            nc.vector.reciprocal(STAT(sb_ + 6), STAT(sb_ + 5))
            nc.vector.drain()
            nc.vector.tensor_scalar(STAT(sb_ + 6), STAT(sb_ + 6), 7.0, None, op0=AO.mult)
            sv.inc(nc.vector.tensor_scalar(STAT(sb_ + 7), STAT(sb_ + 5), 1.0 / 7.0,
                                           None, op0=AO.mult))
            nc.scalar.wait_ge(sv.h, sv.n)
            ss.inc(nc.scalar.activation(zt, yt, AF.Identity, scale=STAT(sb_ + 6)))
            nc.vector.wait_ge(ss.h, ss.n)
            nc.vector.tensor_scalar(f32(ftr2, 1024), zt, MAGIC, None, op0=AO.add)
            nc.vector.tensor_scalar(zt, f32(ftr2, 1024), -MAGIC, None, op0=AO.add)
            sv.inc(nc.vector.tensor_copy(i32f(ftr2, 1024), i32f(foff, 1024)))
            nc.scalar.wait_ge(sv.h, sv.n)
            ss.inc(nc.scalar.activation(yt, f32(ftr2, 1024), AF.Identity,
                                        scale=STAT(sb_ + 7)))
            slot_s[i] = ss.n
            nc.sync.wait_ge(ss.h, ss.n)
            toks = []
            for kc in range(KC8):
                src = yt[:, 128 * kc:128 * kc + 128]
                dst = bf(G1 + 2048 * kc, 2048)[:, 128 * i:128 * i + 128]
                toks.append(cha.issue_t(nc.sync, dst, src))
            yqT_toks[i] = toks

        def p4_wo(i):
            par = i % 2
            xt = f32(3072 + 1024 * par, 1024)
            x2t = f32(5120, 1024)
            if i >= 2:
                nc.gpsimd.wait_ge(sv.h, ev3[2 * i - 3])
            tx = chg.issue(nc.gpsimd, out=xt, in_=x_in[128 * i:128 * i + 128, :])
            for t in yqT_toks[i]:
                W(nc.tensor, t)
            for s2 in range(2):
                kk = 2 * i + s2
                ps = ps_a if kk % 2 == 0 else ps_b
                tgt = ps[:, 0:512]
                if kk >= 2:
                    nc.tensor.wait_ge(sv.h, ev3[kk - 2])
                for kc in range(KC8):
                    inst = nc.tensor.matmul(
                        tgt, bf(G1 + 2048 * kc, 2048)[:, 128 * i:128 * i + 128],
                        bf(b_woT + 1024 * kc, 1024)[:, 512 * s2:512 * s2 + 512],
                        start=(kc == 0), stop=(kc == 7))
                st_.inc(inst)
                nc.vector.wait_ge(st_.h, st_.n)
                if s2 == 0:
                    W(nc.vector, tx)
                    if i >= 1:
                        W(nc.vector, xsp_toks[i - 1])
                nc.vector.tensor_tensor(tgt, tgt,
                                        bf(b_modA + 2048, 1024)[:, 512 * s2:512 * s2 + 512],
                                        op=AO.mult)
                sv.inc(nc.vector.tensor_tensor(x2t[:, 512 * s2:512 * s2 + 512], tgt,
                                               xt[:, 512 * s2:512 * s2 + 512], op=AO.add))
                ev3.append(sv.n)
            nc.sync.wait_ge(sv.h, sv.n)
            xsp_toks[i] = cho.issue(nc.sync, out=d_x2[128 * i:128 * i + 128, :], in_=x2t)

        for i in range(NT + 1):
            if i < NT:
                p4_quant(i)
            if i >= 1:
                p4_wo(i - 1)
        if dbg("yq", [N, D]):
            for i in range(NT):
                dbg_store("yq", dbg_t["yq"][128 * i:128 * i + 128, :], ytile(i), ss)
        if dbg("x2", [N, D]):
            for i in range(NT):
                nc.gpsimd.wait_ge(xsp_toks[i][0], xsp_toks[i][1])
                tokd = chg.issue(nc.gpsimd, out=dbg_t["x2"][128 * i:128 * i + 128, :],
                                 in_=d_x2[128 * i:128 * i + 128, :])
                W(nc.gpsimd, tokd)
        barrier()
        if stop_after == 5:
            cho.wait_all(nc.gpsimd); chg.wait_all(nc.gpsimd)
            cha.wait_all(nc.gpsimd); chs.wait_all(nc.gpsimd)
            return nc, sorted(dbg_t)

        mod_bcast([(b_modB + 1024, "cp", 3), (b_modB, "ca", 4), (b_gb, "g", 5)])
        t_gn = cha.issue(nc.sync, out=f32(f_gn, 1024),
                         in_=ng_in[None, :].to_broadcast((128, 1024)))
        t_bn2 = cha.issue(nc.sync, out=f32(f_bn, 1024),
                          in_=nb_in[None, :].to_broadcast((128, 1024)))
        modb_done = sv.n
        if dbg("mod", [128, 6 * 1024]):
            for jj, off in enumerate([b_modB + 1024, b_modB, b_gb]):
                dbg_store("mod", dbg_t["mod"][:, 1024 * (jj + 3):1024 * (jj + 4)],
                          bf(off, 1024), sv)
        def ln_stages(i, mul_off, add_off, affine_f32, tx):
            par = i % 2
            sb_ = 8 * par
            xt = f32(ftr0 + 1024 * par, 1024)
            st = []

            def s0():
                if tx is not None:
                    W(nc.vector, tx)
                nc.vector.tensor_reduce(STAT(sb_ + 0), xt, AX, AO.add)
                nc.vector.drain()
                sv.inc(nc.vector.tensor_scalar(STAT(sb_ + 1), STAT(sb_ + 0),
                                               -1.0 / D, None, op0=AO.mult))

            def s1():
                nc.scalar.wait_ge(sv.h, sv.n)
                ss.inc(nc.scalar.activation(xt, xt, AF.Identity, bias=STAT(sb_ + 1)))
                ss.inc(nc.scalar.activation(bf(b_j6, 1024), xt, AF.Square,
                                            accum_out=STAT(sb_ + 2)))

            def s2():
                nc.vector.wait_ge(ss.h, ss.n)
                sv.inc(nc.vector.tensor_scalar(STAT(sb_ + 3), STAT(sb_ + 2), 1.0 / D,
                                               EPS, op0=AO.mult, op1=AO.add))

            def s3():
                nc.scalar.wait_ge(sv.h, sv.n)
                ss.inc(nc.scalar.activation(STAT(sb_ + 3), STAT(sb_ + 3), AF.Sqrt))

            def s4():
                nc.vector.wait_ge(ss.h, ss.n)
                sv.inc(nc.vector.reciprocal(STAT(sb_ + 4), STAT(sb_ + 3)))

            def s5():
                nc.scalar.wait_ge(sv.h, sv.n)
                ss.inc(nc.scalar.activation(xt, xt, AF.Identity, scale=STAT(sb_ + 4)))

            def s6():
                nc.vector.wait_ge(ss.h, ss.n)
                if affine_f32:
                    W(nc.vector, t_gn)
                    W(nc.vector, t_bn2)
                    nc.vector.tensor_tensor(xt, xt, f32(mul_off, 1024), op=AO.mult)
                    sv.inc(nc.vector.tensor_tensor(bf(G1 + 1024 * i, 1024), xt,
                                                   f32(add_off, 1024), op=AO.add))
                else:
                    nc.vector.tensor_tensor(xt, xt, bf(mul_off, 1024), op=AO.mult)
                    sv.inc(nc.vector.tensor_tensor(xt, xt, bf(add_off, 1024),
                                                   op=AO.add))
            return [s0, s1, s2, s3, s4, s5, s6]

        def mk_fin(i):
            def fin():
                slot_v[i] = sv.n
                nc.sync.wait_ge(sv.h, sv.n)
                for kc in range(KC8):
                    src = bf(G1 + 1024 * i, 1024)[:, 128 * kc:128 * kc + 128]
                    dst = bf(b_xnT + 2048 * kc, 2048)[:, 128 * i:128 * i + 128]
                    cha.issue_t(nc.sync, dst, src)
            return fin

        def run_zip(a, b):
            for k in range(max(len(a), len(b))):
                if k < len(a):
                    a[k]()
                if k < len(b):
                    b[k]()

        pend = []
        for i in range(NT):
            par = i % 2
            if i >= 2:
                nc.gpsimd.wait_ge(sv.h, slot_v[i - 2])
            else:
                nc.gpsimd.wait_ge(sv.h, modb_done)
            W(nc.gpsimd, xsp_toks[i])
            tx = chg.issue(nc.gpsimd, out=f32(ftr0 + 1024 * par, 1024),
                           in_=d_x2[128 * i:128 * i + 128, :])
            run_zip(pend, ln_stages(i, b_modB, b_modB + 1024, False, tx))
            pend = ln_stages(i, f_gn, f_bn, True, None) + [mk_fin(i)]
        run_zip(pend, [])
        if dbg("xn", [N, D]):
            for i in range(NT):
                dbg_store("xn", dbg_t["xn"][128 * i:128 * i + 128, :],
                          bf(G1 + 1024 * i, 1024), sv)
        barrier()
        if stop_after == 6:
            cho.wait_all(nc.gpsimd); chg.wait_all(nc.gpsimd)
            cha.wait_all(nc.gpsimd); chs.wait_all(nc.gpsimd)
            return nc, sorted(dbg_t)

        se_toks = [chg.issue(nc.gpsimd, out=bf(b_seN + 1024 * et, 1024),
                             in_=semb_in[128 * et:128 * et + 128, :]) for et in range(8)]
        t_sg = cha.issue(nc.sync, out=f32(f_gn, 1024),
                         in_=sg_in[None, :].to_broadcast((128, 1024)))
        t_sb = cha.issue(nc.sync, out=f32(f_bn, 1024),
                         in_=sbv_in[None, :].to_broadcast((128, 1024)))
        t_sc = cha.issue(nc.sync, out=f32(f_sccol, 1),
                         in_=sc_in[0:1][None, :].to_broadcast((128, 1)))
        W(nc.vector, t_sg); W(nc.vector, t_sb); W(nc.vector, t_sc)
        nc.vector.tensor_scalar(f32(f_gn, 1024), f32(f_gn, 1024),
                                f32(f_sccol, 1), None, op0=AO.mult)
        sv.inc(nc.vector.tensor_scalar(f32(f_bn, 1024), f32(f_bn, 1024),
                                       f32(f_sccol, 1), None, op0=AO.mult))
        for et in range(8):
            par = et % 2
            sb_ = 8 * par
            set_ = bf(b_seN + 1024 * et, 1024)
            zt = f32(ftr0 + 1024 * par, 1024)
            W(nc.vector, se_toks[et])
            if et >= 2:
                nc.vector.wait_ge(ss.h, slot_s[et - 2])
            nc.vector.tensor_reduce(STAT(sb_ + 0), set_, AX, AO.add)
            nc.vector.drain()
            sv.inc(nc.vector.tensor_scalar(STAT(sb_ + 1), STAT(sb_ + 0), -1.0 / D,
                                           None, op0=AO.mult))
            nc.scalar.wait_ge(sv.h, sv.n)
            ss.inc(nc.scalar.activation(zt, set_, AF.Identity, bias=STAT(sb_ + 1)))
            ss.inc(nc.scalar.activation(bf(b_j6, 1024), zt, AF.Square,
                                        accum_out=STAT(sb_ + 2)))
            nc.vector.wait_ge(ss.h, ss.n)
            sv.inc(nc.vector.tensor_scalar(STAT(sb_ + 3), STAT(sb_ + 2), 1.0 / D, EPS,
                                           op0=AO.mult, op1=AO.add))
            nc.scalar.wait_ge(sv.h, sv.n)
            ss.inc(nc.scalar.activation(STAT(sb_ + 3), STAT(sb_ + 3), AF.Sqrt))
            nc.vector.wait_ge(ss.h, ss.n)
            sv.inc(nc.vector.reciprocal(STAT(sb_ + 4), STAT(sb_ + 3)))
            nc.scalar.wait_ge(sv.h, sv.n)
            ss.inc(nc.scalar.activation(zt, zt, AF.Identity, scale=STAT(sb_ + 4)))
            slot_s[et] = ss.n
            nc.vector.wait_ge(ss.h, ss.n)
            nc.vector.tensor_tensor(zt, zt, f32(f_gn, 1024), op=AO.mult)
            sv.inc(nc.vector.tensor_tensor(set_, zt, f32(f_bn, 1024), op=AO.add))
        items = []
        for et in range(8):
            for kc in range(KC8):
                items.append((bf(b_seT + 1024 * kc, 1024)[:, 128 * et:128 * et + 128],
                              bf(b_seN + 1024 * et, 1024)[:, 128 * kc:128 * kc + 128]))
        dma_transpose_batch(items, sv, sv.n)
        cha.wait_all(nc.tensor)
        chs.wait_all(nc.tensor)
        ev4 = []
        for et in range(8):
            for sl in range(4):
                kk = 4 * et + sl
                ps = ps_a if kk % 2 == 0 else ps_b
                tgt = ps[:, 0:512]
                if kk >= 2:
                    nc.tensor.wait_ge(sv.h, ev4[kk - 2])
                for kc in range(KC8):
                    inst = nc.tensor.matmul(
                        tgt, bf(b_seT + 1024 * kc, 1024)[:, 128 * et:128 * et + 128],
                        bf(b_xnT + 2048 * kc, 2048)[:, 512 * sl:512 * sl + 512],
                        start=(kc == 0), stop=(kc == 7))
                st_.inc(inst)
                nc.vector.wait_ge(st_.h, st_.n)
                sv.inc(nc.vector.tensor_copy(
                    bf(b_lgT + 2048 * et, 2048)[:, 512 * sl:512 * sl + 512], tgt))
                ev4.append(sv.n)
        if dbg("logitsT", [128, 8 * 2048]):
            dbg_store("logitsT", dbg_t["logitsT"], bf(b_lgT, 16384), sv)
        barrier()
        if stop_after == 7:
            cho.wait_all(nc.gpsimd); chg.wait_all(nc.gpsimd)
            cha.wait_all(nc.gpsimd); chs.wait_all(nc.gpsimd)
            return nc, sorted(dbg_t)

        nc.vector.tensor_tensor(f32(f_mx, 2048), bf(b_lgT, 2048),
                                bf(b_lgT + 2048, 2048), op=AO.max)
        for et in range(2, 8):
            nc.vector.tensor_tensor(f32(f_mx, 2048), f32(f_mx, 2048),
                                    bf(b_lgT + 2048 * et, 2048), op=AO.max)
        sv.inc(nc.vector.tensor_copy(bf(b_mxb, 2048), f32(f_mx, 2048)))
        red_after = {}
        mxv0 = sv.n
        for t in range(16):
            if t >= 2:
                nc.sync.wait_ge(sv.h, red_after[t - 2])
            if t == 0:
                nc.sync.wait_ge(sv.h, mxv0)
            tokt = cha.issue_t(nc.sync, bf(sm_mxt + 128 * (t % 2), 128),
                               bf(b_mxb, 2048)[:, 128 * t:128 * t + 128])
            W(nc.vector, tokt)
            sv.inc(nc.vector.tensor_reduce(f32(f_rl, 16)[:, t:t + 1],
                                           bf(sm_mxt + 128 * (t % 2), 128), AX, AO.max))
            red_after[t] = sv.n
        nc.sync.wait_ge(sv.h, sv.n)
        t6 = cha.issue(nc.sync, out=d_bn[0:2048].rearrange("(t p) -> p t", p=128),
                       in_=f32(f_rl, 16), allow_slow_non_contiguous=True)
        W(nc.sync, t6)
        t7 = cha.issue(nc.sync, out=f32(f_mx, 2048),
                       in_=d_bn[0:2048][None, :].to_broadcast((128, 2048)))
        W(nc.vector, t7)
        for et in range(8):
            sv.inc(nc.vector.tensor_tensor(f32(f_sub, 2048), bf(b_lgT + 2048 * et, 2048),
                                           f32(f_mx, 2048), op=AO.subtract))
            nc.scalar.wait_ge(sv.h, sv.n)
            ss.inc(nc.scalar.activation(bf(b_cu + 2048 * et, 2048), f32(f_sub, 2048),
                                        AF.Exp))
            if et < 7:
                nc.vector.wait_ge(ss.h, ss.n)
        nc.tensor.wait_ge(ss.h, ss.n)
        lcv = []
        for sl in range(4):
            if sl >= 1:
                nc.tensor.wait_ge(sv.h, lcv[sl - 1])
            for et in range(8):
                inst = nc.tensor.matmul(ps_s[0:1, 0:512], bf(sm_ones, 64)[:, 0:1],
                                        bf(b_cu + 2048 * et, 2048)[:, 512 * sl:512 * sl + 512],
                                        start=(et == 0), stop=(et == 7))
            st_.inc(inst)
            nc.vector.wait_ge(st_.h, st_.n)
            sv.inc(nc.vector.tensor_copy(
                f32(f_lcrow, 2048, p1=1)[:, 512 * sl:512 * sl + 512],
                ps_s[0:1, 0:512]))
            lcv.append(sv.n)
        sv.inc(nc.vector.reciprocal(f32(f_lcrow, 2048, p1=1), f32(f_lcrow, 2048, p1=1)))
        nc.sync.wait_ge(sv.h, sv.n)
        t8 = cha.issue(nc.sync, out=d_bn[2048:4096][None, :],
                       in_=f32(f_lcrow, 2048, p1=1))
        W(nc.sync, t8)
        t_rlc = cha.issue(nc.sync, out=f32(f_rlc, 16),
                          in_=d_bn[2048:4096].rearrange("(t p) -> p t", p=128),
                          allow_slow_non_contiguous=True)
        if dbg("combUT", [128, 8 * 2048]):
            dbg_store("combUT", dbg_t["combUT"], bf(b_cu, 16384), ss)
        if dbg("rlc", [128, 16]):
            nc.gpsimd.wait_ge(t_rlc[0], t_rlc[1])
            tokd = chg.issue(nc.gpsimd, out=dbg_t["rlc"], in_=f32(f_rlc, 16))
            W(nc.gpsimd, tokd)
        barrier()
        if stop_after == 8:
            cho.wait_all(nc.gpsimd); chg.wait_all(nc.gpsimd)
            cha.wait_all(nc.gpsimd); chs.wait_all(nc.gpsimd)
            return nc, sorted(dbg_t)

        for et in range(8):
            lg = bf(b_lgT + 2048 * et, 2048)
            nc.vector.tensor_reduce(STAT(0), lg, AX, AO.max)
            nc.vector.drain()
            sv.inc(nc.vector.tensor_scalar(STAT(1), STAT(0), -1.0, None, op0=AO.mult))
            nc.scalar.wait_ge(sv.h, sv.n)
            ss.inc(nc.scalar.activation(lg, lg, AF.Exp, bias=STAT(1),
                                        accum_out=STAT(2)))
            nc.vector.wait_ge(ss.h, ss.n)
            sv.inc(nc.vector.reciprocal(STAT(3), STAT(2)))
            nc.scalar.wait_ge(sv.h, sv.n)
            ss.inc(nc.scalar.activation(lg, lg, AF.Identity, scale=STAT(3)))
            nc.sync.wait_ge(ss.h, ss.n)
            for ntx in range(NT):
                src = lg[:, 128 * ntx:128 * ntx + 128]
                dst = bf(b_disp + 1024 * ntx, 1024)[:, 128 * et:128 * et + 128]
                cha.issue_t(nc.sync, dst, src)
        if dbg("dispT", [128, 8 * 2048]):
            dbg_store("dispT", dbg_t["dispT"], bf(b_lgT, 16384), ss)
        barrier()
        if stop_after == 9:
            cho.wait_all(nc.gpsimd); chg.wait_all(nc.gpsimd)
            cha.wait_all(nc.gpsimd); chs.wait_all(nc.gpsimd)
            return nc, sorted(dbg_t)

        cha.wait_all(nc.tensor)
        chs.wait_all(nc.tensor)
        ev5 = []
        for dt in range(8):
            for es2 in range(2):
                kk = 2 * dt + es2
                ps = ps_a if kk % 2 == 0 else ps_b
                tgt = ps[:, 0:512]
                if kk >= 2:
                    nc.tensor.wait_ge(sv.h, ev5[kk - 2])
                for ncx in range(NT):
                    inst = nc.tensor.matmul(
                        tgt, bf(G1 + 1024 * ncx, 1024)[:, 128 * dt:128 * dt + 128],
                        bf(b_disp + 1024 * ncx, 1024)[:, 512 * es2:512 * es2 + 512],
                        start=(ncx == 0), stop=(ncx == 15))
                st_.inc(inst)
                nc.vector.wait_ge(st_.h, st_.n)
                sv.inc(nc.vector.tensor_copy(
                    bf(b_slt + 1024 * dt, 1024)[:, 512 * es2:512 * es2 + 512], tgt))
                ev5.append(sv.n)
        if dbg("slotsT", [128, 8 * 1024]):
            dbg_store("slotsT", dbg_t["slotsT"], bf(b_slt, 8192), sv)
        barrier()
        if stop_after == 10:
            cho.wait_all(nc.gpsimd); chg.wait_all(nc.gpsimd)
            cha.wait_all(nc.gpsimd); chs.wait_all(nc.gpsimd)
            return nc, sorted(dbg_t)

        ntoks = E * (2 * KC8 + 16)
        wtoks = chg.plan(ntoks)
        wg_ld, wu_ld, wd_ld = {}, {}, {}
        ti = 0
        for e in range(E):
            for kc in range(KC8):
                wg_ld[(e, kc)] = wtoks[ti]; ti += 1
                wu_ld[(e, kc)] = wtoks[ti]; ti += 1
            for kc in range(16):
                wd_ld[(e, kc)] = wtoks[ti]; ti += 1
        gumm, dnmm = {}, {}
        gelu_done, gu_done, eo_done = {}, {}, {}
        fb_toks = {}
        for e in range(E):
            epar = e % 2
            if e >= 2:
                nc.sync.wait_ge(ss.h, gelu_done[e - 2])
                nc.sync.wait_ge(sv.h, eo_done[e - 2])
            ta = cha.issue(nc.sync, out=f32(f_fb + 32 * epar, 16),
                           in_=bg_in[e].rearrange("(t p) -> p t", p=128),
                           allow_slow_non_contiguous=True)
            tb = cha.issue(nc.sync, out=f32(f_fb + 64 + 32 * epar, 16),
                           in_=bu_in[e].rearrange("(t p) -> p t", p=128),
                           allow_slow_non_contiguous=True)
            tc = cha.issue(nc.sync, out=f32(f_bd + 1024 * epar, 1024, p1=64),
                           in_=bd_in[e][None, :].to_broadcast((64, 1024)))
            fb_toks[e] = (ta, tb, tc)
            if e > 0:
                nc.tensor.wait_ge(ss.h, gelu_done[e - 1])
                nc.tensor.wait_ge(sv.h, gu_done[e - 1])
            for kc in range(KC8):
                W(nc.tensor, wu_ld[(e, kc)])
                for ft in range(FT):
                    nc.tensor.matmul(ps_a[:, 64 * ft:64 * ft + 64],
                                     bf(b_wg + 2048 * (kc % 4), 2048)[:, 128 * ft:128 * ft + 128],
                                     bf(b_slt + 1024 * kc, 1024)[:, 64 * e:64 * e + 64],
                                     start=(kc == 0 and ft in (0, 8)),
                                     stop=(kc == 7 and ft in (7, 15)))
                for ft in range(FT):
                    inst = nc.tensor.matmul(ps_b[:, 64 * ft:64 * ft + 64],
                                            bf(b_wu + 2048 * (kc % 4), 2048)[:, 128 * ft:128 * ft + 128],
                                            bf(b_slt + 1024 * kc, 1024)[:, 64 * e:64 * e + 64],
                                            start=(kc == 0 and ft in (0, 8)),
                                            stop=(kc == 7 and ft in (7, 15)))
                st_.inc(inst)
                gumm[(e, kc)] = st_.n
            nc.scalar.wait_ge(st_.h, gumm[(e, KC8 - 1)])
            W(nc.scalar, fb_toks[e][0])
            if e > 0:
                nc.scalar.wait_ge(st_.h, dnmm[(e - 1, 15)])
            for ft in range(FT):
                inst = nc.scalar.activation(bf(b_gu, 1024)[:, 64 * ft:64 * ft + 64],
                                            ps_a[:, 64 * ft:64 * ft + 64], AF.Gelu,
                                            bias=f32(f_fb + 32 * epar, 16)[:, ft:ft + 1])
            ss.inc(inst)
            gelu_done[e] = ss.n
            nc.vector.wait_ge(ss.h, gelu_done[e])
            W(nc.vector, fb_toks[e][1])
            for ft in range(FT):
                inst = nc.vector.scalar_tensor_tensor(
                    bf(b_gu, 1024)[:, 64 * ft:64 * ft + 64],
                    ps_b[:, 64 * ft:64 * ft + 64],
                    f32(f_fb + 64 + 32 * epar, 16)[:, ft:ft + 1],
                    bf(b_gu, 1024)[:, 64 * ft:64 * ft + 64],
                    op0=AO.add, op1=AO.mult)
            sv.inc(inst)
            gu_done[e] = sv.n
            nc.tensor.wait_ge(sv.h, gu_done[e])
            if e > 0:
                nc.tensor.wait_ge(sv.h, eo_done[e - 1])
            for kc in range(16):
                W(nc.tensor, wd_ld[(e, kc)])
                for s2 in range(2):
                    inst = nc.tensor.matmul(
                        ps_y[0:64, 512 * s2:512 * s2 + 512],
                        bf(b_gu, 1024)[:, 64 * kc:64 * kc + 64],
                        bf(b_wd + 1024 * (kc % 4), 1024)[:, 512 * s2:512 * s2 + 512],
                        start=(kc == 0), stop=(kc == 15))
                st_.inc(inst)
                dnmm[(e, kc)] = st_.n
            nc.vector.wait_ge(st_.h, dnmm[(e, 15)])
            W(nc.vector, fb_toks[e][2])
            eb = 64 * (e % 2)
            for s2 in range(2):
                inst = nc.vector.tensor_tensor(
                    bf(b_eo + 1024 * (e // 2), 1024)[eb:eb + 64, 512 * s2:512 * s2 + 512],
                    ps_y[0:64, 512 * s2:512 * s2 + 512],
                    f32(f_bd + 1024 * epar, 1024, p1=64)[:, 512 * s2:512 * s2 + 512],
                    op=AO.add)
            sv.inc(inst)
            eo_done[e] = sv.n
        for e in range(E):
            for kc in range(KC8):
                if e > 0 or kc >= 4:
                    pe, pk = (e, kc - 4) if kc >= 4 else (e - 1, kc + 4)
                    nc.gpsimd.wait_ge(st_.h, gumm[(pe, pk)])
                tok = chg.issue(nc.gpsimd, out=bf(b_wg + 2048 * (kc % 4), 2048),
                                in_=wg_in[e, 128 * kc:128 * kc + 128, :])
                assert tok == wg_ld[(e, kc)]
                tok = chg.issue(nc.gpsimd, out=bf(b_wu + 2048 * (kc % 4), 2048),
                                in_=wu_in[e, 128 * kc:128 * kc + 128, :])
                assert tok == wu_ld[(e, kc)]
            for kc in range(16):
                if e > 0 or kc >= 4:
                    pe, pk = (e, kc - 4) if kc >= 4 else (e - 1, kc + 12)
                    nc.gpsimd.wait_ge(st_.h, dnmm[(pe, pk)])
                tok = chg.issue(nc.gpsimd, out=bf(b_wd + 1024 * (kc % 4), 1024),
                                in_=wd_in[e, 128 * kc:128 * kc + 128, :])
                assert tok == wd_ld[(e, kc)]
        if dbg("eoutb", [128, 8 * 1024]):
            dbg_store("eoutb", dbg_t["eoutb"], bf(b_eo, 8192), sv)
        barrier()
        if stop_after == 11:
            cho.wait_all(nc.gpsimd); chg.wait_all(nc.gpsimd)
            cha.wait_all(nc.gpsimd); chs.wait_all(nc.gpsimd)
            return nc, sorted(dbg_t)

        W(nc.vector, t_rlc)
        ev6 = []
        y_toks = {}
        for i in range(NT):
            par = i % 2
            xt = f32(ftr0 + 1024 * par, 1024)
            yt = f32(ftr2, 1024)
            if i >= 2:
                nc.sync.wait_ge(sv.h, ev6[2 * i - 3])
            W(nc.sync, xsp_toks[i])
            tx = cha.issue(nc.sync, out=xt, in_=d_x2[128 * i:128 * i + 128, :])
            for s2 in range(2):
                kk = 2 * i + s2
                ps = ps_a if kk % 2 == 0 else ps_b
                tgt = ps[:, 0:512]
                if kk >= 2:
                    nc.tensor.wait_ge(sv.h, ev6[kk - 2])
                for et in range(8):
                    inst = nc.tensor.matmul(
                        tgt, bf(b_cu + 2048 * et, 2048)[:, 128 * i:128 * i + 128],
                        bf(b_eo + 1024 * et, 1024)[:, 512 * s2:512 * s2 + 512],
                        start=(et == 0), stop=(et == 7))
                st_.inc(inst)
                nc.vector.wait_ge(st_.h, st_.n)
                if s2 == 0:
                    W(nc.vector, tx)
                    if i >= 1:
                        W(nc.vector, y_toks[i - 1])
                nc.vector.tensor_scalar(tgt, tgt, f32(f_rlc, 16)[:, i:i + 1], None,
                                        op0=AO.mult)
                nc.vector.tensor_tensor(tgt, tgt,
                                        bf(b_gb, 1024)[:, 512 * s2:512 * s2 + 512],
                                        op=AO.mult)
                sv.inc(nc.vector.tensor_tensor(yt[:, 512 * s2:512 * s2 + 512], tgt,
                                               xt[:, 512 * s2:512 * s2 + 512], op=AO.add))
                ev6.append(sv.n)
            nc.sync.wait_ge(sv.h, sv.n)
            y_toks[i] = cho.issue(nc.sync, out=y_out[128 * i:128 * i + 128, :], in_=yt)
        cho.wait_all(nc.gpsimd)
        chg.wait_all(nc.gpsimd)
    return nc, sorted(dbg_t)


_CONSTS = host_consts()


def core_inputs(b, inputs):
    return {
        "x": np.ascontiguousarray(inputs["L0_attn_x"][b]),
        "cvec": np.ascontiguousarray(inputs["L0_c"][b]),
        "wq": inputs["L0_attn_Wqkv_w"], "wqb": inputs["L0_attn_Wqkv_b"],
        "u": inputs["L0_attn_U"], "wo": inputs["L0_attn_Wo_w"],
        "wmod": inputs["L0_wmod"],
        "ng": inputs["L0_ffn_norm_g"], "nb": inputs["L0_ffn_norm_b"],
        "sg": inputs["L0_ffn_slot_g"], "sbv": inputs["L0_ffn_slot_b"],
        "sc": inputs["L0_ffn_scale"],
        "semb": np.ascontiguousarray(np.asarray(inputs["L0_ffn_slot_embeds"]).reshape(E * S, D)),
        "wgx": inputs["L0_ffn_Wg"], "bgx": inputs["L0_ffn_bg"],
        "wux": inputs["L0_ffn_Wu"], "bux": inputs["L0_ffn_bu"],
        "wdx": inputs["L0_ffn_Wd"], "bdx": inputs["L0_ffn_bd"],
        **_CONSTS,
    }


def kernel(**inputs):
    nc, _ = build()
    in_maps = [core_inputs(b, inputs) for b in range(8)]
    res = run_bass_kernel_spmd(nc, in_maps, core_ids=list(range(8)))
    return np.stack([res.results[b]["y"] for b in range(8)], axis=0)
